# version 01
# baseline (speedup 1.0000x reference)
import numpy as np
import ml_dtypes
import concourse.bass as bass
import concourse.mybir as mybir
from concourse.bass_utils import run_bass_kernel_spmd
from contextlib import ExitStack

F32 = mybir.dt.float32
BF16 = mybir.dt.bfloat16
U8 = mybir.dt.uint8
AF = mybir.ActivationFunctionType
ALU = mybir.AluOpType
AX = mybir.AxisListType
bfloat16 = ml_dtypes.bfloat16

NCORES = 8
BL = 4
T = 2048
D = 1024
NT = T // 128
H1, C1, DV1 = 8, 64, 128
E, FF = 8, 4096
LNEPS = 1e-5
SHIFT = 30.0
CUT = 99
ABL = set()
NTILE_DEV = 16

DMAQ = ('DS', 'DP')
ENGINES = ('T', 'V', 'S') + DMAQ
CHANNELS = ENGINES + ('DSq', 'DPq')


class Prog:
    def __init__(self):
        self.ops = {e: [] for e in ENGINES}
        self.count = {c: 0 for c in CHANNELS}
        self.waited = {e: {p: 0 for p in CHANNELS} for e in ENGINES}
        self.qself = {q: 0 for q in DMAQ}

    def op(self, eng, fn, inc=False, sem=None):
        ch = sem or eng
        step = (16 if ch in DMAQ else 1) if inc else 0
        self.count[ch] += step
        self.ops[eng].append(('op', fn, ch if step else None, step))
        return (ch, self.count[ch])

    def dma(self, q, fn):
        return self.op(q, fn, inc=True)

    def wait(self, eng, ev):
        if ev is None:
            return
        peng, val = ev
        if peng in DMAQ:
            val = self.count[peng]
            if self.qself[peng] < val:
                self.qself[peng] = val
                self.ops[peng].append(('wait', peng, val, 0))
        if peng == eng and eng not in DMAQ:
            return
        if peng == eng and eng in DMAQ:
            self.ops[eng].append(('wait', peng, val, 0))
            return
        if self.waited[eng][peng] >= val:
            return
        self.waited[eng][peng] = val
        self.ops[eng].append(('wait', peng, val, 0))

    def barrier(self, engines=ENGINES):
        evs = []
        for e in engines:
            sem = e + 'q' if e in DMAQ else None
            evs.append(self.op(e, (lambda h: h.nop()), inc=True, sem=sem))
        for e in engines:
            for ev in evs:
                self.wait(e, ev)

    def emit_into(self, block, sems):
        prog = self

        def run_stream(handle, eng):
            for kind, a, b, c in prog.ops[eng]:
                if kind == 'wait':
                    handle.wait_ge(sems[a], b)
                else:
                    inst = a(handle)
                    if b is not None:
                        inst.then_inc(sems[b], c)

        @block.tensor
        def _(t):
            run_stream(t, 'T')

        @block.vector
        def _(v):
            run_stream(v, 'V')

        @block.scalar
        def _(s):
            run_stream(s, 'S')

        @block.sync
        def _(s):
            run_stream(s, 'DS')

        @block.gpsimd
        def _(g):
            run_stream(g, 'DP')


class Arena:
    def __init__(self, handle, nbytes, name, align=512):
        self.h = handle
        self.nbytes = nbytes
        self.off = 0
        self.name = name
        self.align = align

    def reset(self, off=0):
        self.off = off

    def take(self, parts, shape, dtype):
        isz = 4 if dtype == F32 else (1 if dtype == U8 else 2)
        cols = int(np.prod(shape))
        start = (self.off + self.align - 1) // self.align * self.align
        self.off = start + cols * isz
        assert self.off <= self.nbytes, (
            f'arena {self.name} overflow: {self.off} > {self.nbytes}')
        ap = self.h.ap()
        if dtype != F32:
            ap = ap.bitcast(dtype)
        v = ap[0:parts, start // isz: start // isz + cols]
        if len(shape) > 1:
            pat = ' '.join(f'a{i}' for i in range(len(shape)))
            kw = {f'a{i}': s for i, s in enumerate(shape)}
            v = v.rearrange(f'p ({pat}) -> p {pat}', **kw)
        return v


def _pc(vec, parts=128):
    return np.ascontiguousarray(np.asarray(vec, np.float32).reshape(-1, parts).T)


def _bc(vec, parts=128):
    return np.ascontiguousarray(np.broadcast_to(
        np.asarray(vec, np.float32)[None, :], (parts, len(vec))))


def host_layout(inputs):
    g = lambda k: np.asarray(inputs[k], dtype=np.float32)
    d = {}
    d['wqt'] = np.ascontiguousarray(g('L0_attn_Wq').T)
    d['wkt'] = np.ascontiguousarray(g('L0_attn_Wk').T)
    d['wvt'] = np.ascontiguousarray(g('L0_attn_Wv').T)
    d['wat'] = np.ascontiguousarray(g('L0_attn_Wa_w').T)
    d['qc'] = np.ascontiguousarray(g('L0_attn_Qc').transpose(1, 0, 2).reshape(C1, H1 * C1))
    d['vc'] = np.ascontiguousarray(g('L0_attn_Vc').transpose(1, 0, 2).reshape(C1, H1 * DV1))
    d['wot'] = np.ascontiguousarray(g('L0_attn_Wo').T)
    d['wab_pc'] = _pc(g('L0_attn_Wa_b'))
    d['wab_bc'] = _bc(g('L0_attn_Wa_b'))
    d['gam_bc'] = _bc(g('L0_attn_gamma'))
    d['bet_bc'] = _bc(g('L0_attn_beta'))
    d['g1_bc'] = _bc(g('L0_ffn_ln1_g'))
    d['b1l_bc'] = _bc(g('L0_ffn_ln1_b'))
    d['g2_bc'] = _bc(g('L0_ffn_ln2_g'))
    d['b2l_bc'] = _bc(g('L0_ffn_ln2_b'))
    d['bo_bc'] = _bc(g('L0_ffn_bo'))
    d['bv_bc'] = _bc(g('L0_ffn_bv'))
    d['bq_pc'] = _pc(g('L0_ffn_bq'))
    d['bk_pc'] = _pc(g('L0_ffn_bk'))
    d['fwq'] = np.ascontiguousarray(g('L0_ffn_wq'))
    d['fwk'] = np.ascontiguousarray(g('L0_ffn_wk'))
    d['fwv'] = np.ascontiguousarray(g('L0_ffn_wv'))
    d['fwo'] = np.ascontiguousarray(g('L0_ffn_wo'))
    d['gatew'] = np.ascontiguousarray(g('L0_ffn_gate_w'))
    d['gateb_bc'] = _bc(g('L0_ffn_gate_b'))
    d['b2t'] = np.ascontiguousarray(g('L0_ffn_b2'))
    d['b1c'] = np.ascontiguousarray(g('L0_ffn_b1').reshape(E * (FF // 128), 128).T)
    d['w1'] = np.ascontiguousarray(g('L0_ffn_w1'))
    d['w2'] = np.ascontiguousarray(g('L0_ffn_w2'))
    d['ident_bf'] = np.eye(128, dtype=np.float32).astype(bfloat16)
    d['ut_f32'] = np.triu(np.ones((128, 128), np.float32))
    d['lt_strict_bf'] = (np.arange(128)[:, None] > np.arange(128)[None, :]
                         ).astype(np.float32).astype(bfloat16)
    d['ones_r_f32'] = np.ones((1, 128), np.float32)
    d['fconsts'] = np.broadcast_to(np.array([[LNEPS]], np.float32), (128, 1)).copy()
    d['ones64_bf'] = np.ones((C1, 1), np.float32).astype(bfloat16)
    return d


WEIGHT_SPECS = [
    ('wqt', (D, C1), F32), ('wkt', (D, C1), F32), ('wvt', (D, C1), F32),
    ('wat', (D, 512), F32),
    ('qc', (C1, H1 * C1), F32), ('vc', (C1, H1 * DV1), F32),
    ('wot', (D, D), F32),
    ('wab_pc', (128, 4), F32), ('wab_bc', (128, 512), F32),
    ('gam_bc', (128, D), F32), ('bet_bc', (128, D), F32),
    ('g1_bc', (128, D), F32), ('b1l_bc', (128, D), F32),
    ('g2_bc', (128, D), F32), ('b2l_bc', (128, D), F32),
    ('bo_bc', (128, D), F32), ('bv_bc', (128, D), F32),
    ('bq_pc', (128, 8), F32), ('bk_pc', (128, 8), F32),
    ('fwq', (D, D), F32), ('fwk', (D, D), F32), ('fwv', (D, D), F32),
    ('fwo', (D, D), F32),
    ('gatew', (D, 8), F32), ('gateb_bc', (128, 8), F32),
    ('b2t', (E, D), F32), ('b1c', (128, E * 32), F32),
    ('w1', (E, D, FF), F32), ('w2', (E, FF, D), F32),
    ('ident_bf', (128, 128), BF16), ('ut_f32', (128, 128), F32),
    ('lt_strict_bf', (128, 128), BF16),
    ('ones_r_f32', (1, 128), F32), ('ones64_bf', (C1, 1), BF16),
    ('fconsts', (128, 1), F32),
]

DEBUG_TAPS = {
    'd_xy': ([BL, T, D], F32), 'd_x1': ([BL, T, D], F32),
    'd_z': ([BL, T, D], F32), 'd_ofm': ([BL, H1 * DV1, T], BF16),
    'd_hn2': ([8, 128, BL * T], BF16), 'd_oatt': ([BL, D, T], BF16),
    'd_x1p': ([BL, T, D], F32), 'd_mask': ([128, BL, NT, 8], F32),
    'd_lg': ([BL, 128, 4, T], F32), 'd_kn': ([BL, C1, T], BF16),
    'd_xn': ([BL, T, D], BF16), 'd_xnfm': ([BL, 128, 8, T], BF16),
    'd_ident': ([128, 128], BF16), 'd_utm': ([128, 128], F32),
    'd_gamv': ([128, D], F32), 'd_qmsc': ([128, T], F32),
    'd_qmh': ([C1, T], BF16), 'd_kmh': ([C1, T], BF16), 'd_vth': ([128, NT * DV1], BF16),
    'd_mids': ([128, 64], F32), 'd_negm': ([128, 64], F32),
    'd_khat': ([BL, 128, NT, 512], BF16), 'd_qlat': ([BL, C1, T], BF16),
    'd_pt': ([128, 128], BF16), 'd_sf': ([C1, 128], F32), 'd_gh': ([C1, 16], F32),
    'd_nst': ([128, 64], F32), 'd_ends': ([128, 64], F32),
    'd_lnat': ([128, 512], BF16), 'd_eks': ([128, 512], F32),
    'd_apre': ([128, 512], F32),
}


def build(debug=(), phases='ABC', nseq=BL, nexp=E):
    nc = bass.Bass('TRN2', target_bir_lowering=False, detect_race_conditions=False)
    P = Prog()
    ED = {}

    dr = {}
    dr['x'] = nc.dram_tensor('x', [BL, T, D], F32, kind='ExternalInput').ap()
    for name, shape, dt in WEIGHT_SPECS:
        dr[name] = nc.dram_tensor(name, list(shape), dt, kind='ExternalInput').ap()
    dr['out'] = nc.dram_tensor('out', [BL, T, D], F32, kind='ExternalOutput').ap()
    dr['s_xy'] = nc.dram_tensor('s_xy', [BL, T, D], F32, kind='Internal').ap()
    dr['s_x1'] = nc.dram_tensor('s_x1', [BL, T, D], F32, kind='Internal').ap()
    dr['s_z'] = nc.dram_tensor('s_z', [BL, T, D], F32, kind='Internal').ap()
    dr['s_hn2'] = nc.dram_tensor('s_hn2', [8, 128, BL * T], BF16, kind='Internal').ap()
    for tname in debug:
        shape, dt = DEBUG_TAPS[tname]
        dr[tname] = nc.dram_tensor(tname, shape, dt, kind='ExternalOutput').ap()

    st = ExitStack()
    AB1 = Arena(st.enter_context(nc.sbuf_tensor('AB1', [128, 65536 // 4], F32)), 65536, 'AB1')
    AB2 = Arena(st.enter_context(nc.sbuf_tensor('AB2', [128, 65536 // 4], F32)), 65536, 'AB2')
    AB3 = Arena(st.enter_context(nc.sbuf_tensor('AB3', [128, 33280 // 4], F32)), 33280, 'AB3')
    AB4 = Arena(st.enter_context(nc.sbuf_tensor('AB4', [128, 31872 // 4], F32)), 31872, 'AB4')

    DED = Arena(st.enter_context(nc.sbuf_tensor('DED', [128, 16512 // 4], F32)),
                16512, 'DED', align=128)

    class _W:
        __slots__ = ('v',)
        def __init__(self, v):
            self.v = v
        def ap(self):
            return self.v

    def ded(name, parts, cols, dt):
        return _W(DED.take(parts, (cols,), dt))

    zcol = ded('zcol', 128, 1, F32)
    fconsts = ded('fconsts', 128, 1, F32)
    nc.const_aps.register(F32, LNEPS, fconsts.ap()[:, 0:1])
    qmxc = ded('qmxc', C1, 4, BF16)
    maskbf = ded('maskbf', 128, 8, BF16)
    ohT = ded('ohT', 8, 128, BF16)
    ident = ded('identt', 128, 128, BF16)
    utm = ded('utm', 128, 128, F32)
    ltm = ded('ltm', 128, 128, BF16)
    ones_rf = ded('ones_rf', 1, 128, F32)
    ones64b = ded('ones64b', C1, 1, BF16)
    statc = ded('statc', 128, 16, F32)
    wabp = ded('wabp', 128, 4, F32)
    bqp = ded('bqp', 128, 8, F32)
    bkp = ded('bkp', 128, 8, F32)
    gatebc = ded('gatebc', 128, 8, F32)
    gatewb = ded('gatewb', 128, 64, BF16)
    b2tb = ded('b2tb', 8, D, BF16)
    b1cc = ded('b1cc', 128, E * 32, F32)
    gamch = ded('gamch', 128, 64, F32)
    midsch = ded('midsch', 128, 64, F32)
    negmid = ded('negmid', 128, 64, F32)
    negst = ded('negst', 128, 64, F32)
    gh = ded('gh', C1, 16, F32)
    kntok = ded('kntok', 128, NT * C1, BF16)
    klatf = ded('klatf', 128, 512, F32)
    ptile = ded('ptile', 128, 256, BF16)
    masks = ded('masks', 128, BL * NT * 8, F32)
    equ8 = ded('equ8', 128, NT * 8, U8)
    keepb = ded('keepb', 128, NT, F32)
    tmpnt = ded('tmpnt', 128, NT, F32)
    onessm = ded('onessm', 128, 128, F32)
    zerossm = ded('zerossm', 128, 128, F32)
    biasc2 = ded('biasc2', 128, 64, F32)
    nubc = ded('nubc', 1, 64, F32)
    lgt_st = ded('lgt_st', 128, 32, F32)
    rncol = ded('rncol', 128, 96, F32)

    ps = {}
    ps['A'] = st.enter_context(nc.psum_tensor('psA', [128, 512], F32))
    ps['B'] = st.enter_context(nc.psum_tensor('psB', [128, 512], F32))
    ps['C'] = st.enter_context(nc.psum_tensor('psC', [128, 512], F32))
    ps['G1'] = st.enter_context(nc.psum_tensor('psG1', [128, 1024], F32))
    ps['G2'] = st.enter_context(nc.psum_tensor('psG2', [128, 1024], F32))
    ps['T'] = st.enter_context(nc.psum_tensor('psT', [128, 1024], BF16))

    sems = {e: st.enter_context(nc.semaphore(f'sem{e}')) for e in CHANNELS}
    block = st.enter_context(nc.Block())

    def LOAD(q, dst_ap, src_ap, waits=()):
        for w in waits:
            P.wait(q, w)
        return P.dma(q, lambda h, dd=dst_ap, ss=src_ap: h.dma_start(out=dd, in_=ss))

    STORE = LOAD

    def ACCUM(q, dst_ap, src_ap, waits=()):
        for w in waits:
            P.wait(q, w)
        return P.dma(q, lambda h, dd=dst_ap, ss=src_ap: h.dma_start(
            out=dd, in_=ss, accum_op=ALU.add))

    def V(fn, inc=True, waits=()):
        for w in waits:
            P.wait('V', w)
        return P.op('V', fn, inc=inc)

    def Vf(fn, waits=()):
        for w in waits:
            P.wait('V', w)
        P.op('V', fn)
        return P.op('V', (lambda v: v.memset(zcol.ap()[:, 0:1], 0.0)), inc=True)

    def S_(fn, inc=True, waits=()):
        for w in waits:
            P.wait('S', w)
        return P.op('S', fn, inc=inc)

    def Sf(fn, waits=()):
        for w in waits:
            P.wait('S', w)
        P.op('S', fn)
        return P.op('S', (lambda s: s.activation(out=zcol.ap()[:, 0:1],
                                                 in_=zcol.ap()[:, 0:1],
                                                 func=AF.Copy)), inc=True)

    def VPAD(n=4):
        for _ in range(n):
            P.op('V', (lambda v: v.memset(zcol.ap()[:, 0:1], 0.0)))

    def SPAD(n=4):
        for _ in range(n):
            P.op('S', (lambda s: s.activation(out=zcol.ap()[:, 0:1],
                                              in_=zcol.ap()[:, 0:1],
                                              func=AF.Copy)))

    def mm_group(out_ap, pairs, waits=(), fence=True):
        for w in waits:
            P.wait('T', w)
        n = len(pairs)
        for j, (lh, rh) in enumerate(pairs):
            ev = P.op('T', (lambda h, o=out_ap, l=lh, r=rh, s=(j == 0),
                            p=(j == n - 1): h.matmul(o, l, r, start=s, stop=p)),
                      inc=(not fence and j == n - 1))
        if fence:
            ev = P.op('T', (lambda h: h.drain()), inc=True)
        return ev

    def transpose8(src_bf, dst_view, nblk=8, waits=()):
        P.wait('T', ED.get('psT_free'))
        for w in waits:
            P.wait('T', w)
        for k in range(nblk):
            P.op('T', (lambda h, k=k: h.transpose(
                ps['T'].ap()[:, k * 128:(k + 1) * 128],
                src_bf[:, k * 128:(k + 1) * 128], ident.ap())))
        evt = P.op('T', (lambda h: h.drain()), inc=True)
        evc = Vf(lambda v: v.tensor_copy(
            dst_view, ps['T'].ap().rearrange('p (a n) -> p a n', a=8)[:, 0:nblk, :]),
            waits=[evt])
        ED['psT_free'] = evc
        return evc

    def dmaj(name):
        return dr[name].rearrange('(a p) n -> p a n', p=128)

    def ln_stats(src_ap, c0, scr, tag='s', src_ev=None):
        sc = statc.ap()
        S_(lambda s: s.activation(out=scr, in_=src_ap, func=AF.Copy,
                                  accum_out=sc[:, c0 + 2:c0 + 3]),
           waits=[ED.get(f'lnscr_{tag}'), src_ev])
        e2 = Sf(lambda s: s.activation(out=scr, in_=src_ap, func=AF.Square,
                                       accum_out=sc[:, c0 + 3:c0 + 4]))
        ED[f'lnscr_{tag}'] = e2
        V(lambda v: v.tensor_scalar(sc[:, c0:c0 + 1], sc[:, c0 + 2:c0 + 3],
                                    1.0 / D, None, ALU.mult), waits=[e2])
        V(lambda v: v.tensor_scalar(sc[:, c0 + 5:c0 + 6], sc[:, c0 + 3:c0 + 4],
                                    1.0 / D, None, ALU.mult))
        VPAD(3)
        V(lambda v: v.tensor_mul(sc[:, c0 + 4:c0 + 5], sc[:, c0:c0 + 1],
                                 sc[:, c0:c0 + 1]))
        VPAD(4)
        e4 = Vf(lambda v: v.tensor_tensor(sc[:, c0 + 5:c0 + 6],
                                          sc[:, c0 + 5:c0 + 6],
                                          sc[:, c0 + 4:c0 + 5], ALU.subtract))
        S_(lambda s: s.activation(out=sc[:, c0 + 5:c0 + 6], in_=sc[:, c0 + 5:c0 + 6],
                                  func=AF.Ln, bias=fconsts.ap()[:, 0:1]), waits=[e4])
        e5 = Sf(lambda s: s.activation(out=sc[:, c0 + 1:c0 + 2],
                                       in_=sc[:, c0 + 5:c0 + 6],
                                       func=AF.Exp, scale=-0.5))
        P.wait('V', e5)
        VPAD(2)
        return sc[:, c0:c0 + 1], sc[:, c0 + 1:c0 + 2]

    def ln_apply(dst, src, mu, rs, waits=(), fence=False):
        fn = Vf if fence else V
        return fn(lambda v: v.tensor_scalar(dst, src, mu, rs,
                                            ALU.subtract, ALU.mult), waits=waits)

    ev_const = []
    for name, tile in [('ident_bf', ident), ('ut_f32', utm), ('lt_strict_bf', ltm),
                       ('ones_r_f32', ones_rf), ('ones64_bf', ones64b),
                       ('wab_pc', wabp), ('bq_pc', bqp),
                       ('bk_pc', bkp), ('gateb_bc', gatebc), ('b1c', b1cc), ('fconsts', fconsts)]:
        ev_const.append(LOAD('DS', tile.ap(), dr[name]))
    if 'b2l' not in ABL:
        ev_const.append(LOAD('DP', b2tb.ap(), dr['b2t']))
        ev_const.append(LOAD('DP', gatewb.ap().rearrange('p (a n) -> p a n', a=8),
                             dmaj('gatew')))
    V(lambda v: v.memset(onessm.ap(), 1.0))
    V(lambda v: v.memset(zerossm.ap(), 0.0))
    V(lambda v: v.memset(zcol.ap(), 0.0))
    for ev in ev_const:
        P.wait('V', ev); P.wait('S', ev); P.wait('T', ev)
    SPAD(4)
    VPAD(4)

    def phaseA(b, sub='1234'):
        AB1.reset(); AB2.reset(); AB3.reset(); AB4.reset()
        xnfm = AB1.take(128, (8, T), BF16)
        ofm = AB1.take(128, (H1, T), BF16)
        gamv = AB2.take(128, (D,), F32)
        betv = AB2.take(128, (D,), F32)
        wkt_b = AB2.take(128, (8, 512), BF16)
        wabbc_v = AB2.take(128, (512,), F32)
        khat = AB3.take(128, (NT, 512), BF16)
        knfm = AB3.take(C1, (T,), BF16)
        qlat = AB3.take(C1, (T,), BF16)
        vlat = AB3.take(C1, (T,), BF16)
        vtok = AB3.take(128, (NT, DV1), BF16)
        x_t = AB4.take(128, (D,), F32)
        y_t = AB4.take(128, (D,), F32)
        scr1 = AB4.take(128, (D,), F32)
        scr2 = AB4.take(128, (D,), F32)
        xn_bf = AB4.take(128, (D,), BF16)
        wqt_b = AB4.take(128, (8, C1), BF16)
        wat_b = AB4.take(128, (8, 512), BF16)
        wvt_b = AB4.take(128, (8, C1), BF16)
        qc_b = AB4.take(C1, (H1 * C1,), BF16)
        vc_b = AB4.take(C1, (H1 * DV1,), BF16)

        evw = [LOAD('DP', wqt_b, dmaj('wqt'), waits=[ED.get('A_w_free')]),
               LOAD('DP', wvt_b, dmaj('wvt')),
               LOAD('DP', wat_b, dmaj('wat'), waits=[ED.get('A_ab2_free')]),
               LOAD('DP', qc_b, dr['qc']),
               LOAD('DP', vc_b, dr['vc']),
               LOAD('DS', gamv, dr['gam_bc'], waits=[ED.get('A_ab2_free')]),
               LOAD('DS', betv, dr['bet_bc']),
               LOAD('DS', wabbc_v, dr['wab_bc'])]
        wk_rep = dmaj('wkt')
        if 'wk' not in ABL:
            for rep in range(8):
                evw.append(LOAD('DP', wkt_b[:, :, rep * 64:(rep + 1) * 64], wk_rep))
        for e in evw:
            P.wait('T', e); P.wait('V', e); P.wait('S', e)
        if 'd_gamv' in debug and b == 0:
            STORE('DS', dr['d_gamv'], gamv, waits=[evw[5]])

        for i in range(min(NT, NTILE_DEV)):
            rows = slice(i * 128, (i + 1) * 128)
            evx = LOAD('DS', x_t, dr['x'][b, rows, :], waits=[ED.get('A_xt_done')])
            P.wait('V', evx); P.wait('S', evx)
            if 'ln' in ABL:
                eyn = V(lambda v: v.tensor_copy(y_t, x_t))
            else:
                P.wait('S', ED.get('A_psB_done'))
                mu, rs = ln_stats(x_t, 0, scr1, src_ev=evx)
                eyn = ln_apply(y_t, x_t, mu, rs, fence=True)
            VPAD(3)
            exy = Vf(lambda v: v.tensor_add(scr2, x_t, y_t),
                     waits=[ED.get('A_xy_stored')])
            if 'st' in ABL:
                ev1 = exy
            else:
                ev1 = STORE('DS', dr['s_xy'][b, rows, :], scr2, waits=[exy])
                if 'd_xy' in debug:
                    ev1 = STORE('DS', dr['d_xy'][b, rows, :], scr2)
            ED['A_xy_stored'] = ev1
            if 'ln' in ABL:
                V(lambda v: v.tensor_copy(x_t, y_t))
            else:
                mu2, rs2 = ln_stats(y_t, 8, scr1, src_ev=eyn)
                ln_apply(x_t, y_t, mu2, rs2)
            VPAD(3)
            ext = V(lambda v: v.tensor_mul(scr1, x_t, gamv))
            ED['A_xt_done'] = ext
            VPAD(3)
            exn = Vf(lambda v: v.tensor_add(xn_bf, scr1, betv))
            if 'd_xn' in debug:
                STORE('DS', dr['d_xn'][b, rows, :], xn_bf, waits=[exn])
            evcp = transpose8(xn_bf, xnfm[:, :, i * 128:(i + 1) * 128], waits=[exn])
            evmm = mm_group(ps['A'].ap(), [
                (xnfm[:, dc, rows], wat_b[:, dc, :]) for dc in range(8)],
                waits=[evcp, ED.get('A_psA_done')])
            ea = Vf(lambda v: v.tensor_add(scr1[:, 0:512], ps['A'].ap(), wabbc_v),
                    waits=[evmm, ED.get('lnscr_s')])
            ED['A_psA_done'] = ea
            es = Sf(lambda s, f=(AF.Copy if 'sig' in ABL else AF.Sigmoid):
                    s.activation(out=scr1[:, 512:1024], in_=scr1[:, 0:512],
                                 func=f), waits=[ea])
            ec = Vf(lambda v: v.tensor_scalar_max(scr1[:, 512:1024],
                                                  scr1[:, 512:1024], 1e-38),
                    waits=[es])
            el = Sf(lambda s, f=(AF.Copy if 'sig' in ABL else AF.Ln):
                    s.activation(out=xn_bf[:, 0:512],
                                 in_=scr1[:, 512:1024], func=f),
                    waits=[ec])
            import os as _os3
            if _os3.environ.get('NOREV'):
                evr = None
            else:
                evr = mm_group(ps['C'].ap(), [(ltm.ap(), xn_bf[:, 0:512])],
                               waits=[el, ED.get('A_psB_done'),
                                      ED.get('A_psC_done')])
            evkl = mm_group(ps['B'].ap(), [
                (xnfm[:, dc, rows],
                 (wat_b if 'wk' in ABL else wkt_b)[:, dc, :]) for dc in range(8)],
                waits=[ED.get('A_psB_done')])
            Vf(lambda v: v.tensor_copy(klatf.ap()[:, 0:512], ps['B'].ap()),
               waits=[evkl])
            ecopy_ev = ('V', P.count['V'])
            if 'sqred' not in ABL:
                esq0 = ecopy_ev
                e1 = Sf(lambda s: s.activation(out=scr1[:, 512:576],
                                               in_=klatf.ap()[:, 0:64],
                                               func=AF.Square,
                                               accum_out=rncol.ap()[:, 32:33]),
                        waits=[esq0, ED.get('A_ksq_red')])
                SPAD(4)
            else:
                e1 = ecopy_ev
            if 'nrm' not in ABL:
                S_(lambda s: s.activation(out=rncol.ap()[:, 32:33],
                                          in_=rncol.ap()[:, 32:33], func=AF.Ln),
                   waits=[e1, ED.get('A_rn_used')])
                e2 = Sf(lambda s: s.activation(out=rncol.ap()[:, 64:65],
                                               in_=rncol.ap()[:, 32:33],
                                               func=AF.Exp, scale=0.5))
                V(lambda v: v.tensor_scalar_max(rncol.ap()[:, 64:65],
                                                rncol.ap()[:, 64:65], 1e-12),
                  waits=[e2])
            if 'tail' in ABL:
                ED['A_psB_done'] = ecopy_ev
                ED['A_psC_done'] = None
                continue
            VPAD(3)
            ED['A_rn_used'] = V(lambda v: v.reciprocal(rncol.ap()[:, 0:1],
                                                       rncol.ap()[:, 64:65]))
            VPAD(4)
            V(lambda v, i=i: v.tensor_scalar(kntok.ap()[:, i * 64:(i + 1) * 64],
                                             klatf.ap()[:, 0:64],
                                             rncol.ap()[:, 0:1], None, ALU.mult))
            if 'd_lnat' in debug and b == 0 and i == 8:
                P.barrier()
                STORE('DS', dr['d_apre'], scr1[:, 0:512])
                evt_ = STORE('DS', dr['d_lnat'], xn_bf[:, 0:512])
                for e_ in ('T', 'V', 'S'):
                    P.wait(e_, evt_)
            eks = Sf(lambda s: s.activation(out=scr1[:, 0:512], in_=ps['C'].ap(),
                                            func=AF.Exp), waits=[evr])
            ED['A_psC_done'] = eks
            if 'd_eks' in debug and b == 0 and i == 8:
                P.barrier()
                evt_ = STORE('DS', dr['d_eks'], scr1[:, 0:512])
                for e_ in ('T', 'V', 'S'):
                    P.wait(e_, evt_)
            ekh = Vf(lambda v, i=i: v.scalar_tensor_tensor(
                khat[:, i, :], ps['B'].ap(), rncol.ap()[:, 0:1], scr1[:, 0:512],
                ALU.mult, ALU.mult), waits=[eks])
            ED['A_psB_done'] = ekh

        if '2' not in sub:
            return
        AB2.reset()
        lnafm = AB2.take(128, (4, T), F32)
        LG = AB2.take(128, (4, T), F32)
        ED['A_ab2_free'] = None
        ED['A_kntok_done'] = ED.get('A_psB_done')
        for tg in range(4):
            cols = slice(tg * 512, (tg + 1) * 512)
            evq = mm_group(ps['A'].ap()[0:64, :], [
                (wqt_b[:, dc, :], xnfm[:, dc, cols]) for dc in range(8)],
                waits=[ED.get('A_psA_done')])
            eq = Vf(lambda v, cols=cols: v.tensor_copy(qlat[:, cols],
                                                       ps['A'].ap()[0:64, :]),
                    waits=[evq])
            ED['A_psA_done'] = eq
            evv = mm_group(ps['B'].ap()[0:64, :], [
                (wvt_b[:, dc, :], xnfm[:, dc, cols]) for dc in range(8)],
                waits=[ED.get('A_psB_done')])
            ev2 = Vf(lambda v, cols=cols: v.tensor_copy(vlat[:, cols],
                                                        ps['B'].ap()[0:64, :]),
                     waits=[evv])
            ED['A_psB_done'] = ev2
        for jc in range(4):
            for tg in range(4):
                cols = slice(tg * 512, (tg + 1) * 512)
                pp, key = (ps['A'], 'A_psA_done') if tg % 2 == 0 else (ps['B'], 'A_psB_done')
                ev = mm_group(pp.ap(), [
                    (wat_b[:, dc, jc * 128:(jc + 1) * 128], xnfm[:, dc, cols])
                    for dc in range(8)], waits=[ED.get(key)])
                es = S_(lambda s, pp=pp, jc=jc: s.activation(
                    out=scr1[:, 0:512], in_=pp.ap(), func=AF.Sigmoid,
                    bias=wabp.ap()[:, jc:jc + 1]),
                    waits=[ev, ED.get('A_scr1_free')])
                ED[key] = es
                ec = Vf(lambda v: v.tensor_scalar_max(scr1[:, 0:512],
                                                      scr1[:, 0:512], 1e-38),
                        waits=[es])
                el = Sf(lambda s, jc=jc, cols=cols: s.activation(
                    out=lnafm[:, jc, cols], in_=scr1[:, 0:512], func=AF.Ln),
                    waits=[ec])
                ED['A_scr1_free'] = el
        for half in range(2):
            P.wait('T', ED.get('psT_free'))
            P.wait('T', ED.get('A_kntok_done'))
            for j in range(8):
                i = half * 8 + j
                P.op('T', (lambda h, i=i, j=j: h.transpose(
                    ps['T'].ap()[0:64, j * 128:(j + 1) * 128],
                    kntok.ap()[:, i * 64:(i + 1) * 64], ident.ap())))
            evt = P.op('T', (lambda h: h.drain()), inc=True)
            evc = Vf(lambda v, half=half: v.tensor_copy(
                knfm[:, half * 1024:(half + 1) * 1024].rearrange(
                    'p (a n) -> p a n', a=8),
                ps['T'].ap().rearrange('p (a n) -> p a n', a=8)[0:64, :, :]),
                waits=[evt])
            ED['psT_free'] = evc
        P.wait('V', ED.get('A_scr1_free'))
        for h in range(H1):
            jc, half = h // 2, h % 2
            r0 = 64 * half
            zb = zcol.ap()[r0:r0 + 64, :].broadcast_to([C1, T])
            P.op('V', (lambda v, jc=jc, r0=r0, zb=zb: v.tensor_tensor_scan(
                LG[r0:r0 + 64, jc, :], lnafm[r0:r0 + 64, jc, :], zb,
                0.0, ALU.add, ALU.add)))
        VPAD(4)
        escan = P.op('V', (lambda v: v.memset(zcol.ap()[:, 0:1], 0.0)), inc=True)
        ED['A_scan'] = escan
        if 'd_xnfm' in debug:
            P.barrier()
            STORE('DS', dr['d_xnfm'][b], xnfm)
        if 'd_ident' in debug and b == 0:
            P.barrier()
            STORE('DS', dr['d_ident'], ident.ap())
            STORE('DS', dr['d_utm'], utm.ap())

        if 'd_lg' in debug:
            P.barrier()
            STORE('DS', dr['d_lg'][b], LG)
        if 'd_kn' in debug:
            P.barrier()
            STORE('DS', dr['d_kn'][b], knfm)
        if 'd_khat' in debug:
            P.barrier()
            STORE('DS', dr['d_khat'][b], khat)
        if 'd_qlat' in debug:
            P.barrier()
            STORE('DS', dr['d_qlat'][b], qlat)
        LG4 = LG.rearrange('p a (c l) -> p a c l', l=128)
        g4 = gamch.ap().rearrange('p (a c) -> p a c', a=4)
        m4 = midsch.ap().rearrange('p (a c) -> p a c', a=4)
        nm4 = negmid.ap().rearrange('p (a c) -> p a c', a=4)
        ns4 = negst.ap().rearrange('p (a c) -> p a c', a=4)
        P.barrier(engines=('T', 'V', 'S'))
        for jc in range(4):
            LG3 = LG[:, jc, :].rearrange('p (c l) -> p c l', l=128)
            V(lambda v, jc=jc, LG3=LG3: v.tensor_copy(g4[:, jc, :], LG3[:, :, 127]),
              waits=[escan] if jc == 0 else ())
            V(lambda v, jc=jc, LG3=LG3: v.tensor_copy(m4[:, jc, :], LG3[:, :, 63]))
            V(lambda v, jc=jc: v.memset(ns4[:, jc, 0:1], 0.0))
            V(lambda v, jc=jc: v.tensor_copy(ns4[:, jc, 1:16], g4[:, jc, 0:15]))
        if 'd_ends' in debug and b == 0:
            P.barrier()
            evt_ = STORE('DS', dr['d_ends'], gamch.ap())
            for e_ in ('T', 'V', 'S'):
                P.wait(e_, evt_)
        VPAD(3)
        V(lambda v: v.tensor_scalar(negmid.ap(), midsch.ap(), -1.0, None, ALU.mult))
        ecp = V(lambda v: v.tensor_scalar(negst.ap(), negst.ap(), -1.0, None,
                                          ALU.mult))
        VPAD(3)
        if 'd_nst' in debug and b == 0:
            P.barrier()
            evt_ = STORE('DS', dr['d_nst'], negst.ap())
            for e_ in ('T', 'V', 'S'):
                P.wait(e_, evt_)
        P.barrier(engines=('T', 'V', 'S'))
        ega = Vf(lambda v: v.tensor_add(gamch.ap(), gamch.ap(), negst.ap()))
        egb = S_(lambda s: s.activation(out=gamch.ap(), in_=gamch.ap(), func=AF.Exp),
                 waits=[ega])
        P.barrier(engines=('T', 'V', 'S'))
        ez1 = None
        ez2 = None
        ED['A_refs'] = egb

        if '3' not in sub:
            return
        AB4.reset()
        qmid_h = AB4.take(C1, (T,), BF16)
        qst_h = AB4.take(C1, (T,), BF16)
        kmid_h = AB4.take(C1, (T,), BF16)
        AB2.reset()
        qmid_sc = AB2.take(128, (T,), F32)
        ksc_h = AB2.take(C1, (T,), F32)
        qst_sc = AB2.take(128, (T,), F32)
        Sfv = AB2.take(C1, (H1 * DV1,), F32)
        Sbv = AB2.take(C1, (H1 * DV1,), BF16)
        assert AB2.off <= 32768
        ez1 = V(lambda v: v.memset(Sfv, 0.0), waits=[ED.get('A_state_add')])
        ez2 = V(lambda v: v.memset(Sbv, 0.0))

        for jc in range(4):
            ev_sc = None
            for w in (ED.get('A_scan'), ED.get('A_refs'), ED.get('A_scales_used')):
                P.wait('S', w)
            SPAD(4)
            for ci in range(16):
                ccols = slice(ci * 128, (ci + 1) * 128)
                S_(lambda s, jc=jc, ci=ci, ccols=ccols: s.activation(
                    out=qmid_sc[:, ccols], in_=LG[:, jc, ccols], func=AF.Exp,
                    bias=nm4[:, jc, ci:ci + 1]))
                ev_sc = S_(lambda s, jc=jc, ci=ci, ccols=ccols: s.activation(
                    out=qst_sc[:, ccols], in_=LG[:, jc, ccols], func=AF.Exp,
                    bias=ns4[:, jc, ci:ci + 1]))
            if 'd_qmsc' in debug and b == 0 and jc == 0:
                P.barrier()
                evt_ = STORE('DS', dr['d_qmsc'], qmid_sc)
                if 'd_mids' in debug:
                    STORE('DS', dr['d_mids'], midsch.ap())
                    evt_ = STORE('DS', dr['d_negm'], negmid.ap())
                for e_ in ('T', 'V', 'S'):
                    P.wait(e_, evt_)
            for half in range(2):
                h = jc * 2 + half
                r0 = 64 * half
                evgh = Vf(lambda v, r0=r0, jc=jc: v.tensor_copy(
                    gh.ap(), g4[r0:r0 + 64, jc, :]),
                    waits=[egb, ED.get('A_gh_used')])
                if 'd_gh' in debug and b == 0 and h == 0:
                    P.barrier()
                    evt_ = STORE('DS', dr['d_gh'], gh.ap())
                    for e_ in ('T', 'V', 'S'):
                        P.wait(e_, evt_)
                evksc = None
                for ci in range(16):
                    ccols2 = slice(ci * 128, (ci + 1) * 128)
                    SX = Sf if ci == 15 else S_
                    evksc = SX(lambda s, jc=jc, ci=ci, ccols2=ccols2, r0=r0:
                               s.activation(
                        out=ksc_h[:, ccols2], in_=LG[r0:r0 + 64, jc, ccols2],
                        func=AF.Exp, scale=-1.0,
                        bias=m4[r0:r0 + 64, jc, ci:ci + 1]),
                        waits=[ED.get('A_ksc_used')] if ci == 0 else ())
                evkm = Vf(lambda v: v.tensor_mul(kmid_h, knfm, ksc_h),
                    waits=[evksc, ED.get('A_kmid_used')])
                ED['A_ksc_used'] = evkm
                for half2 in range(2):
                    P.wait('T', ED.get('psG2_free'))
                    for j in range(8):
                        i2 = half2 * 8 + j
                        P.op('T', (lambda h_, i2=i2, j=j, hh=h: h_.matmul(
                            ps['G2'].ap()[:, j * 128:(j + 1) * 128],
                            vlat[:, i2 * 128:(i2 + 1) * 128],
                            vc_b[:, hh * 128:(hh + 1) * 128],
                            start=True, stop=True)))
                    evm = P.op('T', (lambda h_: h_.drain()), inc=True)
                    evc = Vf(lambda v, half2=half2: v.tensor_copy(
                        vtok[:, half2 * 8:(half2 + 1) * 8, :],
                        ps['G2'].ap().rearrange('p (a n) -> p a n', a=8)),
                        waits=[evm, ED.get('A_vtok_used')])
                    ED['psG2_free'] = evc
                ev_vtok = ED['psG2_free']
                evq2 = None
                for tg in range(4):
                    cols = slice(tg * 512, (tg + 1) * 512)
                    pp, key = (ps['A'], 'A_psA_done') if tg % 2 == 0 else (ps['B'], 'A_psB_done')
                    evq = mm_group(pp.ap()[0:64, :], [
                        (qc_b[:, h * 64:(h + 1) * 64], qlat[:, cols])],
                        waits=[ED.get(key)])
                    V(lambda v, pp=pp, cols=cols, r0=r0: v.scalar_tensor_tensor(
                        qmid_h[:, cols], pp.ap()[0:64, :], 1.0,
                        qmid_sc[r0:r0 + 64, cols], ALU.mult, ALU.mult),
                      waits=[evq, ED.get('A_qmid_used')])
                    evq2 = Vf(lambda v, pp=pp, cols=cols, r0=r0: v.scalar_tensor_tensor(
                        qst_h[:, cols], pp.ap()[0:64, :], 1.0,
                        qst_sc[r0:r0 + 64, cols], ALU.mult, ALU.mult),
                        waits=[ED.get('A_qst_used')])
                    ED[key] = evq2
                ED['A_scales_used'] = evq2
                if 'd_qmh' in debug and b == 0 and h == 0:
                    P.barrier()
                    STORE('DS', dr['d_qmh'], qmid_h)
                    STORE('DS', dr['d_kmh'], kmid_h)
                    evt_ = STORE('DS', dr['d_vth'],
                                 vtok.rearrange('p a n -> p (a n)'))
                    for e_ in ('T', 'V', 'S'):
                        P.wait(e_, evt_)
                for c in range(NT):
                    ccols = slice(c * 128, (c + 1) * 128)
                    pPt, key = (ps['A'], 'A_psA_done') if c % 2 == 0 else (ps['B'], 'A_psB_done')
                    evP = mm_group(pPt.ap()[:, 0:128], [
                        (kmid_h[:, ccols], qmid_h[:, ccols])],
                        waits=[ED.get(key)])
                    evPm = Vf(lambda v, pPt=pPt, c=c: v.tensor_mul(
                        ptile.ap()[:, (c % 2) * 128:(c % 2) * 128 + 128],
                        pPt.ap()[:, 0:128], utm.ap()),
                        waits=[evP, ED.get(f'A_pt{c % 2}_used')])
                    ED[key] = evPm
                    if 'd_pt' in debug and b == 0 and h == 0 and c == 0:
                        P.barrier()
                        evt_ = STORE('DS', dr['d_pt'], ptile.ap()[:, 0:128])
                        for e_ in ('T', 'V', 'S'):
                            P.wait(e_, evt_)
                    evO = mm_group(ps['G1'].ap()[:, 0:128], [
                        (vtok[:, c, :], ptile.ap()[:, (c % 2) * 128:(c % 2) * 128 + 128]),
                        (Sbv[:, h * 128:(h + 1) * 128], qst_h[:, ccols]),
                    ], waits=[evPm, ev_vtok, ED.get('A_state_copy'),
                              ED.get('A_o_evac'), ez2 if (h == 0 and c == 0) else None])
                    ED[f'A_pt{c % 2}_used'] = evO
                    evoe = Vf(lambda v, h=h, ccols=ccols: v.tensor_copy(
                        ofm[:, h, ccols], ps['G1'].ap()[:, 0:128]), waits=[evO])
                    ED['A_o_evac'] = evoe
                    evDl = mm_group(ps['C'].ap()[0:64, 0:128], [
                        (khat[:, c, h * 64:(h + 1) * 64], vtok[:, c, :])],
                        waits=[ED.get('A_state_add'), ev_vtok, ED.get('A_psC_done')])
                    ED['A_psC_done'] = None
                    evsa = V(lambda v, h=h, c=c: v.scalar_tensor_tensor(
                        Sfv[:, h * 128:(h + 1) * 128],
                        Sfv[:, h * 128:(h + 1) * 128],
                        gh.ap()[:, c:c + 1],
                        ps['C'].ap()[0:64, 0:128], ALU.mult, ALU.add),
                      waits=[evgh, evO, evDl, ez1 if (h == 0 and c == 0) else None])
                    ED['A_state_add'] = evsa
                    VPAD(3)
                    evsc = Vf(lambda v, h=h: v.tensor_copy(
                        Sbv[:, h * 128:(h + 1) * 128],
                        Sfv[:, h * 128:(h + 1) * 128]))
                    ED['A_state_copy'] = evsc
                    if 'd_sf' in debug and b == 0 and h == 0 and c == 0:
                        P.barrier()
                        evt_ = STORE('DS', dr['d_sf'], Sfv[:, 0:128])
                        for e_ in ('T', 'V', 'S'):
                            P.wait(e_, evt_)
                ED['A_gh_used'] = ED['A_state_add']
                ED['A_kmid_used'] = ED[f'A_pt{(NT - 1) % 2}_used']
                ED['A_qmid_used'] = ED['A_kmid_used']
                ED['A_qst_used'] = ED['A_o_evac']
                ED['A_vtok_used'] = ED['A_state_add']
        if 'd_ofm' in debug:
            P.barrier()
            STORE('DS', dr['d_ofm'][b].rearrange('(a p) t -> p a t', p=128), ofm)
        P.barrier()

        if '4' not in sub:
            return
        AB2.reset()
        wot_b = AB2.take(128, (8, D), BF16)
        g1v = AB2.take(128, (D,), F32)
        b1lv = AB2.take(128, (D,), F32)
        ev1 = LOAD('DP', wot_b, dmaj('wot'))
        ev2 = LOAD('DS', g1v, dr['g1_bc'])
        ev3 = LOAD('DS', b1lv, dr['b1l_bc'])
        for e in (ev1, ev2, ev3):
            P.wait('T', e); P.wait('V', e); P.wait('S', e)
        hnfm = xnfm
        for i in range(NT):
            rows = slice(i * 128, (i + 1) * 128)
            evxy = LOAD('DS', x_t, dr['s_xy'][b, rows, :],
                        waits=[ED.get('A4_xt_done')])
            pg, key = (ps['G1'], 'A_o_evac') if True else (None, None)
            pg = ps['G2'] if i % 2 == 0 else ps['G1']
            key = f'A4_pg{i % 2}'
            evo = None
            for gc in range(2):
                evo = mm_group(pg.ap()[:, gc * 512:(gc + 1) * 512], [
                    (ofm[:, hh, rows], wot_b[:, hh, gc * 512:(gc + 1) * 512])
                    for hh in range(8)],
                    waits=[ED.get(key), ED.get('A_o_evac') if i == 0 else None,
                           ED.get('psG2_free') if i == 0 else None])
            ex1 = Vf(lambda v, pg=pg: v.tensor_add(y_t, x_t, pg.ap()),
                     waits=[evo, evxy, ED.get('A4_yt_stored')])
            ED[key] = ex1
            ev1 = STORE('DS', dr['s_x1'][b, rows, :], y_t, waits=[ex1])
            if 'd_x1' in debug:
                ev1 = STORE('DS', dr['d_x1'][b, rows, :], y_t)
            ED['A4_yt_stored'] = ev1
            P.wait('S', ED.get('A4_ehn'))
            mu, rs = ln_stats(y_t, 0, scr1, src_ev=ex1)
            ez = ln_apply(x_t, y_t, mu, rs,
                          waits=[ED.get('A4_zt_stored')], fence=True)
            ev2 = STORE('DS', dr['s_z'][b, rows, :], x_t, waits=[ez])
            if 'd_z' in debug:
                ev2 = STORE('DS', dr['d_z'][b, rows, :], x_t)
            ED['A4_zt_stored'] = ev2
            ED['A4_xt_done'] = ev2
            mu2, rs2 = ln_stats(x_t, 8, scr1, src_ev=ez)
            ln_apply(scr2, x_t, mu2, rs2)
            VPAD(3)
            V(lambda v: v.tensor_mul(scr1, scr2, g1v),
              waits=[ED.get('lnscr_s')])
            VPAD(3)
            ehn = Vf(lambda v: v.tensor_add(xn_bf, scr1, b1lv))
            ED['A4_ehn'] = ehn
            transpose8(xn_bf, hnfm[:, :, i * 128:(i + 1) * 128], waits=[ehn])
        P.barrier()

    def phaseB(b):
        AB2.reset()
        fwq_b = AB2.take(128, (8, D), BF16)
        fwk_b = AB2.take(128, (8, D), BF16)
        oatt = AB2.take(128, (8, T), BF16)
        AB3.reset()
        vpr = AB3.take(128, (16, 16, 65), BF16)
        hnfm = AB1.h.ap().bitcast(BF16)[:, 0:8 * T].rearrange('p (a n) -> p a n', a=8)
        pstr = AB1.h.ap().bitcast(BF16)[:, 8 * T:8 * T + 16 * 1024].rearrange(
            'p (a n) -> p a n', a=16)
        AB4.reset()
        qh0 = AB4.take(C1, (T,), BF16)
        qh1 = AB4.take(C1, (T,), BF16)
        kh0 = AB4.take(C1, (T,), BF16)
        kh1 = AB4.take(C1, (T,), BF16)
        rbc_sb = AB4.take(C1, (512,), F32)
        scrB = AB4.take(128, (D,), F32)
        rrow_v = AB4.take(1, (512,), F32)

        evfv = LOAD('DP', fwq_b, dmaj('fwv'))
        evst = V(lambda v: v.memset(vpr[:, :, :, 64:65], 1.0))
        evbv = LOAD('DS', scrB, dr['bv_bc'])
        P.wait('T', evfv)
        for i in range(NT):
            for vh in range(2):
                pp, key = ((ps['A'], 'B_psA_done') if (i * 2 + vh) % 2 == 0
                           else (ps['B'], 'B_psB_done'))
                ev = mm_group(pp.ap(), [
                    (hnfm[:, dc, i * 128:(i + 1) * 128],
                     fwq_b[:, dc, vh * 512:(vh + 1) * 512]) for dc in range(8)],
                    waits=[ED.get(key)])
                evv = Vf(lambda v, pp=pp, i=i, vh=vh: v.tensor_tensor(
                    vpr[:, i, vh * 8:(vh + 1) * 8, 0:64],
                    pp.ap().rearrange('p (a n) -> p a n', a=8),
                    scrB[:, vh * 512:(vh + 1) * 512].rearrange('p (a n) -> p a n', a=8),
                    ALU.add), waits=[ev, evbv, evst])
                ED[key] = evv
        P.barrier()
        evq = LOAD('DP', fwq_b, dmaj('fwq'))
        evk = LOAD('DP', fwk_b, dmaj('fwk'))
        P.wait('T', evq); P.wait('T', evk)

        for fc in range(8):
            for (wsb, bcol, t0, t1) in ((fwq_b, bqp, qh0, qh1),
                                        (fwk_b, bkp, kh0, kh1)):
                for tg in range(4):
                    cols = slice(tg * 512, (tg + 1) * 512)
                    pp, key = ((ps['A'], 'B_psA_done') if tg % 2 == 0
                               else (ps['B'], 'B_psB_done'))
                    ev = mm_group(pp.ap(), [
                        (wsb[:, dc, fc * 128:(fc + 1) * 128], hnfm[:, dc, cols])
                        for dc in range(8)], waits=[ED.get(key)])
                    V(lambda v, pp=pp, cols=cols, t0=t0, bcol=bcol, fc=fc:
                      v.tensor_scalar(t0[:, cols], pp.ap()[0:64, :],
                                      bcol.ap()[0:64, fc:fc + 1], None, ALU.add),
                      waits=[ev, ED.get('B_qk_used')])
                    ev2 = Vf(lambda v, pp=pp, cols=cols, t1=t1, bcol=bcol, fc=fc:
                             v.tensor_scalar(t1[:, cols], pp.ap()[64:128, :],
                                             bcol.ap()[64:128, fc:fc + 1], None,
                                             ALU.add))
                    ED[key] = ev2
            for half in range(2):
                h = fc * 2 + half
                qh = qh0 if half == 0 else qh1
                kh = kh0 if half == 0 else kh1
                V(lambda v, qh=qh: v.reduce_max(
                    qmxc.ap()[:, 0:1], qh, axis=AX.X, apply_absolute_value=True))
                V(lambda v, kh=kh: v.reduce_max(
                    qmxc.ap()[:, 1:2], kh, axis=AX.X, apply_absolute_value=True))
                VPAD(4)
                ep = Vf(lambda v: v.tensor_mul(qmxc.ap()[:, 2:3], qmxc.ap()[:, 0:1],
                                               qmxc.ap()[:, 1:2]))
                evu = mm_group(ps['C'].ap()[0:1, 0:1], [
                    (qmxc.ap()[:, 2:3], ones64b.ap())],
                    waits=[ep, ED.get('B_psC_done')])
                evn = Vf(lambda v, half=half: v.tensor_scalar(
                    nubc.ap()[:, half * 32:half * 32 + 1], ps['C'].ap()[0:1, 0:1],
                    -0.125, SHIFT, ALU.mult, ALU.add), waits=[evu])
                evb = mm_group(ps['C'].ap()[:, 128:129], [
                    (ones_rf.ap(), nubc.ap()[:, half * 32:half * 32 + 1])], waits=[evn])
                evbc = Vf(lambda v, half=half: v.tensor_copy(
                    biasc2.ap()[:, half * 32:half * 32 + 1], ps['C'].ap()[:, 128:129]),
                    waits=[evb, ED.get(f'B_bias{half}_used')])
                ED['B_psC_done'] = evbc
                P.wait('S', evbc)
                SPAD(4)
                for qc in range(2):
                    evex = None
                    for kc in range(16):
                        pg, key = ((ps['G1'], 'B_G1_done') if kc % 2 == 0
                                   else (ps['G2'], 'B_G2_done'))
                        P.wait('T', ED.get(key))
                        for hseg in range(2):
                            P.op('T', (lambda h_, pg=pg, kc=kc, hseg=hseg,
                                       qh=qh, kh=kh, qc=qc: h_.matmul(
                                pg.ap()[:, hseg * 512:(hseg + 1) * 512],
                                kh[:, kc * 128:(kc + 1) * 128],
                                qh[:, qc * 1024 + hseg * 512:
                                   qc * 1024 + (hseg + 1) * 512],
                                start=True, stop=True)))
                        evm = P.op('T', (lambda h_: h_.drain()), inc=True)
                        evex = Sf(lambda s, pg=pg, kc=kc, half=half: s.activation(
                            out=pstr[:, kc, :], in_=pg.ap(), func=AF.Exp,
                            scale=0.125, bias=biasc2.ap()[:, half * 32:half * 32 + 1]),
                            waits=[evm, ED.get(f'B_pstr{kc}_used')])
                        ED[key] = evex
                        ED[f'B_exp{kc % 2}'] = evex
                    ED[f'B_bias{half}_used'] = evex
                    for sg in range(2):
                        pp, key = ((ps['A'], 'B_psA_done') if sg % 2 == 0
                                   else (ps['B'], 'B_psB_done'))
                        ev = mm_group(pp.ap()[0:65, :], [
                            (vpr[:, kc2, h, :], pstr[:, kc2, sg * 512:(sg + 1) * 512])
                            for kc2 in range(16)],
                            waits=[ED.get(key), ED.get('B_exp0'), ED.get('B_exp1')])
                        for kc2 in range(16):
                            ED[f'B_pstr{kc2}_used'] = ev
                        evr = Vf(lambda v, pp=pp: v.reciprocal(
                            rrow_v, pp.ap()[64:65, :]),
                            waits=[ev, ED.get('B_rrow_used')])
                        evb2 = mm_group(ps['C'].ap()[0:64, 0:512], [
                            (ones_rf.ap()[:, 0:64], rrow_v)],
                            waits=[evr, ED.get('B_psC_done')])
                        ED['B_rrow_used'] = evb2
                        evrb = V(lambda v: v.tensor_copy(rbc_sb,
                                                         ps['C'].ap()[0:64, 0:512]),
                                 waits=[evb2, ED.get('B_rbc_used')])
                        ED['B_psC_done'] = evrb
                        VPAD(3)
                        r0 = 64 * (h % 2)
                        evoa = Vf(lambda v, pp=pp, h=h, qc=qc, sg=sg, r0=r0:
                                  v.scalar_tensor_tensor(
                                      oatt[r0:r0 + 64, h // 2,
                                           qc * 1024 + sg * 512:
                                           qc * 1024 + (sg + 1) * 512],
                                      pp.ap()[0:64, :], 1.0, rbc_sb,
                                      ALU.mult, ALU.mult))
                        ED['B_rbc_used'] = evoa
                        ED[key] = evoa
                ED['B_qk_used'] = ED.get('B_psA_done')
        P.barrier()
        if 'd_oatt' in debug:
            STORE('DS', dr['d_oatt'][b].rearrange('(a p) t -> p a t', p=128), oatt)
            P.barrier()

        AB2.off = 0
        fwo_b = AB2.take(128, (8, D), BF16)
        g2v = AB2.take(128, (D,), F32)
        b2lv = AB2.take(128, (D,), F32)
        bov = AB2.take(128, (D,), F32)
        ev1 = LOAD('DP', fwo_b, dmaj('fwo'))
        ev2 = LOAD('DS', g2v, dr['g2_bc'])
        ev3 = LOAD('DS', b2lv, dr['b2l_bc'])
        ev4 = LOAD('DS', bov, dr['bo_bc'])
        for e in (ev1, ev2, ev3, ev4):
            P.wait('T', e); P.wait('V', e); P.wait('S', e)
        AB4.off = 0
        z_t = AB4.take(128, (D,), F32)
        x1p = AB4.take(128, (D,), F32)
        x1_t = AB4.take(128, (D,), F32)
        scr1 = AB4.take(128, (D,), F32)
        scr2b = AB4.take(128, (D,), F32)
        hn2bf = AB4.take(128, (D,), BF16)
        hn2f_t = AB4.take(128, (8, 128), BF16)
        lgt = AB4.take(128, (NT, 8), F32)

        for i in range(NT):
            rows = slice(i * 128, (i + 1) * 128)
            evz = LOAD('DS', z_t, dr['s_z'][b, rows, :], waits=[ED.get('B4_zt_done')])
            evx1 = LOAD('DS', x1_t, dr['s_x1'][b, rows, :],
                        waits=[ED.get('B4_x1_done')])
            pg = ps['G1'] if i % 2 == 0 else ps['G2']
            key = f'B4_pg{i % 2}'
            evo = None
            for gc in range(2):
                evo = mm_group(pg.ap()[:, gc * 512:(gc + 1) * 512], [
                    (oatt[:, hh, rows], fwo_b[:, hh, gc * 512:(gc + 1) * 512])
                    for hh in range(8)], waits=[ED.get(key)])
            ezt = V(lambda v, pg=pg: v.tensor_add(x1p, z_t, pg.ap()),
                    waits=[evo, evz])
            ED['B4_zt_done'] = ezt
            ED[key] = ezt
            VPAD(3)
            ex1p = Vf(lambda v: v.tensor_add(x1p, x1p, bov))
            if 'd_x1p' in debug:
                STORE('DS', dr['d_x1p'][b, rows, :], x1p, waits=[ex1p])
            VPAD(2)
            eob = Vf(lambda v: v.tensor_add(scr2b, x1_t, x1p),
                     waits=[evx1, ED.get('B4_base_stored')])
            ED['B4_x1_done'] = eob
            evbs = STORE('DS', dr['out'][b, rows, :], scr2b, waits=[eob])
            ED['B4_base_stored'] = evbs
            P.wait('S', ED.get('B4_ehn'))
            mu, rs = ln_stats(x1p, 0, scr1, src_ev=ex1p)
            ln_apply(scr2b, x1p, mu, rs, waits=[evbs])
            VPAD(3)
            V(lambda v: v.tensor_mul(scr1, scr2b, g2v),
              waits=[ED.get('lnscr_s')])
            VPAD(3)
            ehn = Vf(lambda v: v.tensor_add(hn2bf, scr1, b2lv))
            ED['B4_ehn'] = ehn
            evcp = transpose8(hn2bf, hn2f_t, waits=[ehn])
            c0 = b * T + i * 128
            evh = STORE('DS', dr['s_hn2'][:, :, c0:c0 + 128].rearrange(
                'd p t -> p d t'), hn2f_t, waits=[evcp])
            if 'd_hn2' in debug:
                evh = STORE('DS', dr['d_hn2'][:, :, c0:c0 + 128].rearrange(
                    'd p t -> p d t'), hn2f_t)
            ED['B4_hn2_stored'] = evh
            P.wait('V', evh)
            evg = mm_group(ps['C'].ap()[:, 256:264], [
                (hn2f_t[:, dc, :],
                 gatewb.ap().rearrange('p (a n) -> p a n', a=8)[:, dc, :])
                for dc in range(8)], waits=[evcp, ED.get('B_psC_done')])
            evl = V(lambda v: v.scalar_tensor_tensor(
                lgt_st.ap()[:, 0:8], ps['C'].ap()[:, 256:264], 1.0, gatebc.ap(),
                ALU.mult, ALU.add), waits=[evg])
            ED['B_psC_done'] = evl
            VPAD(3)
            V(lambda v, i=i: v.tensor_copy(lgt[:, i, :], lgt_st.ap()[:, 0:8]))
        P.barrier()
        for i in range(NT):
            evm = V(lambda v, i=i: v.reduce_max(tmpnt.ap()[:, i:i + 1],
                                                lgt[:, i, :], axis=AX.X))
            VPAD(4)
            V(lambda v, i=i: v.tensor_tensor(
                equ8.ap()[:, i * 8:(i + 1) * 8], lgt[:, i, :],
                tmpnt.ap()[:, i:i + 1].broadcast_to([128, 8]), ALU.is_equal),
              waits=[evm])
        mvw = masks.ap().rearrange('p (b2 a n) -> p b2 a n', b2=BL, a=NT)
        V(lambda v: v.memset(keepb.ap(), 1.0))
        VPAD(3)
        eqv = V(lambda v, b=b: v.select(
            masks.ap()[:, b * 128:(b + 1) * 128], equ8.ap(), onessm.ap(),
            zerossm.ap()))
        VPAD(3)
        m3 = masks.ap()[:, b * 128:(b + 1) * 128].rearrange('p (a n) -> p a n', a=NT)
        for e_ in range(8):
            if e_ > 0:
                V(lambda v, e_=e_: v.tensor_mul(m3[:, :, e_], m3[:, :, e_],
                                                keepb.ap()), waits=[eqv])
                VPAD(3)
            if e_ < 7:
                V(lambda v, e_=e_: v.tensor_scalar(
                    tmpnt.ap(), m3[:, :, e_], -1.0, 1.0, ALU.mult, ALU.add),
                  waits=[eqv])
                VPAD(3)
                V(lambda v: v.tensor_mul(keepb.ap(), keepb.ap(), tmpnt.ap()))
                VPAD(3)
        if 'd_mask' in debug:
            P.barrier()
            STORE('DS', dr['d_mask'][:, b, :, :], m3)
        for i in range(NT):
            rows = slice(i * 128, (i + 1) * 128)
            evmb = Vf(lambda v, i=i: v.tensor_copy(maskbf.ap(), m3[:, i, :]),
                      waits=[ED.get('B5_maskbf_used')])
            P.wait('T', ED.get('psT_free'))
            P.wait('T', evmb)
            P.op('T', (lambda h: h.transpose(ps['T'].ap()[0:8, 0:128],
                                             maskbf.ap(), ident.ap())))
            evt = P.op('T', (lambda h: h.drain()), inc=True)
            ED['B5_maskbf_used'] = evt
            evo = Vf(lambda v: v.tensor_copy(ohT.ap(), ps['T'].ap()[0:8, 0:128]),
                     waits=[evt])
            ED['psT_free'] = evo
            evs = None
            for gc in range(2):
                evs = mm_group(ps['G1'].ap()[:, gc * 512:(gc + 1) * 512], [
                    (ohT.ap(), b2tb.ap()[:, gc * 512:(gc + 1) * 512])],
                    waits=[evo, ED.get('B5_G1_done') if gc == 0 else None])
            evc = Vf(lambda v: v.tensor_copy(scr2b, ps['G1'].ap()),
                     waits=[evs, ED.get('B5_scr_used')])
            ED['B5_G1_done'] = evc
            eva = ACCUM('DP', dr['out'][b, rows, :], scr2b,
                        waits=[evc, ED.get('B4_base_stored')])
            ED['B5_scr_used'] = eva
        P.barrier()

    def phaseC(nseq=BL, nexp=E):
        AB1.reset(); AB2.reset(); AB3.reset(); AB4.reset()
        w1b = AB1.take(128, (8, FF), BF16)
        w2b = AB2.take(128, (32, D), BF16)
        midb = AB3.take(128, (32, 512), BF16)
        h2a = AB4.take(128, (8, 512), BF16)
        h2b = AB4.take(128, (8, 512), BF16)
        y_t = AB4.take(128, (D,), F32)
        y_t2 = AB4.take(128, (D,), F32)
        prev_accum_count = P.count['DP']
        base_stores_done = ('DS', P.count['DS'])
        for e_ in range(nexp):
            evw1 = LOAD('DP', w1b, dr['w1'][e_].rearrange('(a p) n -> p a n', p=128),
                        waits=[ED.get('C_w_used')])
            evw2 = LOAD('DP', w2b, dr['w2'][e_].rearrange('(a p) n -> p a n', p=128))
            P.wait('T', evw1); P.wait('T', evw2)
            this_expert_first_accum = True
            for tc in range(4 * nseq):
                h2 = h2a if tc % 2 == 0 else h2b
                c0 = tc * 512
                evh = LOAD('DS', h2, dr['s_hn2'][:, :, c0:c0 + 512].rearrange(
                    'd p t -> p d t'), waits=[ED.get(f'C_h2{tc % 2}_used')])
                P.wait('T', evh)
                for fc in range(32):
                    pp, key = ((ps['A'], 'C_psA_done') if fc % 2 == 0
                               else (ps['B'], 'C_psB_done'))
                    ev = mm_group(pp.ap(), [
                        (w1b[:, dc, fc * 128:(fc + 1) * 128], h2[:, dc, :])
                        for dc in range(8)], waits=[ED.get(key)])
                    bcol = b1cc.ap()[:, e_ * 32 + fc: e_ * 32 + fc + 1]
                    if fc % 2 == 0:
                        eve = Vf(lambda v, pp=pp, fc=fc, bcol=bcol: v.tensor_scalar(
                            midb[:, fc, :], pp.ap(), bcol, 0.0, ALU.add, ALU.max),
                            waits=[ev])
                    else:
                        eve = Sf(lambda s, pp=pp, fc=fc, bcol=bcol: s.activation(
                            out=midb[:, fc, :], in_=pp.ap(), func=AF.Relu,
                            bias=bcol), waits=[ev])
                    ED[key] = eve
                ED[f'C_h2{tc % 2}_used'] = ED['C_psA_done']
                P.wait('T', ED['C_psA_done']); P.wait('T', ED['C_psB_done'])
                for tt in range(4):
                    g = tc * 4 + tt
                    bq, iq = g // NT, g % NT
                    yt = y_t if tt % 2 == 0 else y_t2
                    pgt = ps['G1'] if tt % 2 == 0 else ps['G2']
                    key = f'C_pg{tt % 2}'
                    evy = None
                    for gc in range(2):
                        evy = mm_group(pgt.ap()[:, gc * 512:(gc + 1) * 512], [
                            (midb[:, fc, tt * 128:(tt + 1) * 128],
                             w2b[:, fc, gc * 512:(gc + 1) * 512])
                            for fc in range(32)],
                            waits=[ED.get(key)] if gc == 0 else [])
                    mcol = masks.ap().rearrange('p (b2 a n) -> p b2 a n',
                                                b2=BL, a=NT)[:, bq, iq, e_:e_ + 1]
                    evm = Vf(lambda v, pgt=pgt, yt=yt, mcol=mcol: v.tensor_scalar(
                        yt, pgt.ap(), mcol, None, ALU.mult),
                        waits=[evy, ED.get(f'C_yt{tt % 2}_accumed')])
                    ED[key] = evm
                    waits = [evm]
                    if this_expert_first_accum:
                        waits.append(('DP', prev_accum_count))
                        waits.append(base_stores_done)
                        this_expert_first_accum = False
                    eva = ACCUM('DP', dr['out'][bq, iq * 128:(iq + 1) * 128, :],
                                yt, waits=waits)
                    ED[f'C_yt{tt % 2}_accumed'] = eva
            ED['C_w_used'] = ED['C_psA_done']
            prev_accum_count = P.count['DP']
        P.wait('DP', ('DP', P.count['DP']))
        P.wait('DS', ('DS', P.count['DS']))
        P.barrier()

    for b in range(nseq):
        if 'A' in phases:
            sub = ''.join(c for c in phases if c.isdigit()) or '1234'
            phaseA(b, sub)
        if 'B' in phases:
            phaseB(b)
    P.barrier()
    if 'C' in phases:
        phaseC(nseq, nexp)
    P.wait('DP', ('DP', P.count['DP']))
    P.wait('DS', ('DS', P.count['DS']))

    P.emit_into(block, sems)
    st.close()
    return nc


def kernel(**inputs):
    lay = host_layout(inputs)
    x = np.ascontiguousarray(np.asarray(inputs['L0_attn_x'], np.float32))
    nc = build()
    in_maps = []
    for i in range(NCORES):
        m = {'x': np.ascontiguousarray(x[i * BL:(i + 1) * BL])}
        m.update(lay)
        in_maps.append(m)
    res = run_bass_kernel_spmd(nc, in_maps, core_ids=list(range(NCORES)))
    return np.concatenate([r['out'] for r in res.results], axis=0)


# version 03
# speedup vs baseline: 1.0332x; 1.0332x over previous
import numpy as np
import ml_dtypes
import concourse.bass as bass
import concourse.mybir as mybir
from concourse.bass_utils import run_bass_kernel_spmd
from contextlib import ExitStack

F32 = mybir.dt.float32
BF16 = mybir.dt.bfloat16
U8 = mybir.dt.uint8
AF = mybir.ActivationFunctionType
ALU = mybir.AluOpType
AX = mybir.AxisListType
bfloat16 = ml_dtypes.bfloat16

NCORES = 8
BL = 4
T = 2048
D = 1024
NT = T // 128
H1, C1, DV1 = 8, 64, 128
E, FF = 8, 4096
LNEPS = 1e-5
SHIFT = 30.0
CUT = 99
ABL = set()
NTILE_DEV = 16

DMAQ = ('DS', 'DP')
ENGINES = ('T', 'V', 'S') + DMAQ
CHANNELS = ENGINES + ('DSq', 'DPq')


class Prog:
    def __init__(self):
        self.ops = {e: [] for e in ENGINES}
        self.count = {c: 0 for c in CHANNELS}
        self.waited = {e: {p: 0 for p in CHANNELS} for e in ENGINES}
        self.qself = {q: 0 for q in DMAQ}

    def op(self, eng, fn, inc=False, sem=None):
        ch = sem or eng
        step = (16 if ch in DMAQ else 1) if inc else 0
        self.count[ch] += step
        self.ops[eng].append(('op', fn, ch if step else None, step))
        return (ch, self.count[ch])

    def dma(self, q, fn):
        return self.op(q, fn, inc=True)

    def wait(self, eng, ev):
        if ev is None:
            return
        peng, val = ev
        if peng in DMAQ:
            val = self.count[peng]
            if self.qself[peng] < val:
                self.qself[peng] = val
                self.ops[peng].append(('wait', peng, val, 0))
        if peng == eng and eng not in DMAQ:
            return
        if peng == eng and eng in DMAQ:
            self.ops[eng].append(('wait', peng, val, 0))
            return
        if self.waited[eng][peng] >= val:
            return
        self.waited[eng][peng] = val
        self.ops[eng].append(('wait', peng, val, 0))

    def barrier(self, engines=ENGINES):
        evs = []
        for e in engines:
            sem = e + 'q' if e in DMAQ else None
            evs.append(self.op(e, (lambda h: h.nop()), inc=True, sem=sem))
        for e in engines:
            for ev in evs:
                self.wait(e, ev)

    def emit_into(self, block, sems):
        prog = self

        def run_stream(handle, eng):
            for kind, a, b, c in prog.ops[eng]:
                if kind == 'wait':
                    handle.wait_ge(sems[a], b)
                else:
                    inst = a(handle)
                    if b is not None:
                        inst.then_inc(sems[b], c)

        @block.tensor
        def _(t):
            run_stream(t, 'T')

        @block.vector
        def _(v):
            run_stream(v, 'V')

        @block.scalar
        def _(s):
            run_stream(s, 'S')

        @block.sync
        def _(s):
            run_stream(s, 'DS')

        @block.gpsimd
        def _(g):
            run_stream(g, 'DP')


class Arena:
    def __init__(self, handle, nbytes, name, align=512):
        self.h = handle
        self.nbytes = nbytes
        self.off = 0
        self.name = name
        self.align = align

    def reset(self, off=0):
        self.off = off

    def take(self, parts, shape, dtype):
        isz = 4 if dtype == F32 else (1 if dtype == U8 else 2)
        cols = int(np.prod(shape))
        start = (self.off + self.align - 1) // self.align * self.align
        self.off = start + cols * isz
        assert self.off <= self.nbytes, (
            f'arena {self.name} overflow: {self.off} > {self.nbytes}')
        ap = self.h.ap()
        if dtype != F32:
            ap = ap.bitcast(dtype)
        v = ap[0:parts, start // isz: start // isz + cols]
        if len(shape) > 1:
            pat = ' '.join(f'a{i}' for i in range(len(shape)))
            kw = {f'a{i}': s for i, s in enumerate(shape)}
            v = v.rearrange(f'p ({pat}) -> p {pat}', **kw)
        return v


def _pc(vec, parts=128):
    return np.ascontiguousarray(np.asarray(vec, np.float32).reshape(-1, parts).T)


def _bc(vec, parts=128):
    return np.ascontiguousarray(np.broadcast_to(
        np.asarray(vec, np.float32)[None, :], (parts, len(vec))))


def host_layout(inputs):
    g = lambda k: np.asarray(inputs[k], dtype=np.float32)
    d = {}
    d['wqt'] = np.ascontiguousarray(g('L0_attn_Wq').T)
    d['wkt'] = np.ascontiguousarray(g('L0_attn_Wk').T)
    d['wvt'] = np.ascontiguousarray(g('L0_attn_Wv').T)
    d['wat'] = np.ascontiguousarray(g('L0_attn_Wa_w').T)
    d['qc'] = np.ascontiguousarray(g('L0_attn_Qc').transpose(1, 0, 2).reshape(C1, H1 * C1))
    d['vc'] = np.ascontiguousarray(g('L0_attn_Vc').transpose(1, 0, 2).reshape(C1, H1 * DV1))
    d['wot'] = np.ascontiguousarray(g('L0_attn_Wo').T)
    d['wab_pc'] = _pc(g('L0_attn_Wa_b'))
    d['wab_bc'] = _bc(g('L0_attn_Wa_b'))
    d['gam_bc'] = _bc(g('L0_attn_gamma'))
    d['bet_bc'] = _bc(g('L0_attn_beta'))
    d['g1_bc'] = _bc(g('L0_ffn_ln1_g'))
    d['b1l_bc'] = _bc(g('L0_ffn_ln1_b'))
    d['g2_bc'] = _bc(g('L0_ffn_ln2_g'))
    d['b2l_bc'] = _bc(g('L0_ffn_ln2_b'))
    d['bo_bc'] = _bc(g('L0_ffn_bo'))
    d['bv_bc'] = _bc(g('L0_ffn_bv'))
    d['bq_pc'] = _pc(g('L0_ffn_bq'))
    d['bk_pc'] = _pc(g('L0_ffn_bk'))
    d['fwq'] = np.ascontiguousarray(g('L0_ffn_wq'))
    d['fwk'] = np.ascontiguousarray(g('L0_ffn_wk'))
    d['fwv'] = np.ascontiguousarray(g('L0_ffn_wv'))
    d['fwo'] = np.ascontiguousarray(g('L0_ffn_wo'))
    d['gatew'] = np.ascontiguousarray(g('L0_ffn_gate_w'))
    d['gateb_bc'] = _bc(g('L0_ffn_gate_b'))
    d['b2t'] = np.ascontiguousarray(g('L0_ffn_b2'))
    d['b1c'] = np.ascontiguousarray(g('L0_ffn_b1').reshape(E * (FF // 128), 128).T)
    d['w1'] = np.ascontiguousarray(g('L0_ffn_w1'))
    d['w2'] = np.ascontiguousarray(g('L0_ffn_w2'))
    d['ident_bf'] = np.eye(128, dtype=np.float32).astype(bfloat16)
    d['ut_f32'] = np.triu(np.ones((128, 128), np.float32))
    d['lt_strict_bf'] = (np.arange(128)[:, None] > np.arange(128)[None, :]
                         ).astype(np.float32).astype(bfloat16)
    d['ones_r_f32'] = np.ones((1, 128), np.float32)
    d['fconsts'] = np.broadcast_to(np.array([[LNEPS]], np.float32), (128, 1)).copy()
    d['ones64_bf'] = np.ones((C1, 1), np.float32).astype(bfloat16)
    return d


WEIGHT_SPECS = [
    ('wqt', (D, C1), F32), ('wkt', (D, C1), F32), ('wvt', (D, C1), F32),
    ('wat', (D, 512), F32),
    ('qc', (C1, H1 * C1), F32), ('vc', (C1, H1 * DV1), F32),
    ('wot', (D, D), F32),
    ('wab_pc', (128, 4), F32), ('wab_bc', (128, 512), F32),
    ('gam_bc', (128, D), F32), ('bet_bc', (128, D), F32),
    ('g1_bc', (128, D), F32), ('b1l_bc', (128, D), F32),
    ('g2_bc', (128, D), F32), ('b2l_bc', (128, D), F32),
    ('bo_bc', (128, D), F32), ('bv_bc', (128, D), F32),
    ('bq_pc', (128, 8), F32), ('bk_pc', (128, 8), F32),
    ('fwq', (D, D), F32), ('fwk', (D, D), F32), ('fwv', (D, D), F32),
    ('fwo', (D, D), F32),
    ('gatew', (D, 8), F32), ('gateb_bc', (128, 8), F32),
    ('b2t', (E, D), F32), ('b1c', (128, E * 32), F32),
    ('w1', (E, D, FF), F32), ('w2', (E, FF, D), F32),
    ('ident_bf', (128, 128), BF16), ('ut_f32', (128, 128), F32),
    ('lt_strict_bf', (128, 128), BF16),
    ('ones_r_f32', (1, 128), F32), ('ones64_bf', (C1, 1), BF16),
    ('fconsts', (128, 1), F32),
]

DEBUG_TAPS = {
    'd_xy': ([BL, T, D], F32), 'd_x1': ([BL, T, D], F32),
    'd_z': ([BL, T, D], F32), 'd_ofm': ([BL, H1 * DV1, T], BF16),
    'd_hn2': ([8, 128, BL * T], BF16), 'd_oatt': ([BL, D, T], BF16),
    'd_x1p': ([BL, T, D], F32), 'd_mask': ([128, BL, NT, 8], F32),
    'd_lg': ([BL, 128, 4, T], F32), 'd_kn': ([BL, C1, T], BF16),
    'd_xn': ([BL, T, D], BF16), 'd_xnfm': ([BL, 128, 8, T], BF16),
    'd_ident': ([128, 128], BF16), 'd_utm': ([128, 128], F32),
    'd_gamv': ([128, D], F32), 'd_qmsc': ([128, T], F32),
    'd_qmh': ([C1, T], BF16), 'd_kmh': ([C1, T], BF16), 'd_vth': ([128, NT * DV1], BF16),
    'd_mids': ([128, 64], F32), 'd_negm': ([128, 64], F32),
    'd_khat': ([BL, 128, NT, 512], BF16), 'd_qlat': ([BL, C1, T], BF16),
    'd_pt': ([128, 128], BF16), 'd_sf': ([C1, 128], F32), 'd_gh': ([C1, 16], F32),
    'd_nst': ([128, 64], F32), 'd_ends': ([128, 64], F32),
    'd_lnat': ([128, 512], BF16), 'd_eks': ([128, 512], F32),
    'd_apre': ([128, 512], F32),
}


def build(debug=(), phases='ABC', nseq=BL, nexp=E):
    nc = bass.Bass('TRN2', target_bir_lowering=False, detect_race_conditions=False)
    P = Prog()
    ED = {}

    dr = {}
    dr['x'] = nc.dram_tensor('x', [BL, T, D], F32, kind='ExternalInput').ap()
    for name, shape, dt in WEIGHT_SPECS:
        dr[name] = nc.dram_tensor(name, list(shape), dt, kind='ExternalInput').ap()
    dr['out'] = nc.dram_tensor('out', [BL, T, D], F32, kind='ExternalOutput').ap()
    dr['s_xy'] = nc.dram_tensor('s_xy', [BL, T, D], F32, kind='Internal').ap()
    dr['s_x1'] = nc.dram_tensor('s_x1', [BL, T, D], F32, kind='Internal').ap()
    dr['s_z'] = nc.dram_tensor('s_z', [BL, T, D], F32, kind='Internal').ap()
    dr['s_hn2'] = nc.dram_tensor('s_hn2', [8, 128, BL * T], BF16, kind='Internal').ap()
    for tname in debug:
        shape, dt = DEBUG_TAPS[tname]
        dr[tname] = nc.dram_tensor(tname, shape, dt, kind='ExternalOutput').ap()

    st = ExitStack()
    AB1 = Arena(st.enter_context(nc.sbuf_tensor('AB1', [128, 65536 // 4], F32)), 65536, 'AB1')
    AB2 = Arena(st.enter_context(nc.sbuf_tensor('AB2', [128, 65536 // 4], F32)), 65536, 'AB2')
    AB3 = Arena(st.enter_context(nc.sbuf_tensor('AB3', [128, 33280 // 4], F32)), 33280, 'AB3')
    AB4 = Arena(st.enter_context(nc.sbuf_tensor('AB4', [128, 31872 // 4], F32)), 31872, 'AB4')

    DED = Arena(st.enter_context(nc.sbuf_tensor('DED', [128, 16512 // 4], F32)),
                16512, 'DED', align=128)

    class _W:
        __slots__ = ('v',)
        def __init__(self, v):
            self.v = v
        def ap(self):
            return self.v

    def ded(name, parts, cols, dt):
        return _W(DED.take(parts, (cols,), dt))

    zcol = ded('zcol', 128, 1, F32)
    fconsts = ded('fconsts', 128, 1, F32)
    nc.const_aps.register(F32, LNEPS, fconsts.ap()[:, 0:1])
    qmxc = ded('qmxc', C1, 4, BF16)
    maskbf = ded('maskbf', 128, 8, BF16)
    ohT = ded('ohT', 8, 128, BF16)
    ident = ded('identt', 128, 128, BF16)
    utm = ded('utm', 128, 128, F32)
    ltm = ded('ltm', 128, 128, BF16)
    ones_rf = ded('ones_rf', 1, 128, F32)
    ones64b = ded('ones64b', C1, 1, BF16)
    statc = ded('statc', 128, 16, F32)
    wabp = ded('wabp', 128, 4, F32)
    bqp = ded('bqp', 128, 8, F32)
    bkp = ded('bkp', 128, 8, F32)
    gatebc = ded('gatebc', 128, 8, F32)
    gatewb = ded('gatewb', 128, 64, BF16)
    b2tb = ded('b2tb', 8, D, BF16)
    b1cc = ded('b1cc', 128, E * 32, F32)
    gamch = ded('gamch', 128, 64, F32)
    midsch = ded('midsch', 128, 64, F32)
    negmid = ded('negmid', 128, 64, F32)
    negst = ded('negst', 128, 64, F32)
    gh = ded('gh', C1, 16, F32)
    kntok = ded('kntok', 128, NT * C1, BF16)
    klatf = ded('klatf', 128, 512, F32)
    ptile = ded('ptile', 128, 256, BF16)
    masks = ded('masks', 128, BL * NT * 8, F32)
    equ8 = ded('equ8', 128, NT * 8, U8)
    keepb = ded('keepb', 128, NT, F32)
    tmpnt = ded('tmpnt', 128, NT, F32)
    onessm = ded('onessm', 128, 128, F32)
    zerossm = ded('zerossm', 128, 128, F32)
    biasc2 = ded('biasc2', 128, 64, F32)
    nubc = ded('nubc', 1, 64, F32)
    lgt_st = ded('lgt_st', 128, 32, F32)
    rncol = ded('rncol', 128, 96, F32)

    ps = {}
    ps['A'] = st.enter_context(nc.psum_tensor('psA', [128, 512], F32))
    ps['B'] = st.enter_context(nc.psum_tensor('psB', [128, 512], F32))
    ps['C'] = st.enter_context(nc.psum_tensor('psC', [128, 512], F32))
    ps['G1'] = st.enter_context(nc.psum_tensor('psG1', [128, 1024], F32))
    ps['G2'] = st.enter_context(nc.psum_tensor('psG2', [128, 1024], F32))
    ps['T'] = st.enter_context(nc.psum_tensor('psT', [128, 1024], BF16))

    sems = {e: st.enter_context(nc.semaphore(f'sem{e}')) for e in CHANNELS}
    block = st.enter_context(nc.Block())

    def LOAD(q, dst_ap, src_ap, waits=()):
        for w in waits:
            P.wait(q, w)
        return P.dma(q, lambda h, dd=dst_ap, ss=src_ap: h.dma_start(out=dd, in_=ss))

    STORE = LOAD

    def ACCUM(q, dst_ap, src_ap, waits=()):
        for w in waits:
            P.wait(q, w)
        return P.dma(q, lambda h, dd=dst_ap, ss=src_ap: h.dma_start(
            out=dd, in_=ss, accum_op=ALU.add))

    def V(fn, inc=True, waits=()):
        for w in waits:
            P.wait('V', w)
        return P.op('V', fn, inc=inc)

    def Vf(fn, waits=()):
        for w in waits:
            P.wait('V', w)
        P.op('V', fn)
        return P.op('V', (lambda v: v.memset(zcol.ap()[:, 0:1], 0.0)), inc=True)

    def S_(fn, inc=True, waits=()):
        for w in waits:
            P.wait('S', w)
        return P.op('S', fn, inc=inc)

    def Sf(fn, waits=()):
        for w in waits:
            P.wait('S', w)
        P.op('S', fn)
        return P.op('S', (lambda s: s.activation(out=zcol.ap()[:, 0:1],
                                                 in_=zcol.ap()[:, 0:1],
                                                 func=AF.Copy)), inc=True)

    def VPAD(n=4):
        for _ in range(n):
            P.op('V', (lambda v: v.memset(zcol.ap()[:, 0:1], 0.0)))

    def SPAD(n=4):
        for _ in range(n):
            P.op('S', (lambda s: s.activation(out=zcol.ap()[:, 0:1],
                                              in_=zcol.ap()[:, 0:1],
                                              func=AF.Copy)))

    def mm_group(out_ap, pairs, waits=(), fence=True):
        for w in waits:
            P.wait('T', w)
        n = len(pairs)
        for j, (lh, rh) in enumerate(pairs):
            ev = P.op('T', (lambda h, o=out_ap, l=lh, r=rh, s=(j == 0),
                            p=(j == n - 1): h.matmul(o, l, r, start=s, stop=p)),
                      inc=(not fence and j == n - 1))
        if fence:
            ev = P.op('T', (lambda h: h.drain()), inc=True)
        return ev

    def transpose8(src_bf, dst_view, nblk=8, waits=()):
        P.wait('T', ED.get('psT_free'))
        for w in waits:
            P.wait('T', w)
        for k in range(nblk):
            P.op('T', (lambda h, k=k: h.transpose(
                ps['T'].ap()[:, k * 128:(k + 1) * 128],
                src_bf[:, k * 128:(k + 1) * 128], ident.ap())))
        evt = P.op('T', (lambda h: h.drain()), inc=True)
        evc = Vf(lambda v: v.tensor_copy(
            dst_view, ps['T'].ap().rearrange('p (a n) -> p a n', a=8)[:, 0:nblk, :]),
            waits=[evt])
        ED['psT_free'] = evc
        return evc

    def dmaj(name):
        return dr[name].rearrange('(a p) n -> p a n', p=128)

    def ln_stats(src_ap, c0, scr, tag='s', src_ev=None):
        sc = statc.ap()
        S_(lambda s: s.activation(out=scr, in_=src_ap, func=AF.Copy,
                                  accum_out=sc[:, c0 + 2:c0 + 3]),
           waits=[ED.get(f'lnscr_{tag}'), src_ev])
        e2 = Sf(lambda s: s.activation(out=scr, in_=src_ap, func=AF.Square,
                                       accum_out=sc[:, c0 + 3:c0 + 4]))
        ED[f'lnscr_{tag}'] = e2
        V(lambda v: v.tensor_scalar(sc[:, c0:c0 + 1], sc[:, c0 + 2:c0 + 3],
                                    1.0 / D, None, ALU.mult), waits=[e2])
        V(lambda v: v.tensor_scalar(sc[:, c0 + 5:c0 + 6], sc[:, c0 + 3:c0 + 4],
                                    1.0 / D, None, ALU.mult))
        VPAD(3)
        V(lambda v: v.tensor_mul(sc[:, c0 + 4:c0 + 5], sc[:, c0:c0 + 1],
                                 sc[:, c0:c0 + 1]))
        VPAD(4)
        e4 = Vf(lambda v: v.tensor_tensor(sc[:, c0 + 5:c0 + 6],
                                          sc[:, c0 + 5:c0 + 6],
                                          sc[:, c0 + 4:c0 + 5], ALU.subtract))
        S_(lambda s: s.activation(out=sc[:, c0 + 5:c0 + 6], in_=sc[:, c0 + 5:c0 + 6],
                                  func=AF.Ln, bias=fconsts.ap()[:, 0:1]), waits=[e4])
        e5 = Sf(lambda s: s.activation(out=sc[:, c0 + 1:c0 + 2],
                                       in_=sc[:, c0 + 5:c0 + 6],
                                       func=AF.Exp, scale=-0.5))
        P.wait('V', e5)
        VPAD(2)
        return sc[:, c0:c0 + 1], sc[:, c0 + 1:c0 + 2]

    def ln_apply(dst, src, mu, rs, waits=(), fence=False):
        fn = Vf if fence else V
        return fn(lambda v: v.tensor_scalar(dst, src, mu, rs,
                                            ALU.subtract, ALU.mult), waits=waits)

    ev_const = []
    for name, tile in [('ident_bf', ident), ('ut_f32', utm), ('lt_strict_bf', ltm),
                       ('ones_r_f32', ones_rf), ('ones64_bf', ones64b),
                       ('wab_pc', wabp), ('bq_pc', bqp),
                       ('bk_pc', bkp), ('gateb_bc', gatebc), ('b1c', b1cc), ('fconsts', fconsts)]:
        ev_const.append(LOAD('DS', tile.ap(), dr[name]))
    if 'b2l' not in ABL:
        ev_const.append(LOAD('DP', b2tb.ap(), dr['b2t']))
        ev_const.append(LOAD('DP', gatewb.ap().rearrange('p (a n) -> p a n', a=8),
                             dmaj('gatew')))
    V(lambda v: v.memset(onessm.ap(), 1.0))
    V(lambda v: v.memset(zerossm.ap(), 0.0))
    V(lambda v: v.memset(zcol.ap(), 0.0))
    for ev in ev_const:
        P.wait('V', ev); P.wait('S', ev); P.wait('T', ev)
    SPAD(4)
    VPAD(4)

    def phaseA(b, sub='1234'):
        AB1.reset(); AB2.reset(); AB3.reset(); AB4.reset()
        xnfm = AB1.take(128, (8, T), BF16)
        ofm = AB1.take(128, (H1, T), BF16)
        gamv = AB2.take(128, (D,), F32)
        betv = AB2.take(128, (D,), F32)
        wkt_b = AB2.take(128, (8, 512), BF16)
        wabbc_v = AB2.take(128, (512,), F32)
        khat = AB3.take(128, (NT, 512), BF16)
        knfm = AB3.take(C1, (T,), BF16)
        qlat = AB3.take(C1, (T,), BF16)
        vlat = AB3.take(C1, (T,), BF16)
        vtok = AB3.take(128, (NT, DV1), BF16)
        x_t = AB4.take(128, (D,), F32)
        y_t = AB4.take(128, (D,), F32)
        scr1 = AB4.take(128, (D,), F32)
        scr2 = AB4.take(128, (D,), F32)
        xn_bf = AB4.take(128, (D,), BF16)
        wqt_b = AB4.take(128, (8, C1), BF16)
        wat_b = AB4.take(128, (8, 512), BF16)
        wvt_b = AB4.take(128, (8, C1), BF16)
        qc_b = AB4.take(C1, (H1 * C1,), BF16)
        vc_b = AB4.take(C1, (H1 * DV1,), BF16)

        evw = [LOAD('DP', wqt_b, dmaj('wqt'), waits=[ED.get('A_w_free')]),
               LOAD('DP', wvt_b, dmaj('wvt')),
               LOAD('DP', wat_b, dmaj('wat'), waits=[ED.get('A_ab2_free')]),
               LOAD('DP', qc_b, dr['qc']),
               LOAD('DP', vc_b, dr['vc']),
               LOAD('DS', gamv, dr['gam_bc'], waits=[ED.get('A_ab2_free')]),
               LOAD('DS', betv, dr['bet_bc']),
               LOAD('DS', wabbc_v, dr['wab_bc'])]
        wk_rep = dmaj('wkt')
        if 'wk' not in ABL:
            for rep in range(8):
                evw.append(LOAD('DP', wkt_b[:, :, rep * 64:(rep + 1) * 64], wk_rep))
        for e in evw:
            P.wait('T', e); P.wait('V', e); P.wait('S', e)
        if 'd_gamv' in debug and b == 0:
            STORE('DS', dr['d_gamv'], gamv, waits=[evw[5]])

        for i in range(min(NT, NTILE_DEV)):
            rows = slice(i * 128, (i + 1) * 128)
            evx = LOAD('DS', x_t, dr['x'][b, rows, :], waits=[ED.get('A_xt_done')])
            P.wait('V', evx); P.wait('S', evx)
            if 'ln' in ABL:
                eyn = V(lambda v: v.tensor_copy(y_t, x_t))
            else:
                P.wait('S', ED.get('A_psB_done'))
                mu, rs = ln_stats(x_t, 0, scr1, src_ev=evx)
                eyn = ln_apply(y_t, x_t, mu, rs, fence=True)
            VPAD(3)
            exy = Vf(lambda v: v.tensor_add(scr2, x_t, y_t),
                     waits=[ED.get('A_xy_stored')])
            if 'st' in ABL:
                ev1 = exy
            else:
                ev1 = STORE('DS', dr['s_xy'][b, rows, :], scr2, waits=[exy])
                if 'd_xy' in debug:
                    ev1 = STORE('DS', dr['d_xy'][b, rows, :], scr2)
            ED['A_xy_stored'] = ev1
            if 'ln' in ABL:
                V(lambda v: v.tensor_copy(x_t, y_t))
            else:
                mu2, rs2 = ln_stats(y_t, 8, scr1, src_ev=eyn)
                ln_apply(x_t, y_t, mu2, rs2)
            VPAD(3)
            ext = V(lambda v: v.tensor_mul(scr1, x_t, gamv))
            ED['A_xt_done'] = ext
            VPAD(3)
            exn = Vf(lambda v: v.tensor_add(xn_bf, scr1, betv))
            if 'd_xn' in debug:
                STORE('DS', dr['d_xn'][b, rows, :], xn_bf, waits=[exn])
            evcp = transpose8(xn_bf, xnfm[:, :, i * 128:(i + 1) * 128], waits=[exn])
            pa = ps['A'].ap()
            pk = ps['B'].ap()
            pr = ps['C'].ap()
            evmm = mm_group(pa, [
                (xnfm[:, dc, rows], wat_b[:, dc, :]) for dc in range(8)],
                waits=[evcp, ED.get('A_psA_done')])
            ea = Vf(lambda v, pa=pa: v.tensor_add(scr1[:, 0:512], pa, wabbc_v),
                    waits=[evmm, ED.get('lnscr_s')])
            ED['A_psA_done'] = ea
            es = Sf(lambda s, f=(AF.Copy if 'sig' in ABL else AF.Sigmoid):
                    s.activation(out=scr1[:, 512:1024], in_=scr1[:, 0:512],
                                 func=f), waits=[ea])
            ec = Vf(lambda v: v.tensor_scalar_max(scr1[:, 512:1024],
                                                  scr1[:, 512:1024], 1e-38),
                    waits=[es])
            el = Sf(lambda s, f=(AF.Copy if 'sig' in ABL else AF.Ln):
                    s.activation(out=xn_bf[:, 0:512],
                                 in_=scr1[:, 512:1024], func=f),
                    waits=[ec])
            import os as _os3
            if _os3.environ.get('NOREV'):
                evr = None
            else:
                evr = mm_group(pr, [(ltm.ap(), xn_bf[:, 0:512])],
                               waits=[el, ED.get('A_psB_done'),
                                      ED.get('A_psC_done')])
            evkl = mm_group(pk, [
                (xnfm[:, dc, rows],
                 (wat_b if 'wk' in ABL else wkt_b)[:, dc, :]) for dc in range(8)],
                waits=[ED.get('A_psB_done')])
            Vf(lambda v, pk=pk: v.tensor_copy(klatf.ap()[:, 0:512], pk),
               waits=[evkl])
            ecopy_ev = ('V', P.count['V'])
            if 'sqred' not in ABL:
                esq0 = ecopy_ev
                e1 = Sf(lambda s: s.activation(out=scr1[:, 512:576],
                                               in_=klatf.ap()[:, 0:64],
                                               func=AF.Square,
                                               accum_out=rncol.ap()[:, 32:33]),
                        waits=[esq0, ED.get('A_ksq_red')])
                SPAD(4)
            else:
                e1 = ecopy_ev
            if 'nrm' not in ABL:
                S_(lambda s: s.activation(out=rncol.ap()[:, 32:33],
                                          in_=rncol.ap()[:, 32:33], func=AF.Ln),
                   waits=[e1, ED.get('A_rn_used')])
                e2 = Sf(lambda s: s.activation(out=rncol.ap()[:, 64:65],
                                               in_=rncol.ap()[:, 32:33],
                                               func=AF.Exp, scale=0.5))
                V(lambda v: v.tensor_scalar_max(rncol.ap()[:, 64:65],
                                                rncol.ap()[:, 64:65], 1e-12),
                  waits=[e2])
            if 'tail' in ABL:
                ED['A_psB_done'] = ecopy_ev
                ED['A_psC_done'] = None
                continue
            VPAD(3)
            ED['A_rn_used'] = V(lambda v: v.reciprocal(rncol.ap()[:, 0:1],
                                                       rncol.ap()[:, 64:65]))
            VPAD(4)
            V(lambda v, i=i: v.tensor_scalar(kntok.ap()[:, i * 64:(i + 1) * 64],
                                             klatf.ap()[:, 0:64],
                                             rncol.ap()[:, 0:1], None, ALU.mult))
            if 'd_lnat' in debug and b == 0 and i == 8:
                P.barrier()
                STORE('DS', dr['d_apre'], scr1[:, 0:512])
                evt_ = STORE('DS', dr['d_lnat'], xn_bf[:, 0:512])
                for e_ in ('T', 'V', 'S'):
                    P.wait(e_, evt_)
            eks = Sf(lambda s, pr=pr: s.activation(out=scr1[:, 0:512], in_=pr,
                                                   func=AF.Exp), waits=[evr])
            ED['A_psC_done'] = eks
            if 'd_eks' in debug and b == 0 and i == 8:
                P.barrier()
                evt_ = STORE('DS', dr['d_eks'], scr1[:, 0:512])
                for e_ in ('T', 'V', 'S'):
                    P.wait(e_, evt_)
            ekh = Vf(lambda v, i=i, pk=pk: v.scalar_tensor_tensor(
                khat[:, i, :], pk, rncol.ap()[:, 0:1], scr1[:, 0:512],
                ALU.mult, ALU.mult), waits=[eks])
            ED['A_psB_done'] = ekh

        if '2' not in sub:
            return
        AB2.reset()
        lnafm = AB2.take(128, (4, T), F32)
        LG = AB2.take(128, (4, T), F32)
        ED['A_ab2_free'] = None
        ED['A_kntok_done'] = ED.get('A_psB_done')
        for tg in range(4):
            cols = slice(tg * 512, (tg + 1) * 512)
            evq = mm_group(ps['A'].ap()[0:64, :], [
                (wqt_b[:, dc, :], xnfm[:, dc, cols]) for dc in range(8)],
                waits=[ED.get('A_psA_done')])
            eq = Vf(lambda v, cols=cols: v.tensor_copy(qlat[:, cols],
                                                       ps['A'].ap()[0:64, :]),
                    waits=[evq])
            ED['A_psA_done'] = eq
            evv = mm_group(ps['B'].ap()[0:64, :], [
                (wvt_b[:, dc, :], xnfm[:, dc, cols]) for dc in range(8)],
                waits=[ED.get('A_psB_done')])
            ev2 = Vf(lambda v, cols=cols: v.tensor_copy(vlat[:, cols],
                                                        ps['B'].ap()[0:64, :]),
                     waits=[evv])
            ED['A_psB_done'] = ev2
        for jc in range(4):
            for tg in range(4):
                cols = slice(tg * 512, (tg + 1) * 512)
                pp, key = (ps['A'], 'A_psA_done') if tg % 2 == 0 else (ps['B'], 'A_psB_done')
                ev = mm_group(pp.ap(), [
                    (wat_b[:, dc, jc * 128:(jc + 1) * 128], xnfm[:, dc, cols])
                    for dc in range(8)], waits=[ED.get(key)])
                es = S_(lambda s, pp=pp, jc=jc: s.activation(
                    out=scr1[:, 0:512], in_=pp.ap(), func=AF.Sigmoid,
                    bias=wabp.ap()[:, jc:jc + 1]),
                    waits=[ev, ED.get('A_scr1_free')])
                ED[key] = es
                ec = Vf(lambda v: v.tensor_scalar_max(scr1[:, 0:512],
                                                      scr1[:, 0:512], 1e-38),
                        waits=[es])
                el = Sf(lambda s, jc=jc, cols=cols: s.activation(
                    out=lnafm[:, jc, cols], in_=scr1[:, 0:512], func=AF.Ln),
                    waits=[ec])
                ED['A_scr1_free'] = el
        for half in range(2):
            P.wait('T', ED.get('psT_free'))
            P.wait('T', ED.get('A_kntok_done'))
            for j in range(8):
                i = half * 8 + j
                P.op('T', (lambda h, i=i, j=j: h.transpose(
                    ps['T'].ap()[0:64, j * 128:(j + 1) * 128],
                    kntok.ap()[:, i * 64:(i + 1) * 64], ident.ap())))
            evt = P.op('T', (lambda h: h.drain()), inc=True)
            evc = Vf(lambda v, half=half: v.tensor_copy(
                knfm[:, half * 1024:(half + 1) * 1024].rearrange(
                    'p (a n) -> p a n', a=8),
                ps['T'].ap().rearrange('p (a n) -> p a n', a=8)[0:64, :, :]),
                waits=[evt])
            ED['psT_free'] = evc
        P.wait('V', ED.get('A_scr1_free'))
        for h in range(H1):
            jc, half = h // 2, h % 2
            r0 = 64 * half
            zb = zcol.ap()[r0:r0 + 64, :].broadcast_to([C1, T])
            P.op('V', (lambda v, jc=jc, r0=r0, zb=zb: v.tensor_tensor_scan(
                LG[r0:r0 + 64, jc, :], lnafm[r0:r0 + 64, jc, :], zb,
                0.0, ALU.add, ALU.add)))
        VPAD(4)
        escan = P.op('V', (lambda v: v.memset(zcol.ap()[:, 0:1], 0.0)), inc=True)
        ED['A_scan'] = escan
        if 'd_xnfm' in debug:
            P.barrier()
            STORE('DS', dr['d_xnfm'][b], xnfm)
        if 'd_ident' in debug and b == 0:
            P.barrier()
            STORE('DS', dr['d_ident'], ident.ap())
            STORE('DS', dr['d_utm'], utm.ap())

        if 'd_lg' in debug:
            P.barrier()
            STORE('DS', dr['d_lg'][b], LG)
        if 'd_kn' in debug:
            P.barrier()
            STORE('DS', dr['d_kn'][b], knfm)
        if 'd_khat' in debug:
            P.barrier()
            STORE('DS', dr['d_khat'][b], khat)
        if 'd_qlat' in debug:
            P.barrier()
            STORE('DS', dr['d_qlat'][b], qlat)
        LG4 = LG.rearrange('p a (c l) -> p a c l', l=128)
        g4 = gamch.ap().rearrange('p (a c) -> p a c', a=4)
        m4 = midsch.ap().rearrange('p (a c) -> p a c', a=4)
        nm4 = negmid.ap().rearrange('p (a c) -> p a c', a=4)
        ns4 = negst.ap().rearrange('p (a c) -> p a c', a=4)
        P.barrier(engines=('T', 'V', 'S'))
        for jc in range(4):
            LG3 = LG[:, jc, :].rearrange('p (c l) -> p c l', l=128)
            V(lambda v, jc=jc, LG3=LG3: v.tensor_copy(g4[:, jc, :], LG3[:, :, 127]),
              waits=[escan] if jc == 0 else ())
            V(lambda v, jc=jc, LG3=LG3: v.tensor_copy(m4[:, jc, :], LG3[:, :, 63]))
            V(lambda v, jc=jc: v.memset(ns4[:, jc, 0:1], 0.0))
            V(lambda v, jc=jc: v.tensor_copy(ns4[:, jc, 1:16], g4[:, jc, 0:15]))
        if 'd_ends' in debug and b == 0:
            P.barrier()
            evt_ = STORE('DS', dr['d_ends'], gamch.ap())
            for e_ in ('T', 'V', 'S'):
                P.wait(e_, evt_)
        VPAD(3)
        V(lambda v: v.tensor_scalar(negmid.ap(), midsch.ap(), -1.0, None, ALU.mult))
        ecp = V(lambda v: v.tensor_scalar(negst.ap(), negst.ap(), -1.0, None,
                                          ALU.mult))
        VPAD(3)
        if 'd_nst' in debug and b == 0:
            P.barrier()
            evt_ = STORE('DS', dr['d_nst'], negst.ap())
            for e_ in ('T', 'V', 'S'):
                P.wait(e_, evt_)
        P.barrier(engines=('T', 'V', 'S'))
        ega = Vf(lambda v: v.tensor_add(gamch.ap(), gamch.ap(), negst.ap()))
        egb = S_(lambda s: s.activation(out=gamch.ap(), in_=gamch.ap(), func=AF.Exp),
                 waits=[ega])
        P.barrier(engines=('T', 'V', 'S'))
        ez1 = None
        ez2 = None
        ED['A_refs'] = egb

        if '3' not in sub:
            return
        AB4.reset()
        qmid_h = AB4.take(C1, (T,), BF16)
        qst_h = AB4.take(C1, (T,), BF16)
        kmid_h = AB4.take(C1, (T,), BF16)
        AB2.reset()
        qmid_sc = AB2.take(128, (T,), F32)
        ksc_h = AB2.take(C1, (T,), F32)
        qst_sc = AB2.take(128, (T,), F32)
        Sfv = AB2.take(C1, (H1 * DV1,), F32)
        Sbv = AB2.take(C1, (H1 * DV1,), BF16)
        assert AB2.off <= 32768
        ez1 = V(lambda v: v.memset(Sfv, 0.0), waits=[ED.get('A_state_add')])
        ez2 = V(lambda v: v.memset(Sbv, 0.0))

        for jc in range(4):
            ev_sc = None
            for w in (ED.get('A_scan'), ED.get('A_refs'), ED.get('A_scales_used')):
                P.wait('S', w)
            SPAD(4)
            for ci in range(16):
                ccols = slice(ci * 128, (ci + 1) * 128)
                S_(lambda s, jc=jc, ci=ci, ccols=ccols: s.activation(
                    out=qmid_sc[:, ccols], in_=LG[:, jc, ccols], func=AF.Exp,
                    bias=nm4[:, jc, ci:ci + 1]))
                ev_sc = S_(lambda s, jc=jc, ci=ci, ccols=ccols: s.activation(
                    out=qst_sc[:, ccols], in_=LG[:, jc, ccols], func=AF.Exp,
                    bias=ns4[:, jc, ci:ci + 1]))
            if 'd_qmsc' in debug and b == 0 and jc == 0:
                P.barrier()
                evt_ = STORE('DS', dr['d_qmsc'], qmid_sc)
                if 'd_mids' in debug:
                    STORE('DS', dr['d_mids'], midsch.ap())
                    evt_ = STORE('DS', dr['d_negm'], negmid.ap())
                for e_ in ('T', 'V', 'S'):
                    P.wait(e_, evt_)
            for half in range(2):
                h = jc * 2 + half
                r0 = 64 * half
                evgh = Vf(lambda v, r0=r0, jc=jc: v.tensor_copy(
                    gh.ap(), g4[r0:r0 + 64, jc, :]),
                    waits=[egb, ED.get('A_gh_used')])
                if 'd_gh' in debug and b == 0 and h == 0:
                    P.barrier()
                    evt_ = STORE('DS', dr['d_gh'], gh.ap())
                    for e_ in ('T', 'V', 'S'):
                        P.wait(e_, evt_)
                evksc = None
                for ci in range(16):
                    ccols2 = slice(ci * 128, (ci + 1) * 128)
                    SX = Sf if ci == 15 else S_
                    evksc = SX(lambda s, jc=jc, ci=ci, ccols2=ccols2, r0=r0:
                               s.activation(
                        out=ksc_h[:, ccols2], in_=LG[r0:r0 + 64, jc, ccols2],
                        func=AF.Exp, scale=-1.0,
                        bias=m4[r0:r0 + 64, jc, ci:ci + 1]),
                        waits=[ED.get('A_ksc_used')] if ci == 0 else ())
                evkm = Vf(lambda v: v.tensor_mul(kmid_h, knfm, ksc_h),
                    waits=[evksc, ED.get('A_kmid_used')])
                ED['A_ksc_used'] = evkm
                for half2 in range(2):
                    P.wait('T', ED.get('psG2_free'))
                    for j in range(8):
                        i2 = half2 * 8 + j
                        P.op('T', (lambda h_, i2=i2, j=j, hh=h: h_.matmul(
                            ps['G2'].ap()[:, j * 128:(j + 1) * 128],
                            vlat[:, i2 * 128:(i2 + 1) * 128],
                            vc_b[:, hh * 128:(hh + 1) * 128],
                            start=True, stop=True)))
                    evm = P.op('T', (lambda h_: h_.drain()), inc=True)
                    evc = Vf(lambda v, half2=half2: v.tensor_copy(
                        vtok[:, half2 * 8:(half2 + 1) * 8, :],
                        ps['G2'].ap().rearrange('p (a n) -> p a n', a=8)),
                        waits=[evm, ED.get('A_vtok_used')])
                    ED['psG2_free'] = evc
                ev_vtok = ED['psG2_free']
                evq2 = None
                for tg in range(4):
                    cols = slice(tg * 512, (tg + 1) * 512)
                    pp, key = (ps['A'], 'A_psA_done') if tg % 2 == 0 else (ps['B'], 'A_psB_done')
                    evq = mm_group(pp.ap()[0:64, :], [
                        (qc_b[:, h * 64:(h + 1) * 64], qlat[:, cols])],
                        waits=[ED.get(key)])
                    V(lambda v, pp=pp, cols=cols, r0=r0: v.scalar_tensor_tensor(
                        qmid_h[:, cols], pp.ap()[0:64, :], 1.0,
                        qmid_sc[r0:r0 + 64, cols], ALU.mult, ALU.mult),
                      waits=[evq, ED.get('A_qmid_used')])
                    evq2 = Vf(lambda v, pp=pp, cols=cols, r0=r0: v.scalar_tensor_tensor(
                        qst_h[:, cols], pp.ap()[0:64, :], 1.0,
                        qst_sc[r0:r0 + 64, cols], ALU.mult, ALU.mult),
                        waits=[ED.get('A_qst_used')])
                    ED[key] = evq2
                ED['A_scales_used'] = evq2
                if 'd_qmh' in debug and b == 0 and h == 0:
                    P.barrier()
                    STORE('DS', dr['d_qmh'], qmid_h)
                    STORE('DS', dr['d_kmh'], kmid_h)
                    evt_ = STORE('DS', dr['d_vth'],
                                 vtok.rearrange('p a n -> p (a n)'))
                    for e_ in ('T', 'V', 'S'):
                        P.wait(e_, evt_)
                for c in range(NT):
                    ccols = slice(c * 128, (c + 1) * 128)
                    pPt, key = (ps['A'], 'A_psA_done') if c % 2 == 0 else (ps['B'], 'A_psB_done')
                    evP = mm_group(pPt.ap()[:, 0:128], [
                        (kmid_h[:, ccols], qmid_h[:, ccols])],
                        waits=[ED.get(key)])
                    evPm = Vf(lambda v, pPt=pPt, c=c: v.tensor_mul(
                        ptile.ap()[:, (c % 2) * 128:(c % 2) * 128 + 128],
                        pPt.ap()[:, 0:128], utm.ap()),
                        waits=[evP, ED.get(f'A_pt{c % 2}_used')])
                    ED[key] = evPm
                    if 'd_pt' in debug and b == 0 and h == 0 and c == 0:
                        P.barrier()
                        evt_ = STORE('DS', dr['d_pt'], ptile.ap()[:, 0:128])
                        for e_ in ('T', 'V', 'S'):
                            P.wait(e_, evt_)
                    evDl = mm_group(ps['C'].ap()[0:64, 0:128], [
                        (khat[:, c, h * 64:(h + 1) * 64], vtok[:, c, :])],
                        waits=[ED.get('A_state_add'), ev_vtok, ED.get('A_psC_done')])
                    ED['A_psC_done'] = None
                    evO = mm_group(ps['G1'].ap()[:, 0:128], [
                        (vtok[:, c, :], ptile.ap()[:, (c % 2) * 128:(c % 2) * 128 + 128]),
                        (Sbv[:, h * 128:(h + 1) * 128], qst_h[:, ccols]),
                    ], waits=[evPm, ev_vtok, ED.get('A_state_copy'),
                              ED.get('A_o_evac'), ez2 if (h == 0 and c == 0) else None])
                    ED[f'A_pt{c % 2}_used'] = evO
                    evoe = Vf(lambda v, h=h, ccols=ccols: v.tensor_copy(
                        ofm[:, h, ccols], ps['G1'].ap()[:, 0:128]), waits=[evO])
                    ED['A_o_evac'] = evoe
                    evsa = V(lambda v, h=h, c=c: v.scalar_tensor_tensor(
                        Sfv[:, h * 128:(h + 1) * 128],
                        Sfv[:, h * 128:(h + 1) * 128],
                        gh.ap()[:, c:c + 1],
                        ps['C'].ap()[0:64, 0:128], ALU.mult, ALU.add),
                      waits=[evgh, evO, evDl, ez1 if (h == 0 and c == 0) else None])
                    ED['A_state_add'] = evsa
                    VPAD(3)
                    evsc = Vf(lambda v, h=h: v.tensor_copy(
                        Sbv[:, h * 128:(h + 1) * 128],
                        Sfv[:, h * 128:(h + 1) * 128]))
                    ED['A_state_copy'] = evsc
                    if 'd_sf' in debug and b == 0 and h == 0 and c == 0:
                        P.barrier()
                        evt_ = STORE('DS', dr['d_sf'], Sfv[:, 0:128])
                        for e_ in ('T', 'V', 'S'):
                            P.wait(e_, evt_)
                ED['A_gh_used'] = ED['A_state_add']
                ED['A_kmid_used'] = ED[f'A_pt{(NT - 1) % 2}_used']
                ED['A_qmid_used'] = ED['A_kmid_used']
                ED['A_qst_used'] = ED['A_o_evac']
                ED['A_vtok_used'] = ED['A_state_add']
        if 'd_ofm' in debug:
            P.barrier()
            STORE('DS', dr['d_ofm'][b].rearrange('(a p) t -> p a t', p=128), ofm)
        P.barrier()

        if '4' not in sub:
            return
        AB2.reset()
        wot_b = AB2.take(128, (8, D), BF16)
        g1v = AB2.take(128, (D,), F32)
        b1lv = AB2.take(128, (D,), F32)
        ev1 = LOAD('DP', wot_b, dmaj('wot'))
        ev2 = LOAD('DS', g1v, dr['g1_bc'])
        ev3 = LOAD('DS', b1lv, dr['b1l_bc'])
        for e in (ev1, ev2, ev3):
            P.wait('T', e); P.wait('V', e); P.wait('S', e)
        hnfm = xnfm
        for i in range(NT):
            rows = slice(i * 128, (i + 1) * 128)
            evxy = LOAD('DS', x_t, dr['s_xy'][b, rows, :],
                        waits=[ED.get('A4_xt_done')])
            pg, key = (ps['G1'], 'A_o_evac') if True else (None, None)
            pg = ps['G2'] if i % 2 == 0 else ps['G1']
            key = f'A4_pg{i % 2}'
            evo = None
            for gc in range(2):
                evo = mm_group(pg.ap()[:, gc * 512:(gc + 1) * 512], [
                    (ofm[:, hh, rows], wot_b[:, hh, gc * 512:(gc + 1) * 512])
                    for hh in range(8)],
                    waits=[ED.get(key), ED.get('A_o_evac') if i == 0 else None,
                           ED.get('psG2_free') if i == 0 else None])
            ex1 = Vf(lambda v, pg=pg: v.tensor_add(y_t, x_t, pg.ap()),
                     waits=[evo, evxy, ED.get('A4_yt_stored')])
            ED[key] = ex1
            ev1 = STORE('DS', dr['s_x1'][b, rows, :], y_t, waits=[ex1])
            if 'd_x1' in debug:
                ev1 = STORE('DS', dr['d_x1'][b, rows, :], y_t)
            ED['A4_yt_stored'] = ev1
            P.wait('S', ED.get('A4_ehn'))
            mu, rs = ln_stats(y_t, 0, scr1, src_ev=ex1)
            ez = ln_apply(x_t, y_t, mu, rs,
                          waits=[ED.get('A4_zt_stored')], fence=True)
            ev2 = STORE('DS', dr['s_z'][b, rows, :], x_t, waits=[ez])
            if 'd_z' in debug:
                ev2 = STORE('DS', dr['d_z'][b, rows, :], x_t)
            ED['A4_zt_stored'] = ev2
            ED['A4_xt_done'] = ev2
            mu2, rs2 = ln_stats(x_t, 8, scr1, src_ev=ez)
            ln_apply(scr2, x_t, mu2, rs2)
            VPAD(3)
            V(lambda v: v.tensor_mul(scr1, scr2, g1v),
              waits=[ED.get('lnscr_s')])
            VPAD(3)
            ehn = Vf(lambda v: v.tensor_add(xn_bf, scr1, b1lv))
            ED['A4_ehn'] = ehn
            transpose8(xn_bf, hnfm[:, :, i * 128:(i + 1) * 128], waits=[ehn])
        P.barrier()

    def phaseB(b):
        AB2.reset()
        fwq_b = AB2.take(128, (8, D), BF16)
        fwk_b = AB2.take(128, (8, D), BF16)
        oatt = AB2.take(128, (8, T), BF16)
        AB3.reset()
        vpr = AB3.take(128, (16, 16, 65), BF16)
        hnfm = AB1.h.ap().bitcast(BF16)[:, 0:8 * T].rearrange('p (a n) -> p a n', a=8)
        pstr = AB1.h.ap().bitcast(BF16)[:, 8 * T:8 * T + 16 * 1024].rearrange(
            'p (a n) -> p a n', a=16)
        AB4.reset()
        qh0 = AB4.take(C1, (T,), BF16)
        qh1 = AB4.take(C1, (T,), BF16)
        kh0 = AB4.take(C1, (T,), BF16)
        kh1 = AB4.take(C1, (T,), BF16)
        rbc_sb = AB4.take(C1, (512,), F32)
        scrB = AB4.take(128, (D,), F32)
        rrow_v = AB4.take(1, (512,), F32)

        evfv = LOAD('DP', fwq_b, dmaj('fwv'))
        evst = V(lambda v: v.memset(vpr[:, :, :, 64:65], 1.0))
        evbv = LOAD('DS', scrB, dr['bv_bc'])
        P.wait('T', evfv)
        for i in range(NT):
            for vh in range(2):
                pp, key = ((ps['A'], 'B_psA_done') if (i * 2 + vh) % 2 == 0
                           else (ps['B'], 'B_psB_done'))
                ev = mm_group(pp.ap(), [
                    (hnfm[:, dc, i * 128:(i + 1) * 128],
                     fwq_b[:, dc, vh * 512:(vh + 1) * 512]) for dc in range(8)],
                    waits=[ED.get(key)])
                evv = Vf(lambda v, pp=pp, i=i, vh=vh: v.tensor_tensor(
                    vpr[:, i, vh * 8:(vh + 1) * 8, 0:64],
                    pp.ap().rearrange('p (a n) -> p a n', a=8),
                    scrB[:, vh * 512:(vh + 1) * 512].rearrange('p (a n) -> p a n', a=8),
                    ALU.add), waits=[ev, evbv, evst])
                ED[key] = evv
        P.barrier()
        evq = LOAD('DP', fwq_b, dmaj('fwq'))
        evk = LOAD('DP', fwk_b, dmaj('fwk'))
        P.wait('T', evq); P.wait('T', evk)

        for fc in range(8):
            for (wsb, bcol, t0, t1) in ((fwq_b, bqp, qh0, qh1),
                                        (fwk_b, bkp, kh0, kh1)):
                for tg in range(4):
                    cols = slice(tg * 512, (tg + 1) * 512)
                    pp, key = ((ps['A'], 'B_psA_done') if tg % 2 == 0
                               else (ps['B'], 'B_psB_done'))
                    ev = mm_group(pp.ap(), [
                        (wsb[:, dc, fc * 128:(fc + 1) * 128], hnfm[:, dc, cols])
                        for dc in range(8)], waits=[ED.get(key)])
                    V(lambda v, pp=pp, cols=cols, t0=t0, bcol=bcol, fc=fc:
                      v.tensor_scalar(t0[:, cols], pp.ap()[0:64, :],
                                      bcol.ap()[0:64, fc:fc + 1], None, ALU.add),
                      waits=[ev, ED.get('B_qk_used')])
                    ev2 = Vf(lambda v, pp=pp, cols=cols, t1=t1, bcol=bcol, fc=fc:
                             v.tensor_scalar(t1[:, cols], pp.ap()[64:128, :],
                                             bcol.ap()[64:128, fc:fc + 1], None,
                                             ALU.add))
                    ED[key] = ev2
            for half in range(2):
                h = fc * 2 + half
                qh = qh0 if half == 0 else qh1
                kh = kh0 if half == 0 else kh1
                V(lambda v, qh=qh: v.reduce_max(
                    qmxc.ap()[:, 0:1], qh, axis=AX.X, apply_absolute_value=True))
                V(lambda v, kh=kh: v.reduce_max(
                    qmxc.ap()[:, 1:2], kh, axis=AX.X, apply_absolute_value=True))
                VPAD(4)
                ep = Vf(lambda v: v.tensor_mul(qmxc.ap()[:, 2:3], qmxc.ap()[:, 0:1],
                                               qmxc.ap()[:, 1:2]))
                evu = mm_group(ps['C'].ap()[0:1, 0:1], [
                    (qmxc.ap()[:, 2:3], ones64b.ap())],
                    waits=[ep, ED.get('B_psC_done')])
                evn = Vf(lambda v, half=half: v.tensor_scalar(
                    nubc.ap()[:, half * 32:half * 32 + 1], ps['C'].ap()[0:1, 0:1],
                    -0.125, SHIFT, ALU.mult, ALU.add), waits=[evu])
                evb = mm_group(ps['C'].ap()[:, 128:129], [
                    (ones_rf.ap(), nubc.ap()[:, half * 32:half * 32 + 1])], waits=[evn])
                evbc = Vf(lambda v, half=half: v.tensor_copy(
                    biasc2.ap()[:, half * 32:half * 32 + 1], ps['C'].ap()[:, 128:129]),
                    waits=[evb, ED.get(f'B_bias{half}_used')])
                ED['B_psC_done'] = evbc
                P.wait('S', evbc)
                SPAD(4)
                for qc in range(2):
                    evex = None
                    for kc in range(16):
                        pg, key = ((ps['G1'], 'B_G1_done') if kc % 2 == 0
                                   else (ps['G2'], 'B_G2_done'))
                        P.wait('T', ED.get(key))
                        for hseg in range(2):
                            P.op('T', (lambda h_, pg=pg, kc=kc, hseg=hseg,
                                       qh=qh, kh=kh, qc=qc: h_.matmul(
                                pg.ap()[:, hseg * 512:(hseg + 1) * 512],
                                kh[:, kc * 128:(kc + 1) * 128],
                                qh[:, qc * 1024 + hseg * 512:
                                   qc * 1024 + (hseg + 1) * 512],
                                start=True, stop=True)))
                        evm = P.op('T', (lambda h_: h_.drain()), inc=True)
                        evex = Sf(lambda s, pg=pg, kc=kc, half=half: s.activation(
                            out=pstr[:, kc, :], in_=pg.ap(), func=AF.Exp,
                            scale=0.125, bias=biasc2.ap()[:, half * 32:half * 32 + 1]),
                            waits=[evm, ED.get(f'B_pstr{kc}_used')])
                        ED[key] = evex
                        ED[f'B_exp{kc % 2}'] = evex
                    ED[f'B_bias{half}_used'] = evex
                    for sg in range(2):
                        pp, key = ((ps['A'], 'B_psA_done') if sg % 2 == 0
                                   else (ps['B'], 'B_psB_done'))
                        ev = mm_group(pp.ap()[0:65, :], [
                            (vpr[:, kc2, h, :], pstr[:, kc2, sg * 512:(sg + 1) * 512])
                            for kc2 in range(16)],
                            waits=[ED.get(key), ED.get('B_exp0'), ED.get('B_exp1')])
                        for kc2 in range(16):
                            ED[f'B_pstr{kc2}_used'] = ev
                        evr = Vf(lambda v, pp=pp: v.reciprocal(
                            rrow_v, pp.ap()[64:65, :]),
                            waits=[ev, ED.get('B_rrow_used')])
                        evb2 = mm_group(ps['C'].ap()[0:64, 0:512], [
                            (ones_rf.ap()[:, 0:64], rrow_v)],
                            waits=[evr, ED.get('B_psC_done')])
                        ED['B_rrow_used'] = evb2
                        evrb = V(lambda v: v.tensor_copy(rbc_sb,
                                                         ps['C'].ap()[0:64, 0:512]),
                                 waits=[evb2, ED.get('B_rbc_used')])
                        ED['B_psC_done'] = evrb
                        VPAD(3)
                        r0 = 64 * (h % 2)
                        evoa = Vf(lambda v, pp=pp, h=h, qc=qc, sg=sg, r0=r0:
                                  v.scalar_tensor_tensor(
                                      oatt[r0:r0 + 64, h // 2,
                                           qc * 1024 + sg * 512:
                                           qc * 1024 + (sg + 1) * 512],
                                      pp.ap()[0:64, :], 1.0, rbc_sb,
                                      ALU.mult, ALU.mult))
                        ED['B_rbc_used'] = evoa
                        ED[key] = evoa
                ED['B_qk_used'] = ED.get('B_psA_done')
        P.barrier()
        if 'd_oatt' in debug:
            STORE('DS', dr['d_oatt'][b].rearrange('(a p) t -> p a t', p=128), oatt)
            P.barrier()

        AB2.off = 0
        fwo_b = AB2.take(128, (8, D), BF16)
        g2v = AB2.take(128, (D,), F32)
        b2lv = AB2.take(128, (D,), F32)
        bov = AB2.take(128, (D,), F32)
        ev1 = LOAD('DP', fwo_b, dmaj('fwo'))
        ev2 = LOAD('DS', g2v, dr['g2_bc'])
        ev3 = LOAD('DS', b2lv, dr['b2l_bc'])
        ev4 = LOAD('DS', bov, dr['bo_bc'])
        for e in (ev1, ev2, ev3, ev4):
            P.wait('T', e); P.wait('V', e); P.wait('S', e)
        AB4.off = 0
        z_t = AB4.take(128, (D,), F32)
        x1p = AB4.take(128, (D,), F32)
        x1_t = AB4.take(128, (D,), F32)
        scr1 = AB4.take(128, (D,), F32)
        scr2b = AB4.take(128, (D,), F32)
        hn2bf = AB4.take(128, (D,), BF16)
        hn2f_t = AB4.take(128, (8, 128), BF16)
        lgt = AB4.take(128, (NT, 8), F32)

        for i in range(NT):
            rows = slice(i * 128, (i + 1) * 128)
            evz = LOAD('DS', z_t, dr['s_z'][b, rows, :], waits=[ED.get('B4_zt_done')])
            evx1 = LOAD('DS', x1_t, dr['s_x1'][b, rows, :],
                        waits=[ED.get('B4_x1_done')])
            pg = ps['G1'] if i % 2 == 0 else ps['G2']
            key = f'B4_pg{i % 2}'
            evo = None
            for gc in range(2):
                evo = mm_group(pg.ap()[:, gc * 512:(gc + 1) * 512], [
                    (oatt[:, hh, rows], fwo_b[:, hh, gc * 512:(gc + 1) * 512])
                    for hh in range(8)], waits=[ED.get(key)])
            ezt = V(lambda v, pg=pg: v.tensor_add(x1p, z_t, pg.ap()),
                    waits=[evo, evz])
            ED['B4_zt_done'] = ezt
            ED[key] = ezt
            VPAD(3)
            ex1p = Vf(lambda v: v.tensor_add(x1p, x1p, bov))
            if 'd_x1p' in debug:
                STORE('DS', dr['d_x1p'][b, rows, :], x1p, waits=[ex1p])
            VPAD(2)
            eob = Vf(lambda v: v.tensor_add(scr2b, x1_t, x1p),
                     waits=[evx1, ED.get('B4_base_stored')])
            ED['B4_x1_done'] = eob
            evbs = STORE('DS', dr['out'][b, rows, :], scr2b, waits=[eob])
            ED['B4_base_stored'] = evbs
            P.wait('S', ED.get('B4_ehn'))
            mu, rs = ln_stats(x1p, 0, scr1, src_ev=ex1p)
            ln_apply(scr2b, x1p, mu, rs, waits=[evbs])
            VPAD(3)
            V(lambda v: v.tensor_mul(scr1, scr2b, g2v),
              waits=[ED.get('lnscr_s')])
            VPAD(3)
            ehn = Vf(lambda v: v.tensor_add(hn2bf, scr1, b2lv))
            ED['B4_ehn'] = ehn
            evcp = transpose8(hn2bf, hn2f_t, waits=[ehn])
            c0 = b * T + i * 128
            evh = STORE('DS', dr['s_hn2'][:, :, c0:c0 + 128].rearrange(
                'd p t -> p d t'), hn2f_t, waits=[evcp])
            if 'd_hn2' in debug:
                evh = STORE('DS', dr['d_hn2'][:, :, c0:c0 + 128].rearrange(
                    'd p t -> p d t'), hn2f_t)
            ED['B4_hn2_stored'] = evh
            P.wait('V', evh)
            evg = mm_group(ps['C'].ap()[:, 256:264], [
                (hn2f_t[:, dc, :],
                 gatewb.ap().rearrange('p (a n) -> p a n', a=8)[:, dc, :])
                for dc in range(8)], waits=[evcp, ED.get('B_psC_done')])
            evl = V(lambda v: v.scalar_tensor_tensor(
                lgt_st.ap()[:, 0:8], ps['C'].ap()[:, 256:264], 1.0, gatebc.ap(),
                ALU.mult, ALU.add), waits=[evg])
            ED['B_psC_done'] = evl
            VPAD(3)
            V(lambda v, i=i: v.tensor_copy(lgt[:, i, :], lgt_st.ap()[:, 0:8]))
        P.barrier()
        for i in range(NT):
            evm = V(lambda v, i=i: v.reduce_max(tmpnt.ap()[:, i:i + 1],
                                                lgt[:, i, :], axis=AX.X))
            VPAD(4)
            V(lambda v, i=i: v.tensor_tensor(
                equ8.ap()[:, i * 8:(i + 1) * 8], lgt[:, i, :],
                tmpnt.ap()[:, i:i + 1].broadcast_to([128, 8]), ALU.is_equal),
              waits=[evm])
        mvw = masks.ap().rearrange('p (b2 a n) -> p b2 a n', b2=BL, a=NT)
        V(lambda v: v.memset(keepb.ap(), 1.0))
        VPAD(3)
        eqv = V(lambda v, b=b: v.select(
            masks.ap()[:, b * 128:(b + 1) * 128], equ8.ap(), onessm.ap(),
            zerossm.ap()))
        VPAD(3)
        m3 = masks.ap()[:, b * 128:(b + 1) * 128].rearrange('p (a n) -> p a n', a=NT)
        for e_ in range(8):
            if e_ > 0:
                V(lambda v, e_=e_: v.tensor_mul(m3[:, :, e_], m3[:, :, e_],
                                                keepb.ap()), waits=[eqv])
                VPAD(3)
            if e_ < 7:
                V(lambda v, e_=e_: v.tensor_scalar(
                    tmpnt.ap(), m3[:, :, e_], -1.0, 1.0, ALU.mult, ALU.add),
                  waits=[eqv])
                VPAD(3)
                V(lambda v: v.tensor_mul(keepb.ap(), keepb.ap(), tmpnt.ap()))
                VPAD(3)
        if 'd_mask' in debug:
            P.barrier()
            STORE('DS', dr['d_mask'][:, b, :, :], m3)
        for i in range(NT):
            rows = slice(i * 128, (i + 1) * 128)
            evmb = Vf(lambda v, i=i: v.tensor_copy(maskbf.ap(), m3[:, i, :]),
                      waits=[ED.get('B5_maskbf_used')])
            P.wait('T', ED.get('psT_free'))
            P.wait('T', evmb)
            P.op('T', (lambda h: h.transpose(ps['T'].ap()[0:8, 0:128],
                                             maskbf.ap(), ident.ap())))
            evt = P.op('T', (lambda h: h.drain()), inc=True)
            ED['B5_maskbf_used'] = evt
            evo = Vf(lambda v: v.tensor_copy(ohT.ap(), ps['T'].ap()[0:8, 0:128]),
                     waits=[evt])
            ED['psT_free'] = evo
            evs = None
            for gc in range(2):
                evs = mm_group(ps['G1'].ap()[:, gc * 512:(gc + 1) * 512], [
                    (ohT.ap(), b2tb.ap()[:, gc * 512:(gc + 1) * 512])],
                    waits=[evo, ED.get('B5_G1_done') if gc == 0 else None])
            evc = Vf(lambda v: v.tensor_copy(scr2b, ps['G1'].ap()),
                     waits=[evs, ED.get('B5_scr_used')])
            ED['B5_G1_done'] = evc
            eva = ACCUM('DP', dr['out'][b, rows, :], scr2b,
                        waits=[evc, ED.get('B4_base_stored')])
            ED['B5_scr_used'] = eva
        P.barrier()

    def phaseC(nseq=BL, nexp=E):
        AB1.reset(); AB2.reset(); AB3.reset(); AB4.reset()
        w1b = AB1.take(128, (8, FF), BF16)
        w2b = AB2.take(128, (32, D), BF16)
        midb = AB3.take(128, (32, 512), BF16)
        h2a = AB4.take(128, (8, 512), BF16)
        h2b = AB4.take(128, (8, 512), BF16)
        y_t = AB4.take(128, (D,), F32)
        y_t2 = AB4.take(128, (D,), F32)
        prev_accum_count = P.count['DP']
        base_stores_done = ('DS', P.count['DS'])
        for e_ in range(nexp):
            evw1 = LOAD('DP', w1b, dr['w1'][e_].rearrange('(a p) n -> p a n', p=128),
                        waits=[ED.get('C_w_used'), ED.get('C_psB_done'),
                               ED.get('C_psC_done')])
            evw2 = LOAD('DP', w2b, dr['w2'][e_].rearrange('(a p) n -> p a n', p=128),
                        waits=[ED.get('C_pg0'), ED.get('C_pg1')])
            P.wait('T', evw1)
            this_expert_first_accum = True
            for tc in range(4 * nseq):
                h2 = h2a if tc % 2 == 0 else h2b
                c0 = tc * 512
                evh = LOAD('DS', h2, dr['s_hn2'][:, :, c0:c0 + 512].rearrange(
                    'd p t -> p d t'), waits=[ED.get(f'C_h2{tc % 2}_used')])
                P.wait('T', evh)
                for fc in range(32):
                    pp, key = ((ps['A'], 'C_psA_done'), (ps['B'], 'C_psB_done'),
                               (ps['C'], 'C_psC_done'))[fc % 3]
                    ev = mm_group(pp.ap(), [
                        (w1b[:, dc, fc * 128:(fc + 1) * 128], h2[:, dc, :])
                        for dc in range(8)], waits=[ED.get(key)])
                    bcol = b1cc.ap()[:, e_ * 32 + fc: e_ * 32 + fc + 1]
                    if fc % 2 == 0:
                        eve = Vf(lambda v, pp=pp, fc=fc, bcol=bcol: v.tensor_scalar(
                            midb[:, fc, :], pp.ap(), bcol, 0.0, ALU.add, ALU.max),
                            waits=[ev])
                    else:
                        eve = Sf(lambda s, pp=pp, fc=fc, bcol=bcol: s.activation(
                            out=midb[:, fc, :], in_=pp.ap(), func=AF.Relu,
                            bias=bcol), waits=[ev])
                    ED[key] = eve
                ED[f'C_h2{tc % 2}_used'] = ED['C_psA_done']
                for k_ in ('C_psA_done', 'C_psB_done', 'C_psC_done'):
                    P.wait('T', ED.get(k_))
                if tc == 0:
                    P.wait('T', evw2)
                for tt in range(4):
                    g = tc * 4 + tt
                    bq, iq = g // NT, g % NT
                    yt = y_t if tt % 2 == 0 else y_t2
                    pgt = ps['G1'] if tt % 2 == 0 else ps['G2']
                    key = f'C_pg{tt % 2}'
                    evy = None
                    for gc in range(2):
                        evy = mm_group(pgt.ap()[:, gc * 512:(gc + 1) * 512], [
                            (midb[:, fc, tt * 128:(tt + 1) * 128],
                             w2b[:, fc, gc * 512:(gc + 1) * 512])
                            for fc in range(32)],
                            waits=[ED.get(key)] if gc == 0 else [])
                    mcol = masks.ap().rearrange('p (b2 a n) -> p b2 a n',
                                                b2=BL, a=NT)[:, bq, iq, e_:e_ + 1]
                    evm = Vf(lambda v, pgt=pgt, yt=yt, mcol=mcol: v.tensor_scalar(
                        yt, pgt.ap(), mcol, None, ALU.mult),
                        waits=[evy, ED.get(f'C_yt{tt % 2}_accumed')])
                    ED[key] = evm
                    waits = [evm]
                    if this_expert_first_accum:
                        waits.append(('DP', prev_accum_count))
                        waits.append(base_stores_done)
                        this_expert_first_accum = False
                    eva = ACCUM('DP', dr['out'][bq, iq * 128:(iq + 1) * 128, :],
                                yt, waits=waits)
                    ED[f'C_yt{tt % 2}_accumed'] = eva
            ED['C_w_used'] = ED['C_psA_done']
            prev_accum_count = P.count['DP']
        P.wait('DP', ('DP', P.count['DP']))
        P.wait('DS', ('DS', P.count['DS']))
        P.barrier()

    for b in range(nseq):
        if 'A' in phases:
            sub = ''.join(c for c in phases if c.isdigit()) or '1234'
            phaseA(b, sub)
        if 'B' in phases:
            phaseB(b)
    P.barrier()
    if 'C' in phases:
        phaseC(nseq, nexp)
    P.wait('DP', ('DP', P.count['DP']))
    P.wait('DS', ('DS', P.count['DS']))

    P.emit_into(block, sems)
    st.close()
    return nc


def kernel(**inputs):
    lay = host_layout(inputs)
    x = np.ascontiguousarray(np.asarray(inputs['L0_attn_x'], np.float32))
    nc = build()
    in_maps = []
    for i in range(NCORES):
        m = {'x': np.ascontiguousarray(x[i * BL:(i + 1) * BL])}
        m.update(lay)
        in_maps.append(m)
    res = run_bass_kernel_spmd(nc, in_maps, core_ids=list(range(NCORES)))
    return np.concatenate([r['out'] for r in res.results], axis=0)


# version 04 (final)
# speedup vs baseline: 1.0435x; 1.0099x over previous
import numpy as np
import ml_dtypes
import concourse.bass as bass
import concourse.mybir as mybir
from concourse.bass_utils import run_bass_kernel_spmd
from contextlib import ExitStack

F32 = mybir.dt.float32
BF16 = mybir.dt.bfloat16
U8 = mybir.dt.uint8
AF = mybir.ActivationFunctionType
ALU = mybir.AluOpType
AX = mybir.AxisListType
bfloat16 = ml_dtypes.bfloat16

NCORES = 8
BL = 4
T = 2048
D = 1024
NT = T // 128
H1, C1, DV1 = 8, 64, 128
E, FF = 8, 4096
LNEPS = 1e-5
SHIFT = 30.0
CUT = 99
ABL = set()
NTILE_DEV = 16

DMAQ = ('DS', 'DP')
ENGINES = ('T', 'V', 'S') + DMAQ
CHANNELS = ENGINES + ('DSq', 'DPq')


class Prog:
    def __init__(self):
        self.ops = {e: [] for e in ENGINES}
        self.count = {c: 0 for c in CHANNELS}
        self.waited = {e: {p: 0 for p in CHANNELS} for e in ENGINES}
        self.qself = {q: 0 for q in DMAQ}

    def op(self, eng, fn, inc=False, sem=None):
        ch = sem or eng
        step = (16 if ch in DMAQ else 1) if inc else 0
        self.count[ch] += step
        self.ops[eng].append(('op', fn, ch if step else None, step))
        return (ch, self.count[ch])

    def dma(self, q, fn):
        return self.op(q, fn, inc=True)

    def wait(self, eng, ev):
        if ev is None:
            return
        peng, val = ev
        if peng in DMAQ:
            val = self.count[peng]
            if self.qself[peng] < val:
                self.qself[peng] = val
                self.ops[peng].append(('wait', peng, val, 0))
        if peng == eng and eng not in DMAQ:
            return
        if peng == eng and eng in DMAQ:
            self.ops[eng].append(('wait', peng, val, 0))
            return
        if self.waited[eng][peng] >= val:
            return
        self.waited[eng][peng] = val
        self.ops[eng].append(('wait', peng, val, 0))

    def barrier(self, engines=ENGINES):
        evs = []
        for e in engines:
            sem = e + 'q' if e in DMAQ else None
            evs.append(self.op(e, (lambda h: h.nop()), inc=True, sem=sem))
        for e in engines:
            for ev in evs:
                self.wait(e, ev)

    def emit_into(self, block, sems):
        prog = self

        def run_stream(handle, eng):
            for kind, a, b, c in prog.ops[eng]:
                if kind == 'wait':
                    handle.wait_ge(sems[a], b)
                else:
                    inst = a(handle)
                    if b is not None:
                        inst.then_inc(sems[b], c)

        @block.tensor
        def _(t):
            run_stream(t, 'T')

        @block.vector
        def _(v):
            run_stream(v, 'V')

        @block.scalar
        def _(s):
            run_stream(s, 'S')

        @block.sync
        def _(s):
            run_stream(s, 'DS')

        @block.gpsimd
        def _(g):
            run_stream(g, 'DP')


class Arena:
    def __init__(self, handle, nbytes, name, align=512):
        self.h = handle
        self.nbytes = nbytes
        self.off = 0
        self.name = name
        self.align = align

    def reset(self, off=0):
        self.off = off

    def take(self, parts, shape, dtype):
        isz = 4 if dtype == F32 else (1 if dtype == U8 else 2)
        cols = int(np.prod(shape))
        start = (self.off + self.align - 1) // self.align * self.align
        self.off = start + cols * isz
        assert self.off <= self.nbytes, (
            f'arena {self.name} overflow: {self.off} > {self.nbytes}')
        ap = self.h.ap()
        if dtype != F32:
            ap = ap.bitcast(dtype)
        v = ap[0:parts, start // isz: start // isz + cols]
        if len(shape) > 1:
            pat = ' '.join(f'a{i}' for i in range(len(shape)))
            kw = {f'a{i}': s for i, s in enumerate(shape)}
            v = v.rearrange(f'p ({pat}) -> p {pat}', **kw)
        return v


def _pc(vec, parts=128):
    return np.ascontiguousarray(np.asarray(vec, np.float32).reshape(-1, parts).T)


def _bc(vec, parts=128):
    return np.ascontiguousarray(np.broadcast_to(
        np.asarray(vec, np.float32)[None, :], (parts, len(vec))))


def host_layout(inputs):
    g = lambda k: np.asarray(inputs[k], dtype=np.float32)
    d = {}
    d['wqt'] = np.ascontiguousarray(g('L0_attn_Wq').T)
    d['wkt'] = np.ascontiguousarray(g('L0_attn_Wk').T)
    d['wvt'] = np.ascontiguousarray(g('L0_attn_Wv').T)
    d['wat'] = np.ascontiguousarray(g('L0_attn_Wa_w').T)
    d['qc'] = np.ascontiguousarray(g('L0_attn_Qc').transpose(1, 0, 2).reshape(C1, H1 * C1))
    d['vc'] = np.ascontiguousarray(g('L0_attn_Vc').transpose(1, 0, 2).reshape(C1, H1 * DV1))
    d['wot'] = np.ascontiguousarray(g('L0_attn_Wo').T)
    d['wab_pc'] = _pc(g('L0_attn_Wa_b'))
    d['wab_bc'] = _bc(g('L0_attn_Wa_b'))
    d['gam_bc'] = _bc(g('L0_attn_gamma'))
    d['bet_bc'] = _bc(g('L0_attn_beta'))
    d['g1_bc'] = _bc(g('L0_ffn_ln1_g'))
    d['b1l_bc'] = _bc(g('L0_ffn_ln1_b'))
    d['g2_bc'] = _bc(g('L0_ffn_ln2_g'))
    d['b2l_bc'] = _bc(g('L0_ffn_ln2_b'))
    d['bo_bc'] = _bc(g('L0_ffn_bo'))
    d['bv_bc'] = _bc(g('L0_ffn_bv'))
    d['bq_pc'] = _pc(g('L0_ffn_bq'))
    d['bk_pc'] = _pc(g('L0_ffn_bk'))
    d['fwq'] = np.ascontiguousarray(g('L0_ffn_wq'))
    d['fwk'] = np.ascontiguousarray(g('L0_ffn_wk'))
    d['fwv'] = np.ascontiguousarray(g('L0_ffn_wv'))
    d['fwo'] = np.ascontiguousarray(g('L0_ffn_wo'))
    d['gatew'] = np.ascontiguousarray(g('L0_ffn_gate_w'))
    d['gateb_bc'] = _bc(g('L0_ffn_gate_b'))
    d['b2t'] = np.ascontiguousarray(g('L0_ffn_b2'))
    d['b1c'] = np.ascontiguousarray(g('L0_ffn_b1').reshape(E * (FF // 128), 128).T)
    d['w1'] = np.ascontiguousarray(g('L0_ffn_w1'))
    d['w2'] = np.ascontiguousarray(g('L0_ffn_w2'))
    d['ident_bf'] = np.eye(128, dtype=np.float32).astype(bfloat16)
    d['ut_f32'] = np.triu(np.ones((128, 128), np.float32))
    d['lt_strict_bf'] = (np.arange(128)[:, None] > np.arange(128)[None, :]
                         ).astype(np.float32).astype(bfloat16)
    d['ones_r_f32'] = np.ones((1, 128), np.float32)
    d['fconsts'] = np.broadcast_to(np.array([[LNEPS]], np.float32), (128, 1)).copy()
    d['ones64_bf'] = np.ones((C1, 1), np.float32).astype(bfloat16)
    return d


WEIGHT_SPECS = [
    ('wqt', (D, C1), F32), ('wkt', (D, C1), F32), ('wvt', (D, C1), F32),
    ('wat', (D, 512), F32),
    ('qc', (C1, H1 * C1), F32), ('vc', (C1, H1 * DV1), F32),
    ('wot', (D, D), F32),
    ('wab_pc', (128, 4), F32), ('wab_bc', (128, 512), F32),
    ('gam_bc', (128, D), F32), ('bet_bc', (128, D), F32),
    ('g1_bc', (128, D), F32), ('b1l_bc', (128, D), F32),
    ('g2_bc', (128, D), F32), ('b2l_bc', (128, D), F32),
    ('bo_bc', (128, D), F32), ('bv_bc', (128, D), F32),
    ('bq_pc', (128, 8), F32), ('bk_pc', (128, 8), F32),
    ('fwq', (D, D), F32), ('fwk', (D, D), F32), ('fwv', (D, D), F32),
    ('fwo', (D, D), F32),
    ('gatew', (D, 8), F32), ('gateb_bc', (128, 8), F32),
    ('b2t', (E, D), F32), ('b1c', (128, E * 32), F32),
    ('w1', (E, D, FF), F32), ('w2', (E, FF, D), F32),
    ('ident_bf', (128, 128), BF16), ('ut_f32', (128, 128), F32),
    ('lt_strict_bf', (128, 128), BF16),
    ('ones_r_f32', (1, 128), F32), ('ones64_bf', (C1, 1), BF16),
    ('fconsts', (128, 1), F32),
]

DEBUG_TAPS = {
    'd_xy': ([BL, T, D], F32), 'd_x1': ([BL, T, D], F32),
    'd_z': ([BL, T, D], F32), 'd_ofm': ([BL, H1 * DV1, T], BF16),
    'd_hn2': ([8, 128, BL * T], BF16), 'd_oatt': ([BL, D, T], BF16),
    'd_x1p': ([BL, T, D], F32), 'd_mask': ([128, BL, NT, 8], F32),
    'd_lg': ([BL, 128, 4, T], F32), 'd_kn': ([BL, C1, T], BF16),
    'd_xn': ([BL, T, D], BF16), 'd_xnfm': ([BL, 128, 8, T], BF16),
    'd_ident': ([128, 128], BF16), 'd_utm': ([128, 128], F32),
    'd_gamv': ([128, D], F32), 'd_qmsc': ([128, T], F32),
    'd_qmh': ([C1, T], BF16), 'd_kmh': ([C1, T], BF16), 'd_vth': ([128, NT * DV1], BF16),
    'd_mids': ([128, 64], F32), 'd_negm': ([128, 64], F32),
    'd_khat': ([BL, 128, NT, 512], BF16), 'd_qlat': ([BL, C1, T], BF16),
    'd_pt': ([128, 128], BF16), 'd_sf': ([C1, 128], F32), 'd_gh': ([C1, 16], F32),
    'd_nst': ([128, 64], F32), 'd_ends': ([128, 64], F32),
    'd_lnat': ([128, 512], BF16), 'd_eks': ([128, 512], F32),
    'd_apre': ([128, 512], F32),
}


def build(debug=(), phases='ABC', nseq=BL, nexp=E):
    nc = bass.Bass('TRN2', target_bir_lowering=False, detect_race_conditions=False)
    P = Prog()
    ED = {}

    dr = {}
    dr['x'] = nc.dram_tensor('x', [BL, T, D], F32, kind='ExternalInput').ap()
    for name, shape, dt in WEIGHT_SPECS:
        dr[name] = nc.dram_tensor(name, list(shape), dt, kind='ExternalInput').ap()
    dr['out'] = nc.dram_tensor('out', [BL, T, D], F32, kind='ExternalOutput').ap()
    dr['s_xy'] = nc.dram_tensor('s_xy', [BL, T, D], F32, kind='Internal').ap()
    dr['s_x1'] = nc.dram_tensor('s_x1', [BL, T, D], F32, kind='Internal').ap()
    dr['s_z'] = nc.dram_tensor('s_z', [BL, T, D], F32, kind='Internal').ap()
    dr['s_hn2'] = nc.dram_tensor('s_hn2', [8, 128, BL * T], BF16, kind='Internal').ap()
    for tname in debug:
        shape, dt = DEBUG_TAPS[tname]
        dr[tname] = nc.dram_tensor(tname, shape, dt, kind='ExternalOutput').ap()

    st = ExitStack()
    AB1 = Arena(st.enter_context(nc.sbuf_tensor('AB1', [128, 65536 // 4], F32)), 65536, 'AB1')
    AB2 = Arena(st.enter_context(nc.sbuf_tensor('AB2', [128, 65536 // 4], F32)), 65536, 'AB2')
    AB3 = Arena(st.enter_context(nc.sbuf_tensor('AB3', [128, 33280 // 4], F32)), 33280, 'AB3')
    AB4 = Arena(st.enter_context(nc.sbuf_tensor('AB4', [128, 31872 // 4], F32)), 31872, 'AB4')

    DED = Arena(st.enter_context(nc.sbuf_tensor('DED', [128, 16512 // 4], F32)),
                16512, 'DED', align=128)

    class _W:
        __slots__ = ('v',)
        def __init__(self, v):
            self.v = v
        def ap(self):
            return self.v

    def ded(name, parts, cols, dt):
        return _W(DED.take(parts, (cols,), dt))

    zcol = ded('zcol', 128, 1, F32)
    fconsts = ded('fconsts', 128, 1, F32)
    nc.const_aps.register(F32, LNEPS, fconsts.ap()[:, 0:1])
    qmxc = ded('qmxc', C1, 4, BF16)
    maskbf = ded('maskbf', 128, 8, BF16)
    ohT = ded('ohT', 8, 128, BF16)
    ident = ded('identt', 128, 128, BF16)
    utm = ded('utm', 128, 128, F32)
    ltm = ded('ltm', 128, 128, BF16)
    ones_rf = ded('ones_rf', 1, 128, F32)
    ones64b = ded('ones64b', C1, 1, BF16)
    statc = ded('statc', 128, 16, F32)
    wabp = ded('wabp', 128, 4, F32)
    bqp = ded('bqp', 128, 8, F32)
    bkp = ded('bkp', 128, 8, F32)
    gatebc = ded('gatebc', 128, 8, F32)
    gatewb = ded('gatewb', 128, 64, BF16)
    b2tb = ded('b2tb', 8, D, BF16)
    b1cc = ded('b1cc', 128, E * 32, F32)
    gamch = ded('gamch', 128, 64, F32)
    midsch = ded('midsch', 128, 64, F32)
    negmid = ded('negmid', 128, 64, F32)
    negst = ded('negst', 128, 64, F32)
    gh = ded('gh', C1, 16, F32)
    kntok = ded('kntok', 128, NT * C1, BF16)
    klatf = ded('klatf', 128, 512, F32)
    ptile = ded('ptile', 128, 256, BF16)
    masks = ded('masks', 128, BL * NT * 8, F32)
    equ8 = ded('equ8', 128, NT * 8, U8)
    keepb = ded('keepb', 128, NT, F32)
    tmpnt = ded('tmpnt', 128, NT, F32)
    onessm = ded('onessm', 128, 128, F32)
    zerossm = ded('zerossm', 128, 128, F32)
    biasc2 = ded('biasc2', 128, 64, F32)
    nubc = ded('nubc', 1, 64, F32)
    lgt_st = ded('lgt_st', 128, 32, F32)
    rncol = ded('rncol', 128, 96, F32)

    ps = {}
    ps['A'] = st.enter_context(nc.psum_tensor('psA', [128, 512], F32))
    ps['B'] = st.enter_context(nc.psum_tensor('psB', [128, 512], F32))
    ps['C'] = st.enter_context(nc.psum_tensor('psC', [128, 512], F32))
    ps['G1'] = st.enter_context(nc.psum_tensor('psG1', [128, 1024], F32))
    ps['G2'] = st.enter_context(nc.psum_tensor('psG2', [128, 1024], F32))
    ps['T'] = st.enter_context(nc.psum_tensor('psT', [128, 1024], BF16))

    sems = {e: st.enter_context(nc.semaphore(f'sem{e}')) for e in CHANNELS}
    block = st.enter_context(nc.Block())

    def LOAD(q, dst_ap, src_ap, waits=()):
        for w in waits:
            P.wait(q, w)
        return P.dma(q, lambda h, dd=dst_ap, ss=src_ap: h.dma_start(out=dd, in_=ss))

    STORE = LOAD

    def ACCUM(q, dst_ap, src_ap, waits=()):
        for w in waits:
            P.wait(q, w)
        return P.dma(q, lambda h, dd=dst_ap, ss=src_ap: h.dma_start(
            out=dd, in_=ss, accum_op=ALU.add))

    def V(fn, inc=True, waits=()):
        for w in waits:
            P.wait('V', w)
        return P.op('V', fn, inc=inc)

    def Vf(fn, waits=()):
        for w in waits:
            P.wait('V', w)
        P.op('V', fn)
        return P.op('V', (lambda v: v.memset(zcol.ap()[:, 0:1], 0.0)), inc=True)

    def S_(fn, inc=True, waits=()):
        for w in waits:
            P.wait('S', w)
        return P.op('S', fn, inc=inc)

    def Sf(fn, waits=()):
        for w in waits:
            P.wait('S', w)
        P.op('S', fn)
        return P.op('S', (lambda s: s.activation(out=zcol.ap()[:, 0:1],
                                                 in_=zcol.ap()[:, 0:1],
                                                 func=AF.Copy)), inc=True)

    def VPAD(n=4):
        for _ in range(n):
            P.op('V', (lambda v: v.memset(zcol.ap()[:, 0:1], 0.0)))

    def SPAD(n=4):
        for _ in range(n):
            P.op('S', (lambda s: s.activation(out=zcol.ap()[:, 0:1],
                                              in_=zcol.ap()[:, 0:1],
                                              func=AF.Copy)))

    def mm_group(out_ap, pairs, waits=(), fence=True):
        for w in waits:
            P.wait('T', w)
        n = len(pairs)
        for j, (lh, rh) in enumerate(pairs):
            ev = P.op('T', (lambda h, o=out_ap, l=lh, r=rh, s=(j == 0),
                            p=(j == n - 1): h.matmul(o, l, r, start=s, stop=p)),
                      inc=(not fence and j == n - 1))
        if fence:
            ev = P.op('T', (lambda h: h.drain()), inc=True)
        return ev

    def transpose8(src_bf, dst_view, nblk=8, waits=()):
        P.wait('T', ED.get('psT_free'))
        for w in waits:
            P.wait('T', w)
        for k in range(nblk):
            P.op('T', (lambda h, k=k: h.transpose(
                ps['T'].ap()[:, k * 128:(k + 1) * 128],
                src_bf[:, k * 128:(k + 1) * 128], ident.ap())))
        evt = P.op('T', (lambda h: h.drain()), inc=True)
        evc = Vf(lambda v: v.tensor_copy(
            dst_view, ps['T'].ap().rearrange('p (a n) -> p a n', a=8)[:, 0:nblk, :]),
            waits=[evt])
        ED['psT_free'] = evc
        return evc

    def dmaj(name):
        return dr[name].rearrange('(a p) n -> p a n', p=128)

    def ln_stats(src_ap, c0, scr, tag='s', src_ev=None):
        sc = statc.ap()
        S_(lambda s: s.activation(out=scr, in_=src_ap, func=AF.Copy,
                                  accum_out=sc[:, c0 + 2:c0 + 3]),
           waits=[ED.get(f'lnscr_{tag}'), src_ev])
        e2 = Sf(lambda s: s.activation(out=scr, in_=src_ap, func=AF.Square,
                                       accum_out=sc[:, c0 + 3:c0 + 4]))
        ED[f'lnscr_{tag}'] = e2
        V(lambda v: v.tensor_scalar(sc[:, c0:c0 + 1], sc[:, c0 + 2:c0 + 3],
                                    1.0 / D, None, ALU.mult), waits=[e2])
        V(lambda v: v.tensor_scalar(sc[:, c0 + 5:c0 + 6], sc[:, c0 + 3:c0 + 4],
                                    1.0 / D, None, ALU.mult))
        VPAD(3)
        V(lambda v: v.tensor_mul(sc[:, c0 + 4:c0 + 5], sc[:, c0:c0 + 1],
                                 sc[:, c0:c0 + 1]))
        VPAD(4)
        e4 = Vf(lambda v: v.tensor_tensor(sc[:, c0 + 5:c0 + 6],
                                          sc[:, c0 + 5:c0 + 6],
                                          sc[:, c0 + 4:c0 + 5], ALU.subtract))
        S_(lambda s: s.activation(out=sc[:, c0 + 5:c0 + 6], in_=sc[:, c0 + 5:c0 + 6],
                                  func=AF.Ln, bias=fconsts.ap()[:, 0:1]), waits=[e4])
        e5 = Sf(lambda s: s.activation(out=sc[:, c0 + 1:c0 + 2],
                                       in_=sc[:, c0 + 5:c0 + 6],
                                       func=AF.Exp, scale=-0.5))
        P.wait('V', e5)
        VPAD(2)
        return sc[:, c0:c0 + 1], sc[:, c0 + 1:c0 + 2]

    def ln_apply(dst, src, mu, rs, waits=(), fence=False):
        fn = Vf if fence else V
        return fn(lambda v: v.tensor_scalar(dst, src, mu, rs,
                                            ALU.subtract, ALU.mult), waits=waits)

    ev_const = []
    for name, tile in [('ident_bf', ident), ('ut_f32', utm), ('lt_strict_bf', ltm),
                       ('ones_r_f32', ones_rf), ('ones64_bf', ones64b),
                       ('wab_pc', wabp), ('bq_pc', bqp),
                       ('bk_pc', bkp), ('gateb_bc', gatebc), ('b1c', b1cc), ('fconsts', fconsts)]:
        ev_const.append(LOAD('DS', tile.ap(), dr[name]))
    if 'b2l' not in ABL:
        ev_const.append(LOAD('DP', b2tb.ap(), dr['b2t']))
        ev_const.append(LOAD('DP', gatewb.ap().rearrange('p (a n) -> p a n', a=8),
                             dmaj('gatew')))
    V(lambda v: v.memset(onessm.ap(), 1.0))
    V(lambda v: v.memset(zerossm.ap(), 0.0))
    V(lambda v: v.memset(zcol.ap(), 0.0))
    for ev in ev_const:
        P.wait('V', ev); P.wait('S', ev); P.wait('T', ev)
    SPAD(4)
    VPAD(4)

    def phaseA(b, sub='1234'):
        AB1.reset(); AB2.reset(); AB3.reset(); AB4.reset()
        xnfm = AB1.take(128, (8, T), BF16)
        ofm = AB1.take(128, (H1, T), BF16)
        gamv = AB2.take(128, (D,), F32)
        betv = AB2.take(128, (D,), F32)
        wkt_b = AB2.take(128, (8, 512), BF16)
        wabbc_v = AB2.take(128, (512,), F32)
        khat = AB3.take(128, (NT, 512), BF16)
        knfm = AB3.take(C1, (T,), BF16)
        qlat = AB3.take(C1, (T,), BF16)
        vlat = AB3.take(C1, (T,), BF16)
        vtok = AB3.take(128, (NT, DV1), BF16)
        x_t = AB4.take(128, (D,), F32)
        y_t = AB4.take(128, (D,), F32)
        scr1 = AB4.take(128, (D,), F32)
        scr2 = AB4.take(128, (D,), F32)
        xn_bf = AB4.take(128, (D,), BF16)
        wqt_b = AB4.take(128, (8, C1), BF16)
        wat_b = AB4.take(128, (8, 512), BF16)
        wvt_b = AB4.take(128, (8, C1), BF16)
        qc_b = AB4.take(C1, (H1 * C1,), BF16)
        vc_b = AB4.take(C1, (H1 * DV1,), BF16)

        evw = [LOAD('DP', wqt_b, dmaj('wqt'), waits=[ED.get('A_w_free')]),
               LOAD('DP', wvt_b, dmaj('wvt')),
               LOAD('DP', wat_b, dmaj('wat'), waits=[ED.get('A_ab2_free')]),
               LOAD('DP', qc_b, dr['qc']),
               LOAD('DP', vc_b, dr['vc']),
               LOAD('DS', gamv, dr['gam_bc'], waits=[ED.get('A_ab2_free')]),
               LOAD('DS', betv, dr['bet_bc']),
               LOAD('DS', wabbc_v, dr['wab_bc'])]
        wk_rep = dmaj('wkt')
        if 'wk' not in ABL:
            for rep in range(8):
                evw.append(LOAD('DP', wkt_b[:, :, rep * 64:(rep + 1) * 64], wk_rep))
        for e in evw:
            P.wait('T', e); P.wait('V', e); P.wait('S', e)
        if 'd_gamv' in debug and b == 0:
            STORE('DS', dr['d_gamv'], gamv, waits=[evw[5]])

        for i in range(min(NT, NTILE_DEV)):
            rows = slice(i * 128, (i + 1) * 128)
            evx = LOAD('DS', x_t, dr['x'][b, rows, :], waits=[ED.get('A_xt_done')])
            P.wait('V', evx); P.wait('S', evx)
            if 'ln' in ABL:
                eyn = V(lambda v: v.tensor_copy(y_t, x_t))
            else:
                P.wait('S', ED.get('A_psB_done'))
                mu, rs = ln_stats(x_t, 0, scr1, src_ev=evx)
                eyn = ln_apply(y_t, x_t, mu, rs, fence=True)
            VPAD(3)
            exy = Vf(lambda v: v.tensor_add(scr2, x_t, y_t),
                     waits=[ED.get('A_xy_stored')])
            if 'st' in ABL:
                ev1 = exy
            else:
                ev1 = STORE('DS', dr['s_xy'][b, rows, :], scr2, waits=[exy])
                if 'd_xy' in debug:
                    ev1 = STORE('DS', dr['d_xy'][b, rows, :], scr2)
            ED['A_xy_stored'] = ev1
            if 'ln' in ABL:
                V(lambda v: v.tensor_copy(x_t, y_t))
            else:
                mu2, rs2 = ln_stats(y_t, 8, scr1, src_ev=eyn)
                ln_apply(x_t, y_t, mu2, rs2)
            VPAD(3)
            ext = V(lambda v: v.tensor_mul(scr1, x_t, gamv))
            ED['A_xt_done'] = ext
            VPAD(3)
            exn = Vf(lambda v: v.tensor_add(xn_bf, scr1, betv))
            if 'd_xn' in debug:
                STORE('DS', dr['d_xn'][b, rows, :], xn_bf, waits=[exn])
            evcp = transpose8(xn_bf, xnfm[:, :, i * 128:(i + 1) * 128], waits=[exn])
            pa = ps['A'].ap()
            pk = ps['B'].ap()
            pr = ps['C'].ap()
            evmm = mm_group(pa, [
                (xnfm[:, dc, rows], wat_b[:, dc, :]) for dc in range(8)],
                waits=[evcp, ED.get('A_psA_done')])
            ea = Vf(lambda v, pa=pa: v.tensor_add(scr1[:, 0:512], pa, wabbc_v),
                    waits=[evmm, ED.get('lnscr_s')])
            ED['A_psA_done'] = ea
            es = Sf(lambda s, f=(AF.Copy if 'sig' in ABL else AF.Sigmoid):
                    s.activation(out=scr1[:, 512:1024], in_=scr1[:, 0:512],
                                 func=f), waits=[ea])
            ec = Vf(lambda v: v.tensor_scalar_max(scr1[:, 512:1024],
                                                  scr1[:, 512:1024], 1e-38),
                    waits=[es])
            el = Sf(lambda s, f=(AF.Copy if 'sig' in ABL else AF.Ln):
                    s.activation(out=xn_bf[:, 0:512],
                                 in_=scr1[:, 512:1024], func=f),
                    waits=[ec])
            import os as _os3
            if _os3.environ.get('NOREV'):
                evr = None
            else:
                evr = mm_group(pr, [(ltm.ap(), xn_bf[:, 0:512])],
                               waits=[el, ED.get('A_psB_done'),
                                      ED.get('A_psC_done')])
            evkl = mm_group(pk, [
                (xnfm[:, dc, rows],
                 (wat_b if 'wk' in ABL else wkt_b)[:, dc, :]) for dc in range(8)],
                waits=[ED.get('A_psB_done')])
            Vf(lambda v, pk=pk: v.tensor_copy(klatf.ap()[:, 0:512], pk),
               waits=[evkl])
            ecopy_ev = ('V', P.count['V'])
            if 'sqred' not in ABL:
                esq0 = ecopy_ev
                e1 = Sf(lambda s: s.activation(out=scr1[:, 512:576],
                                               in_=klatf.ap()[:, 0:64],
                                               func=AF.Square,
                                               accum_out=rncol.ap()[:, 32:33]),
                        waits=[esq0, ED.get('A_ksq_red')])
                SPAD(4)
            else:
                e1 = ecopy_ev
            if 'nrm' not in ABL:
                S_(lambda s: s.activation(out=rncol.ap()[:, 32:33],
                                          in_=rncol.ap()[:, 32:33], func=AF.Ln),
                   waits=[e1, ED.get('A_rn_used')])
                e2 = Sf(lambda s: s.activation(out=rncol.ap()[:, 64:65],
                                               in_=rncol.ap()[:, 32:33],
                                               func=AF.Exp, scale=0.5))
                V(lambda v: v.tensor_scalar_max(rncol.ap()[:, 64:65],
                                                rncol.ap()[:, 64:65], 1e-12),
                  waits=[e2])
            if 'tail' in ABL:
                ED['A_psB_done'] = ecopy_ev
                ED['A_psC_done'] = None
                continue
            VPAD(3)
            ED['A_rn_used'] = V(lambda v: v.reciprocal(rncol.ap()[:, 0:1],
                                                       rncol.ap()[:, 64:65]))
            VPAD(4)
            V(lambda v, i=i: v.tensor_scalar(kntok.ap()[:, i * 64:(i + 1) * 64],
                                             klatf.ap()[:, 0:64],
                                             rncol.ap()[:, 0:1], None, ALU.mult))
            if 'd_lnat' in debug and b == 0 and i == 8:
                P.barrier()
                STORE('DS', dr['d_apre'], scr1[:, 0:512])
                evt_ = STORE('DS', dr['d_lnat'], xn_bf[:, 0:512])
                for e_ in ('T', 'V', 'S'):
                    P.wait(e_, evt_)
            eks = Sf(lambda s, pr=pr: s.activation(out=scr1[:, 0:512], in_=pr,
                                                   func=AF.Exp), waits=[evr])
            ED['A_psC_done'] = eks
            if 'd_eks' in debug and b == 0 and i == 8:
                P.barrier()
                evt_ = STORE('DS', dr['d_eks'], scr1[:, 0:512])
                for e_ in ('T', 'V', 'S'):
                    P.wait(e_, evt_)
            ekh = Vf(lambda v, i=i, pk=pk: v.scalar_tensor_tensor(
                khat[:, i, :], pk, rncol.ap()[:, 0:1], scr1[:, 0:512],
                ALU.mult, ALU.mult), waits=[eks])
            ED['A_psB_done'] = ekh

        if '2' not in sub:
            return
        AB2.reset()
        lnafm = AB2.take(128, (4, T), F32)
        LG = AB2.take(128, (4, T), F32)
        ED['A_ab2_free'] = None
        ED['A_kntok_done'] = ED.get('A_psB_done')
        for tg in range(4):
            cols = slice(tg * 512, (tg + 1) * 512)
            evq = mm_group(ps['A'].ap()[0:64, :], [
                (wqt_b[:, dc, :], xnfm[:, dc, cols]) for dc in range(8)],
                waits=[ED.get('A_psA_done')])
            eq = Vf(lambda v, cols=cols: v.tensor_copy(qlat[:, cols],
                                                       ps['A'].ap()[0:64, :]),
                    waits=[evq])
            ED['A_psA_done'] = eq
            evv = mm_group(ps['B'].ap()[0:64, :], [
                (wvt_b[:, dc, :], xnfm[:, dc, cols]) for dc in range(8)],
                waits=[ED.get('A_psB_done')])
            ev2 = Vf(lambda v, cols=cols: v.tensor_copy(vlat[:, cols],
                                                        ps['B'].ap()[0:64, :]),
                     waits=[evv])
            ED['A_psB_done'] = ev2
        for jc in range(4):
            for tg in range(4):
                cols = slice(tg * 512, (tg + 1) * 512)
                pp, key = (ps['A'], 'A_psA_done') if tg % 2 == 0 else (ps['B'], 'A_psB_done')
                ev = mm_group(pp.ap(), [
                    (wat_b[:, dc, jc * 128:(jc + 1) * 128], xnfm[:, dc, cols])
                    for dc in range(8)], waits=[ED.get(key)])
                es = S_(lambda s, pp=pp, jc=jc: s.activation(
                    out=scr1[:, 0:512], in_=pp.ap(), func=AF.Sigmoid,
                    bias=wabp.ap()[:, jc:jc + 1]),
                    waits=[ev, ED.get('A_scr1_free')])
                ED[key] = es
                ec = Vf(lambda v: v.tensor_scalar_max(scr1[:, 0:512],
                                                      scr1[:, 0:512], 1e-38),
                        waits=[es])
                el = Sf(lambda s, jc=jc, cols=cols: s.activation(
                    out=lnafm[:, jc, cols], in_=scr1[:, 0:512], func=AF.Ln),
                    waits=[ec])
                ED['A_scr1_free'] = el
        for half in range(2):
            P.wait('T', ED.get('psT_free'))
            P.wait('T', ED.get('A_kntok_done'))
            for j in range(8):
                i = half * 8 + j
                P.op('T', (lambda h, i=i, j=j: h.transpose(
                    ps['T'].ap()[0:64, j * 128:(j + 1) * 128],
                    kntok.ap()[:, i * 64:(i + 1) * 64], ident.ap())))
            evt = P.op('T', (lambda h: h.drain()), inc=True)
            evc = Vf(lambda v, half=half: v.tensor_copy(
                knfm[:, half * 1024:(half + 1) * 1024].rearrange(
                    'p (a n) -> p a n', a=8),
                ps['T'].ap().rearrange('p (a n) -> p a n', a=8)[0:64, :, :]),
                waits=[evt])
            ED['psT_free'] = evc
        P.wait('V', ED.get('A_scr1_free'))
        for h in range(H1):
            jc, half = h // 2, h % 2
            r0 = 64 * half
            zb = zcol.ap()[r0:r0 + 64, :].broadcast_to([C1, T])
            P.op('V', (lambda v, jc=jc, r0=r0, zb=zb: v.tensor_tensor_scan(
                LG[r0:r0 + 64, jc, :], lnafm[r0:r0 + 64, jc, :], zb,
                0.0, ALU.add, ALU.add)))
        VPAD(4)
        escan = P.op('V', (lambda v: v.memset(zcol.ap()[:, 0:1], 0.0)), inc=True)
        ED['A_scan'] = escan
        if 'd_xnfm' in debug:
            P.barrier()
            STORE('DS', dr['d_xnfm'][b], xnfm)
        if 'd_ident' in debug and b == 0:
            P.barrier()
            STORE('DS', dr['d_ident'], ident.ap())
            STORE('DS', dr['d_utm'], utm.ap())

        if 'd_lg' in debug:
            P.barrier()
            STORE('DS', dr['d_lg'][b], LG)
        if 'd_kn' in debug:
            P.barrier()
            STORE('DS', dr['d_kn'][b], knfm)
        if 'd_khat' in debug:
            P.barrier()
            STORE('DS', dr['d_khat'][b], khat)
        if 'd_qlat' in debug:
            P.barrier()
            STORE('DS', dr['d_qlat'][b], qlat)
        LG4 = LG.rearrange('p a (c l) -> p a c l', l=128)
        g4 = gamch.ap().rearrange('p (a c) -> p a c', a=4)
        m4 = midsch.ap().rearrange('p (a c) -> p a c', a=4)
        nm4 = negmid.ap().rearrange('p (a c) -> p a c', a=4)
        ns4 = negst.ap().rearrange('p (a c) -> p a c', a=4)
        P.barrier(engines=('T', 'V', 'S'))
        for jc in range(4):
            LG3 = LG[:, jc, :].rearrange('p (c l) -> p c l', l=128)
            V(lambda v, jc=jc, LG3=LG3: v.tensor_copy(g4[:, jc, :], LG3[:, :, 127]),
              waits=[escan] if jc == 0 else ())
            V(lambda v, jc=jc, LG3=LG3: v.tensor_copy(m4[:, jc, :], LG3[:, :, 63]))
            V(lambda v, jc=jc: v.memset(ns4[:, jc, 0:1], 0.0))
            V(lambda v, jc=jc: v.tensor_copy(ns4[:, jc, 1:16], g4[:, jc, 0:15]))
        if 'd_ends' in debug and b == 0:
            P.barrier()
            evt_ = STORE('DS', dr['d_ends'], gamch.ap())
            for e_ in ('T', 'V', 'S'):
                P.wait(e_, evt_)
        VPAD(3)
        V(lambda v: v.tensor_scalar(negmid.ap(), midsch.ap(), -1.0, None, ALU.mult))
        ecp = V(lambda v: v.tensor_scalar(negst.ap(), negst.ap(), -1.0, None,
                                          ALU.mult))
        VPAD(3)
        if 'd_nst' in debug and b == 0:
            P.barrier()
            evt_ = STORE('DS', dr['d_nst'], negst.ap())
            for e_ in ('T', 'V', 'S'):
                P.wait(e_, evt_)
        P.barrier(engines=('T', 'V', 'S'))
        ega = Vf(lambda v: v.tensor_add(gamch.ap(), gamch.ap(), negst.ap()))
        egb = S_(lambda s: s.activation(out=gamch.ap(), in_=gamch.ap(), func=AF.Exp),
                 waits=[ega])
        P.barrier(engines=('T', 'V', 'S'))
        ez1 = None
        ez2 = None
        ED['A_refs'] = egb

        if '3' not in sub:
            return
        AB4.reset()
        qmid_h = AB4.take(C1, (T,), BF16)
        qst_h = AB4.take(C1, (T,), BF16)
        kmid_h = AB4.take(C1, (T,), BF16)
        AB2.reset()
        qmid_sc = AB2.take(128, (T,), F32)
        ksc_h = AB2.take(C1, (T,), F32)
        qst_sc = AB2.take(128, (T,), F32)
        Sfv = AB2.take(C1, (H1 * DV1,), F32)
        Sbv = AB2.take(C1, (H1 * DV1,), BF16)
        assert AB2.off <= 32768
        ez1 = V(lambda v: v.memset(Sfv, 0.0), waits=[ED.get('A_state_add')])
        ez2 = V(lambda v: v.memset(Sbv, 0.0))

        for jc in range(4):
            ev_sc = None
            for w in (ED.get('A_scan'), ED.get('A_refs'), ED.get('A_scales_used')):
                P.wait('S', w)
            SPAD(4)
            for ci in range(16):
                ccols = slice(ci * 128, (ci + 1) * 128)
                S_(lambda s, jc=jc, ci=ci, ccols=ccols: s.activation(
                    out=qmid_sc[:, ccols], in_=LG[:, jc, ccols], func=AF.Exp,
                    bias=nm4[:, jc, ci:ci + 1]))
                ev_sc = S_(lambda s, jc=jc, ci=ci, ccols=ccols: s.activation(
                    out=qst_sc[:, ccols], in_=LG[:, jc, ccols], func=AF.Exp,
                    bias=ns4[:, jc, ci:ci + 1]))
            if 'd_qmsc' in debug and b == 0 and jc == 0:
                P.barrier()
                evt_ = STORE('DS', dr['d_qmsc'], qmid_sc)
                if 'd_mids' in debug:
                    STORE('DS', dr['d_mids'], midsch.ap())
                    evt_ = STORE('DS', dr['d_negm'], negmid.ap())
                for e_ in ('T', 'V', 'S'):
                    P.wait(e_, evt_)
            for half in range(2):
                h = jc * 2 + half
                r0 = 64 * half
                evgh = Vf(lambda v, r0=r0, jc=jc: v.tensor_copy(
                    gh.ap(), g4[r0:r0 + 64, jc, :]),
                    waits=[egb, ED.get('A_gh_used')])
                if 'd_gh' in debug and b == 0 and h == 0:
                    P.barrier()
                    evt_ = STORE('DS', dr['d_gh'], gh.ap())
                    for e_ in ('T', 'V', 'S'):
                        P.wait(e_, evt_)
                evksc = None
                for ci in range(16):
                    ccols2 = slice(ci * 128, (ci + 1) * 128)
                    SX = Sf if ci == 15 else S_
                    evksc = SX(lambda s, jc=jc, ci=ci, ccols2=ccols2, r0=r0:
                               s.activation(
                        out=ksc_h[:, ccols2], in_=LG[r0:r0 + 64, jc, ccols2],
                        func=AF.Exp, scale=-1.0,
                        bias=m4[r0:r0 + 64, jc, ci:ci + 1]),
                        waits=[ED.get('A_ksc_used')] if ci == 0 else ())
                evkm = Vf(lambda v: v.tensor_mul(kmid_h, knfm, ksc_h),
                    waits=[evksc, ED.get('A_kmid_used')])
                ED['A_ksc_used'] = evkm
                for half2 in range(2):
                    P.wait('T', ED.get('psG2_free'))
                    for j in range(8):
                        i2 = half2 * 8 + j
                        P.op('T', (lambda h_, i2=i2, j=j, hh=h: h_.matmul(
                            ps['G2'].ap()[:, j * 128:(j + 1) * 128],
                            vlat[:, i2 * 128:(i2 + 1) * 128],
                            vc_b[:, hh * 128:(hh + 1) * 128],
                            start=True, stop=True)))
                    evm = P.op('T', (lambda h_: h_.drain()), inc=True)
                    evc = Vf(lambda v, half2=half2: v.tensor_copy(
                        vtok[:, half2 * 8:(half2 + 1) * 8, :],
                        ps['G2'].ap().rearrange('p (a n) -> p a n', a=8)),
                        waits=[evm, ED.get('A_vtok_used')])
                    ED['psG2_free'] = evc
                ev_vtok = ED['psG2_free']
                evq2 = None
                for tg in range(4):
                    cols = slice(tg * 512, (tg + 1) * 512)
                    pp, key = (ps['A'], 'A_psA_done') if tg % 2 == 0 else (ps['B'], 'A_psB_done')
                    evq = mm_group(pp.ap()[0:64, :], [
                        (qc_b[:, h * 64:(h + 1) * 64], qlat[:, cols])],
                        waits=[ED.get(key)])
                    V(lambda v, pp=pp, cols=cols, r0=r0: v.scalar_tensor_tensor(
                        qmid_h[:, cols], pp.ap()[0:64, :], 1.0,
                        qmid_sc[r0:r0 + 64, cols], ALU.mult, ALU.mult),
                      waits=[evq, ED.get('A_qmid_used')])
                    evq2 = Vf(lambda v, pp=pp, cols=cols, r0=r0: v.scalar_tensor_tensor(
                        qst_h[:, cols], pp.ap()[0:64, :], 1.0,
                        qst_sc[r0:r0 + 64, cols], ALU.mult, ALU.mult),
                        waits=[ED.get('A_qst_used')])
                    ED[key] = evq2
                ED['A_scales_used'] = evq2
                if 'd_qmh' in debug and b == 0 and h == 0:
                    P.barrier()
                    STORE('DS', dr['d_qmh'], qmid_h)
                    STORE('DS', dr['d_kmh'], kmid_h)
                    evt_ = STORE('DS', dr['d_vth'],
                                 vtok.rearrange('p a n -> p (a n)'))
                    for e_ in ('T', 'V', 'S'):
                        P.wait(e_, evt_)
                for c in range(NT):
                    ccols = slice(c * 128, (c + 1) * 128)
                    pPt, key = (ps['A'], 'A_psA_done') if c % 2 == 0 else (ps['B'], 'A_psB_done')
                    evP = mm_group(pPt.ap()[:, 0:128], [
                        (kmid_h[:, ccols], qmid_h[:, ccols])],
                        waits=[ED.get(key)])
                    evPm = Vf(lambda v, pPt=pPt, c=c: v.tensor_mul(
                        ptile.ap()[:, (c % 2) * 128:(c % 2) * 128 + 128],
                        pPt.ap()[:, 0:128], utm.ap()),
                        waits=[evP, ED.get(f'A_pt{c % 2}_used')])
                    ED[key] = evPm
                    if 'd_pt' in debug and b == 0 and h == 0 and c == 0:
                        P.barrier()
                        evt_ = STORE('DS', dr['d_pt'], ptile.ap()[:, 0:128])
                        for e_ in ('T', 'V', 'S'):
                            P.wait(e_, evt_)
                    evO = mm_group(ps['G1'].ap()[:, 0:128], [
                        (vtok[:, c, :], ptile.ap()[:, (c % 2) * 128:(c % 2) * 128 + 128]),
                        (Sbv[:, h * 128:(h + 1) * 128], qst_h[:, ccols]),
                    ], waits=[evPm, ev_vtok, ED.get('A_state_copy'),
                              ED.get('A_o_evac'), ez2 if (h == 0 and c == 0) else None])
                    ED[f'A_pt{c % 2}_used'] = evO
                    evoe = Vf(lambda v, h=h, ccols=ccols: v.tensor_copy(
                        ofm[:, h, ccols], ps['G1'].ap()[:, 0:128]), waits=[evO])
                    ED['A_o_evac'] = evoe
                    evDl = mm_group(ps['C'].ap()[0:64, 0:128], [
                        (khat[:, c, h * 64:(h + 1) * 64], vtok[:, c, :])],
                        waits=[ED.get('A_state_add'), ev_vtok, ED.get('A_psC_done')])
                    ED['A_psC_done'] = None
                    evsa = V(lambda v, h=h, c=c: v.scalar_tensor_tensor(
                        Sfv[:, h * 128:(h + 1) * 128],
                        Sfv[:, h * 128:(h + 1) * 128],
                        gh.ap()[:, c:c + 1],
                        ps['C'].ap()[0:64, 0:128], ALU.mult, ALU.add),
                      waits=[evgh, evO, evDl, ez1 if (h == 0 and c == 0) else None])
                    ED['A_state_add'] = evsa
                    VPAD(3)
                    evsc = Vf(lambda v, h=h: v.tensor_copy(
                        Sbv[:, h * 128:(h + 1) * 128],
                        Sfv[:, h * 128:(h + 1) * 128]))
                    ED['A_state_copy'] = evsc
                    if 'd_sf' in debug and b == 0 and h == 0 and c == 0:
                        P.barrier()
                        evt_ = STORE('DS', dr['d_sf'], Sfv[:, 0:128])
                        for e_ in ('T', 'V', 'S'):
                            P.wait(e_, evt_)
                ED['A_gh_used'] = ED['A_state_add']
                ED['A_kmid_used'] = ED[f'A_pt{(NT - 1) % 2}_used']
                ED['A_qmid_used'] = ED['A_kmid_used']
                ED['A_qst_used'] = ED['A_o_evac']
                ED['A_vtok_used'] = ED['A_state_add']
        if 'd_ofm' in debug:
            P.barrier()
            STORE('DS', dr['d_ofm'][b].rearrange('(a p) t -> p a t', p=128), ofm)
        P.barrier()

        if '4' not in sub:
            return
        AB2.reset()
        wot_b = AB2.take(128, (8, D), BF16)
        g1v = AB2.take(128, (D,), F32)
        b1lv = AB2.take(128, (D,), F32)
        ev1 = LOAD('DP', wot_b, dmaj('wot'))
        ev2 = LOAD('DS', g1v, dr['g1_bc'])
        ev3 = LOAD('DS', b1lv, dr['b1l_bc'])
        for e in (ev1, ev2, ev3):
            P.wait('T', e); P.wait('V', e); P.wait('S', e)
        hnfm = xnfm

        def a4_ow(i):
            rows = slice(i * 128, (i + 1) * 128)
            pg = ps['G2'] if i % 2 == 0 else ps['G1']
            evo = None
            for gc in range(2):
                evo = mm_group(pg.ap()[:, gc * 512:(gc + 1) * 512], [
                    (ofm[:, hh, rows], wot_b[:, hh, gc * 512:(gc + 1) * 512])
                    for hh in range(8)],
                    waits=[ED.get(f'A4_pg{i % 2}'),
                           ED.get('A_o_evac') if i == 0 else None,
                           ED.get('psG2_free') if i == 0 else None])
            return evo
        evo_next = a4_ow(0)
        for i in range(NT):
            rows = slice(i * 128, (i + 1) * 128)
            evxy = LOAD('DS', x_t, dr['s_xy'][b, rows, :],
                        waits=[ED.get('A4_xt_done')])
            pg = ps['G2'] if i % 2 == 0 else ps['G1']
            key = f'A4_pg{i % 2}'
            evo = evo_next
            if i + 1 < NT:
                evo_next = a4_ow(i + 1)
            ex1 = Vf(lambda v, pg=pg: v.tensor_add(y_t, x_t, pg.ap()),
                     waits=[evo, evxy, ED.get('A4_yt_stored')])
            ED[key] = ex1
            ev1 = STORE('DS', dr['s_x1'][b, rows, :], y_t, waits=[ex1])
            if 'd_x1' in debug:
                ev1 = STORE('DS', dr['d_x1'][b, rows, :], y_t)
            ED['A4_yt_stored'] = ev1
            P.wait('S', ED.get('A4_ehn'))
            mu, rs = ln_stats(y_t, 0, scr1, src_ev=ex1)
            ez = ln_apply(x_t, y_t, mu, rs,
                          waits=[ED.get('A4_zt_stored')], fence=True)
            ev2 = STORE('DS', dr['s_z'][b, rows, :], x_t, waits=[ez])
            if 'd_z' in debug:
                ev2 = STORE('DS', dr['d_z'][b, rows, :], x_t)
            ED['A4_zt_stored'] = ev2
            ED['A4_xt_done'] = ev2
            mu2, rs2 = ln_stats(x_t, 8, scr1, src_ev=ez)
            ln_apply(scr2, x_t, mu2, rs2)
            VPAD(3)
            V(lambda v: v.tensor_mul(scr1, scr2, g1v),
              waits=[ED.get('lnscr_s')])
            VPAD(3)
            ehn = Vf(lambda v: v.tensor_add(xn_bf, scr1, b1lv))
            ED['A4_ehn'] = ehn
            transpose8(xn_bf, hnfm[:, :, i * 128:(i + 1) * 128], waits=[ehn])
        P.barrier()

    def phaseB(b):
        AB2.reset()
        fwq_b = AB2.take(128, (8, D), BF16)
        fwk_b = AB2.take(128, (8, D), BF16)
        oatt = AB2.take(128, (8, T), BF16)
        AB3.reset()
        vpr = AB3.take(128, (16, 16, 65), BF16)
        hnfm = AB1.h.ap().bitcast(BF16)[:, 0:8 * T].rearrange('p (a n) -> p a n', a=8)
        pstr = AB1.h.ap().bitcast(BF16)[:, 8 * T:8 * T + 16 * 1024].rearrange(
            'p (a n) -> p a n', a=16)
        AB4.reset()
        qh0 = AB4.take(C1, (T,), BF16)
        qh1 = AB4.take(C1, (T,), BF16)
        kh0 = AB4.take(C1, (T,), BF16)
        kh1 = AB4.take(C1, (T,), BF16)
        rbc_sb = AB4.take(C1, (512,), F32)
        scrB = AB4.take(128, (D,), F32)
        rrow_v = AB4.take(1, (512,), F32)

        evfv = LOAD('DP', fwq_b, dmaj('fwv'))
        evst = V(lambda v: v.memset(vpr[:, :, :, 64:65], 1.0))
        evbv = LOAD('DS', scrB, dr['bv_bc'])
        P.wait('T', evfv)
        for i in range(NT):
            for vh in range(2):
                pp, key = ((ps['A'], 'B_psA_done') if (i * 2 + vh) % 2 == 0
                           else (ps['B'], 'B_psB_done'))
                ev = mm_group(pp.ap(), [
                    (hnfm[:, dc, i * 128:(i + 1) * 128],
                     fwq_b[:, dc, vh * 512:(vh + 1) * 512]) for dc in range(8)],
                    waits=[ED.get(key)])
                evv = Vf(lambda v, pp=pp, i=i, vh=vh: v.tensor_tensor(
                    vpr[:, i, vh * 8:(vh + 1) * 8, 0:64],
                    pp.ap().rearrange('p (a n) -> p a n', a=8),
                    scrB[:, vh * 512:(vh + 1) * 512].rearrange('p (a n) -> p a n', a=8),
                    ALU.add), waits=[ev, evbv, evst])
                ED[key] = evv
        P.barrier()
        evq = LOAD('DP', fwq_b, dmaj('fwq'))
        evk = LOAD('DP', fwk_b, dmaj('fwk'))
        P.wait('T', evq); P.wait('T', evk)

        for fc in range(8):
            for (wsb, bcol, t0, t1) in ((fwq_b, bqp, qh0, qh1),
                                        (fwk_b, bkp, kh0, kh1)):
                for tg in range(4):
                    cols = slice(tg * 512, (tg + 1) * 512)
                    pp, key = ((ps['A'], 'B_psA_done') if tg % 2 == 0
                               else (ps['B'], 'B_psB_done'))
                    ev = mm_group(pp.ap(), [
                        (wsb[:, dc, fc * 128:(fc + 1) * 128], hnfm[:, dc, cols])
                        for dc in range(8)], waits=[ED.get(key)])
                    V(lambda v, pp=pp, cols=cols, t0=t0, bcol=bcol, fc=fc:
                      v.tensor_scalar(t0[:, cols], pp.ap()[0:64, :],
                                      bcol.ap()[0:64, fc:fc + 1], None, ALU.add),
                      waits=[ev, ED.get('B_qk_used')])
                    ev2 = Vf(lambda v, pp=pp, cols=cols, t1=t1, bcol=bcol, fc=fc:
                             v.tensor_scalar(t1[:, cols], pp.ap()[64:128, :],
                                             bcol.ap()[64:128, fc:fc + 1], None,
                                             ALU.add))
                    ED[key] = ev2
            for half in range(2):
                h = fc * 2 + half
                qh = qh0 if half == 0 else qh1
                kh = kh0 if half == 0 else kh1
                V(lambda v, qh=qh: v.reduce_max(
                    qmxc.ap()[:, 0:1], qh, axis=AX.X, apply_absolute_value=True))
                V(lambda v, kh=kh: v.reduce_max(
                    qmxc.ap()[:, 1:2], kh, axis=AX.X, apply_absolute_value=True))
                VPAD(4)
                ep = Vf(lambda v: v.tensor_mul(qmxc.ap()[:, 2:3], qmxc.ap()[:, 0:1],
                                               qmxc.ap()[:, 1:2]))
                evu = mm_group(ps['C'].ap()[0:1, 0:1], [
                    (qmxc.ap()[:, 2:3], ones64b.ap())],
                    waits=[ep, ED.get('B_psC_done')])
                evn = Vf(lambda v, half=half: v.tensor_scalar(
                    nubc.ap()[:, half * 32:half * 32 + 1], ps['C'].ap()[0:1, 0:1],
                    -0.125, SHIFT, ALU.mult, ALU.add), waits=[evu])
                evb = mm_group(ps['C'].ap()[:, 128:129], [
                    (ones_rf.ap(), nubc.ap()[:, half * 32:half * 32 + 1])], waits=[evn])
                evbc = Vf(lambda v, half=half: v.tensor_copy(
                    biasc2.ap()[:, half * 32:half * 32 + 1], ps['C'].ap()[:, 128:129]),
                    waits=[evb, ED.get(f'B_bias{half}_used')])
                ED['B_psC_done'] = evbc
                P.wait('S', evbc)
                SPAD(4)
                for qc in range(2):
                    evex = None
                    for kc in range(16):
                        pg, key = ((ps['G1'], 'B_G1_done') if kc % 2 == 0
                                   else (ps['G2'], 'B_G2_done'))
                        P.wait('T', ED.get(key))
                        for hseg in range(2):
                            P.op('T', (lambda h_, pg=pg, kc=kc, hseg=hseg,
                                       qh=qh, kh=kh, qc=qc: h_.matmul(
                                pg.ap()[:, hseg * 512:(hseg + 1) * 512],
                                kh[:, kc * 128:(kc + 1) * 128],
                                qh[:, qc * 1024 + hseg * 512:
                                   qc * 1024 + (hseg + 1) * 512],
                                start=True, stop=True)))
                        evm = P.op('T', (lambda h_: h_.drain()), inc=True)
                        evex = Sf(lambda s, pg=pg, kc=kc, half=half: s.activation(
                            out=pstr[:, kc, :], in_=pg.ap(), func=AF.Exp,
                            scale=0.125, bias=biasc2.ap()[:, half * 32:half * 32 + 1]),
                            waits=[evm, ED.get(f'B_pstr{kc}_used')])
                        ED[key] = evex
                        ED[f'B_exp{kc % 2}'] = evex
                    ED[f'B_bias{half}_used'] = evex
                    for sg in range(2):
                        pp, key = ((ps['A'], 'B_psA_done') if sg % 2 == 0
                                   else (ps['B'], 'B_psB_done'))
                        ev = mm_group(pp.ap()[0:65, :], [
                            (vpr[:, kc2, h, :], pstr[:, kc2, sg * 512:(sg + 1) * 512])
                            for kc2 in range(16)],
                            waits=[ED.get(key), ED.get('B_exp0'), ED.get('B_exp1')])
                        for kc2 in range(16):
                            ED[f'B_pstr{kc2}_used'] = ev
                        evr = Vf(lambda v, pp=pp: v.reciprocal(
                            rrow_v, pp.ap()[64:65, :]),
                            waits=[ev, ED.get('B_rrow_used')])
                        evb2 = mm_group(ps['C'].ap()[0:64, 0:512], [
                            (ones_rf.ap()[:, 0:64], rrow_v)],
                            waits=[evr, ED.get('B_psC_done')])
                        ED['B_rrow_used'] = evb2
                        evrb = V(lambda v: v.tensor_copy(rbc_sb,
                                                         ps['C'].ap()[0:64, 0:512]),
                                 waits=[evb2, ED.get('B_rbc_used')])
                        ED['B_psC_done'] = evrb
                        VPAD(3)
                        r0 = 64 * (h % 2)
                        evoa = Vf(lambda v, pp=pp, h=h, qc=qc, sg=sg, r0=r0:
                                  v.scalar_tensor_tensor(
                                      oatt[r0:r0 + 64, h // 2,
                                           qc * 1024 + sg * 512:
                                           qc * 1024 + (sg + 1) * 512],
                                      pp.ap()[0:64, :], 1.0, rbc_sb,
                                      ALU.mult, ALU.mult))
                        ED['B_rbc_used'] = evoa
                        ED[key] = evoa
                ED['B_qk_used'] = ED.get('B_psA_done')
        P.barrier()
        if 'd_oatt' in debug:
            STORE('DS', dr['d_oatt'][b].rearrange('(a p) t -> p a t', p=128), oatt)
            P.barrier()

        AB2.off = 0
        fwo_b = AB2.take(128, (8, D), BF16)
        g2v = AB2.take(128, (D,), F32)
        b2lv = AB2.take(128, (D,), F32)
        bov = AB2.take(128, (D,), F32)
        ev1 = LOAD('DP', fwo_b, dmaj('fwo'))
        ev2 = LOAD('DS', g2v, dr['g2_bc'])
        ev3 = LOAD('DS', b2lv, dr['b2l_bc'])
        ev4 = LOAD('DS', bov, dr['bo_bc'])
        for e in (ev1, ev2, ev3, ev4):
            P.wait('T', e); P.wait('V', e); P.wait('S', e)
        AB4.off = 0
        z_t = AB4.take(128, (D,), F32)
        x1p = AB4.take(128, (D,), F32)
        x1_t = AB4.take(128, (D,), F32)
        scr1 = AB4.take(128, (D,), F32)
        scr2b = AB4.take(128, (D,), F32)
        hn2bf = AB4.take(128, (D,), BF16)
        hn2f_t = AB4.take(128, (8, 128), BF16)
        lgt = AB4.take(128, (NT, 8), F32)

        def b4_ow(i):
            rows = slice(i * 128, (i + 1) * 128)
            pg = ps['G1'] if i % 2 == 0 else ps['G2']
            evo = None
            for gc in range(2):
                evo = mm_group(pg.ap()[:, gc * 512:(gc + 1) * 512], [
                    (oatt[:, hh, rows], fwo_b[:, hh, gc * 512:(gc + 1) * 512])
                    for hh in range(8)], waits=[ED.get(f'B4_pg{i % 2}')])
            return evo
        evo_next = b4_ow(0)
        for i in range(NT):
            rows = slice(i * 128, (i + 1) * 128)
            evz = LOAD('DS', z_t, dr['s_z'][b, rows, :], waits=[ED.get('B4_zt_done')])
            evx1 = LOAD('DS', x1_t, dr['s_x1'][b, rows, :],
                        waits=[ED.get('B4_x1_done')])
            pg = ps['G1'] if i % 2 == 0 else ps['G2']
            key = f'B4_pg{i % 2}'
            evo = evo_next
            if i + 1 < NT:
                evo_next = b4_ow(i + 1)
            ezt = V(lambda v, pg=pg: v.tensor_add(x1p, z_t, pg.ap()),
                    waits=[evo, evz])
            ED['B4_zt_done'] = ezt
            ED[key] = ezt
            VPAD(3)
            ex1p = Vf(lambda v: v.tensor_add(x1p, x1p, bov))
            if 'd_x1p' in debug:
                STORE('DS', dr['d_x1p'][b, rows, :], x1p, waits=[ex1p])
            VPAD(2)
            eob = Vf(lambda v: v.tensor_add(scr2b, x1_t, x1p),
                     waits=[evx1, ED.get('B4_base_stored')])
            ED['B4_x1_done'] = eob
            evbs = STORE('DS', dr['out'][b, rows, :], scr2b, waits=[eob])
            ED['B4_base_stored'] = evbs
            P.wait('S', ED.get('B4_ehn'))
            mu, rs = ln_stats(x1p, 0, scr1, src_ev=ex1p)
            ln_apply(scr2b, x1p, mu, rs, waits=[evbs])
            VPAD(3)
            V(lambda v: v.tensor_mul(scr1, scr2b, g2v),
              waits=[ED.get('lnscr_s')])
            VPAD(3)
            ehn = Vf(lambda v: v.tensor_add(hn2bf, scr1, b2lv))
            ED['B4_ehn'] = ehn
            evcp = transpose8(hn2bf, hn2f_t, waits=[ehn])
            c0 = b * T + i * 128
            evh = STORE('DS', dr['s_hn2'][:, :, c0:c0 + 128].rearrange(
                'd p t -> p d t'), hn2f_t, waits=[evcp])
            if 'd_hn2' in debug:
                evh = STORE('DS', dr['d_hn2'][:, :, c0:c0 + 128].rearrange(
                    'd p t -> p d t'), hn2f_t)
            ED['B4_hn2_stored'] = evh
            P.wait('V', evh)
            evg = mm_group(ps['C'].ap()[:, 256:264], [
                (hn2f_t[:, dc, :],
                 gatewb.ap().rearrange('p (a n) -> p a n', a=8)[:, dc, :])
                for dc in range(8)], waits=[evcp, ED.get('B_psC_done')])
            evl = V(lambda v: v.scalar_tensor_tensor(
                lgt_st.ap()[:, 0:8], ps['C'].ap()[:, 256:264], 1.0, gatebc.ap(),
                ALU.mult, ALU.add), waits=[evg])
            ED['B_psC_done'] = evl
            VPAD(3)
            V(lambda v, i=i: v.tensor_copy(lgt[:, i, :], lgt_st.ap()[:, 0:8]))
        P.barrier()
        for i in range(NT):
            evm = V(lambda v, i=i: v.reduce_max(tmpnt.ap()[:, i:i + 1],
                                                lgt[:, i, :], axis=AX.X))
            VPAD(4)
            V(lambda v, i=i: v.tensor_tensor(
                equ8.ap()[:, i * 8:(i + 1) * 8], lgt[:, i, :],
                tmpnt.ap()[:, i:i + 1].broadcast_to([128, 8]), ALU.is_equal),
              waits=[evm])
        mvw = masks.ap().rearrange('p (b2 a n) -> p b2 a n', b2=BL, a=NT)
        V(lambda v: v.memset(keepb.ap(), 1.0))
        VPAD(3)
        eqv = V(lambda v, b=b: v.select(
            masks.ap()[:, b * 128:(b + 1) * 128], equ8.ap(), onessm.ap(),
            zerossm.ap()))
        VPAD(3)
        m3 = masks.ap()[:, b * 128:(b + 1) * 128].rearrange('p (a n) -> p a n', a=NT)
        for e_ in range(8):
            if e_ > 0:
                V(lambda v, e_=e_: v.tensor_mul(m3[:, :, e_], m3[:, :, e_],
                                                keepb.ap()), waits=[eqv])
                VPAD(3)
            if e_ < 7:
                V(lambda v, e_=e_: v.tensor_scalar(
                    tmpnt.ap(), m3[:, :, e_], -1.0, 1.0, ALU.mult, ALU.add),
                  waits=[eqv])
                VPAD(3)
                V(lambda v: v.tensor_mul(keepb.ap(), keepb.ap(), tmpnt.ap()))
                VPAD(3)
        if 'd_mask' in debug:
            P.barrier()
            STORE('DS', dr['d_mask'][:, b, :, :], m3)
        for i in range(NT):
            rows = slice(i * 128, (i + 1) * 128)
            evmb = Vf(lambda v, i=i: v.tensor_copy(maskbf.ap(), m3[:, i, :]),
                      waits=[ED.get('B5_maskbf_used')])
            P.wait('T', ED.get('psT_free'))
            P.wait('T', evmb)
            P.op('T', (lambda h: h.transpose(ps['T'].ap()[0:8, 0:128],
                                             maskbf.ap(), ident.ap())))
            evt = P.op('T', (lambda h: h.drain()), inc=True)
            ED['B5_maskbf_used'] = evt
            evo = Vf(lambda v: v.tensor_copy(ohT.ap(), ps['T'].ap()[0:8, 0:128]),
                     waits=[evt])
            ED['psT_free'] = evo
            evs = None
            for gc in range(2):
                evs = mm_group(ps['G1'].ap()[:, gc * 512:(gc + 1) * 512], [
                    (ohT.ap(), b2tb.ap()[:, gc * 512:(gc + 1) * 512])],
                    waits=[evo, ED.get('B5_G1_done') if gc == 0 else None])
            evc = Vf(lambda v: v.tensor_copy(scr2b, ps['G1'].ap()),
                     waits=[evs, ED.get('B5_scr_used')])
            ED['B5_G1_done'] = evc
            eva = ACCUM('DP', dr['out'][b, rows, :], scr2b,
                        waits=[evc, ED.get('B4_base_stored')])
            ED['B5_scr_used'] = eva
        P.barrier()

    def phaseC(nseq=BL, nexp=E):
        AB1.reset(); AB2.reset(); AB3.reset(); AB4.reset()
        w1b = AB1.take(128, (8, FF), BF16)
        w2b = AB2.take(128, (32, D), BF16)
        midb = AB3.take(128, (32, 512), BF16)
        h2a = AB4.take(128, (8, 512), BF16)
        h2b = AB4.take(128, (8, 512), BF16)
        y_t = AB4.take(128, (D,), F32)
        y_t2 = AB4.take(128, (D,), F32)
        prev_accum_count = P.count['DP']
        base_stores_done = ('DS', P.count['DS'])
        for e_ in range(nexp):
            evw1 = LOAD('DP', w1b, dr['w1'][e_].rearrange('(a p) n -> p a n', p=128),
                        waits=[ED.get('C_w_used'), ED.get('C_psB_done'),
                               ED.get('C_psC_done')])
            evw2 = LOAD('DP', w2b, dr['w2'][e_].rearrange('(a p) n -> p a n', p=128),
                        waits=[ED.get('C_pg0'), ED.get('C_pg1')])
            P.wait('T', evw1)
            this_expert_first_accum = True
            for tc in range(4 * nseq):
                h2 = h2a if tc % 2 == 0 else h2b
                c0 = tc * 512
                evh = LOAD('DS', h2, dr['s_hn2'][:, :, c0:c0 + 512].rearrange(
                    'd p t -> p d t'), waits=[ED.get(f'C_h2{tc % 2}_used')])
                P.wait('T', evh)
                for fc in range(32):
                    pp, key = ((ps['A'], 'C_psA_done'), (ps['B'], 'C_psB_done'),
                               (ps['C'], 'C_psC_done'))[fc % 3]
                    ev = mm_group(pp.ap(), [
                        (w1b[:, dc, fc * 128:(fc + 1) * 128], h2[:, dc, :])
                        for dc in range(8)], waits=[ED.get(key)])
                    bcol = b1cc.ap()[:, e_ * 32 + fc: e_ * 32 + fc + 1]
                    if fc % 2 == 0:
                        eve = Vf(lambda v, pp=pp, fc=fc, bcol=bcol: v.tensor_scalar(
                            midb[:, fc, :], pp.ap(), bcol, 0.0, ALU.add, ALU.max),
                            waits=[ev])
                    else:
                        eve = Sf(lambda s, pp=pp, fc=fc, bcol=bcol: s.activation(
                            out=midb[:, fc, :], in_=pp.ap(), func=AF.Relu,
                            bias=bcol), waits=[ev])
                    ED[key] = eve
                ED[f'C_h2{tc % 2}_used'] = ED['C_psA_done']
                for k_ in ('C_psA_done', 'C_psB_done', 'C_psC_done'):
                    P.wait('T', ED.get(k_))
                if tc == 0:
                    P.wait('T', evw2)
                for tt in range(4):
                    g = tc * 4 + tt
                    bq, iq = g // NT, g % NT
                    yt = y_t if tt % 2 == 0 else y_t2
                    pgt = ps['G1'] if tt % 2 == 0 else ps['G2']
                    key = f'C_pg{tt % 2}'
                    evy = None
                    for gc in range(2):
                        evy = mm_group(pgt.ap()[:, gc * 512:(gc + 1) * 512], [
                            (midb[:, fc, tt * 128:(tt + 1) * 128],
                             w2b[:, fc, gc * 512:(gc + 1) * 512])
                            for fc in range(32)],
                            waits=[ED.get(key)] if gc == 0 else [])
                    mcol = masks.ap().rearrange('p (b2 a n) -> p b2 a n',
                                                b2=BL, a=NT)[:, bq, iq, e_:e_ + 1]
                    evm = Vf(lambda v, pgt=pgt, yt=yt, mcol=mcol: v.tensor_scalar(
                        yt, pgt.ap(), mcol, None, ALU.mult),
                        waits=[evy, ED.get(f'C_yt{tt % 2}_accumed')])
                    ED[key] = evm
                    waits = [evm]
                    if this_expert_first_accum:
                        waits.append(('DP', prev_accum_count))
                        waits.append(base_stores_done)
                        this_expert_first_accum = False
                    eva = ACCUM('DP', dr['out'][bq, iq * 128:(iq + 1) * 128, :],
                                yt, waits=waits)
                    ED[f'C_yt{tt % 2}_accumed'] = eva
            ED['C_w_used'] = ED['C_psA_done']
            prev_accum_count = P.count['DP']
        P.wait('DP', ('DP', P.count['DP']))
        P.wait('DS', ('DS', P.count['DS']))
        P.barrier()

    for b in range(nseq):
        if 'A' in phases:
            sub = ''.join(c for c in phases if c.isdigit()) or '1234'
            phaseA(b, sub)
        if 'B' in phases:
            phaseB(b)
    P.barrier()
    if 'C' in phases:
        phaseC(nseq, nexp)
    P.wait('DP', ('DP', P.count['DP']))
    P.wait('DS', ('DS', P.count['DS']))

    P.emit_into(block, sems)
    st.close()
    return nc


def kernel(**inputs):
    lay = host_layout(inputs)
    x = np.ascontiguousarray(np.asarray(inputs['L0_attn_x'], np.float32))
    nc = build()
    in_maps = []
    for i in range(NCORES):
        m = {'x': np.ascontiguousarray(x[i * BL:(i + 1) * BL])}
        m.update(lay)
        in_maps.append(m)
    res = run_bass_kernel_spmd(nc, in_maps, core_ids=list(range(NCORES)))
    return np.concatenate([r['out'] for r in res.results], axis=0)
